# version 07
# baseline (speedup 1.0000x reference)
import numpy as np
import concourse.bass as bass
import concourse.mybir as mybir
from concourse.bass_utils import run_bass_kernel_spmd

F32 = mybir.dt.float32
BF16 = mybir.dt.bfloat16
AF = mybir.ActivationFunctionType
ALU = mybir.AluOpType
AX = mybir.AxisListType

D = 1024
DI = 2048
S = 8192
T = 512
NTILE = S // T
NS = 16
L = 4
KC_D = D // 128
KC_DI = DI // 128
ND_D = D // 128
ND_DI = DI // 128
EPS = 1e-5
DEPTH_FF = 7
PAR_FF = 8
NN_FF = 255
LOFF = [0, 1, 2, 3, 4, 5, 7, 11, 19]
NT_FF = 19
J2 = NT_FF * 128
SAME_ENGINE_EDGES = 2



def _vec2sb(v, ntiles):
    out = np.zeros(128 * ntiles, np.float32)
    out[:v.shape[0]] = v
    return np.ascontiguousarray(out.reshape(ntiles, 128).T)


def _ffn_perm():
    perm = np.full(J2, -1, np.int64)
    for d in range(DEPTH_FF + 1):
        sz = 1 << d
        base = 128 * LOFF[d]
        for p in range(PAR_FF):
            for i in range(sz):
                perm[base + p * sz + i] = p * NN_FF + (sz - 1 + i)
    return perm


def _tree_blocks():
    blocks = []
    for d in range(DEPTH_FF):
        psz = 1 << d
        csz = 1 << (d + 1)
        nchild_rows = PAR_FF * csz
        nct = (nchild_rows + 127) // 128
        for ctl in range(nct):
            ct = LOFF[d + 1] + ctl
            matL = np.zeros((128, 128), np.float32)
            matR = np.zeros((128, 128), np.float32)
            pt = None
            for cl in range(min(128, nchild_rows - ctl * 128)):
                crow = ctl * 128 + cl
                p = crow // csz
                ip = crow % csz
                prow = p * psz + (ip >> 1)
                side = ip & 1
                pt_this = LOFF[d] + prow // 128
                if pt is None:
                    pt = pt_this
                assert pt == pt_this
                (matR if side else matL)[prow % 128, cl] = 1.0
            blocks.append((d, pt, ct, 0, matL))
            blocks.append((d, pt, ct, 1, matR))
    return blocks


_TREE_BLOCKS = _tree_blocks()
_PERM = _ffn_perm()


def _prep(inputs):
    g = {}
    hs = np.asarray(inputs["L0_attn_hidden_states"], np.float32)
    g["_hx_per_core"] = [np.ascontiguousarray(hs[b].T) for b in range(8)]

    W_in = np.asarray(inputs["L0_ffn_W_in"], np.float32)
    b_in = np.asarray(inputs["L0_ffn_b_in"], np.float32)
    W_out = np.asarray(inputs["L0_ffn_W_out"], np.float32)
    WiT = np.zeros((D, J2), np.float32)
    WoT = np.zeros((J2, D), np.float32)
    bff = np.zeros(J2, np.float32)
    valid = _PERM >= 0
    WiT[:, valid] = W_in[_PERM[valid], :].T
    WoT[valid, :] = W_out[:, _PERM[valid]].T
    bff[valid] = b_in[_PERM[valid]]
    g["wffinT"] = np.ascontiguousarray(WiT)
    g["wffoutT"] = np.ascontiguousarray(WoT)
    g["bff"] = _vec2sb(bff, NT_FF)
    g["treeblk"] = np.ascontiguousarray(
        np.stack([m for (_, _, _, _, m) in _TREE_BLOCKS], 0))
    rootsel = np.zeros((1, 128), np.float32)
    rootsel[0, :PAR_FF] = 1.0
    g["rootsel"] = rootsel

    g["op1T"] = np.ascontiguousarray(np.asarray(inputs["L0_attn_op1_w"], np.float32).T)
    g["op2T"] = np.ascontiguousarray(np.asarray(inputs["L0_attn_op2_w"], np.float32).T)
    g["b1"] = _vec2sb(np.asarray(inputs["L0_attn_op1_b"], np.float32), ND_D)
    g["b2"] = _vec2sb(np.asarray(inputs["L0_attn_op2_b"], np.float32), ND_D)
    g["inng"] = _vec2sb(np.asarray(inputs["L0_attn_in_norm_g"], np.float32), ND_D)
    g["innb"] = _vec2sb(np.asarray(inputs["L0_attn_in_norm_b"], np.float32), ND_D)

    lng = np.asarray(inputs["L0_attn_ln_g"], np.float32)
    lnb = np.asarray(inputs["L0_attn_ln_b"], np.float32)
    g["lng"] = np.stack([_vec2sb(lng[l], ND_D) for l in range(L)])
    g["lnb"] = np.stack([_vec2sb(lnb[l], ND_D) for l in range(L)])
    g["inprojT"] = np.ascontiguousarray(
        np.asarray(inputs["L0_attn_in_proj_w"], np.float32).transpose(0, 2, 1))
    g["xprojT"] = np.ascontiguousarray(
        np.asarray(inputs["L0_attn_x_proj_w"], np.float32).transpose(0, 2, 1))
    g["outprojT"] = np.ascontiguousarray(
        np.asarray(inputs["L0_attn_out_proj_w"], np.float32).transpose(0, 2, 1))
    cw = np.asarray(inputs["L0_attn_conv_w"], np.float32)
    g["convw"] = np.ascontiguousarray(np.stack([
        np.stack([np.stack([_vec2sb(cw[l, c, :, k], ND_DI) for k in range(4)])
                  for c in range(2)]) for l in range(L)]))
    cb = np.asarray(inputs["L0_attn_conv_b"], np.float32)
    g["convb"] = np.ascontiguousarray(np.stack([
        np.stack([_vec2sb(cb[l, c], ND_DI) for c in range(2)]) for l in range(L)]))
    g["dtw"] = np.stack([_vec2sb(np.asarray(inputs["L0_attn_dt_proj_w"], np.float32)[l, :, 0], ND_DI) for l in range(L)])
    g["dtb"] = np.stack([_vec2sb(np.asarray(inputs["L0_attn_dt_proj_b"], np.float32)[l], ND_DI) for l in range(L)])
    g["Dv"] = np.stack([_vec2sb(np.asarray(inputs["L0_attn_D"], np.float32)[l], ND_DI) for l in range(L)])
    g["mlng"] = np.stack([_vec2sb(np.asarray(inputs["L0_attn_mamba_ln_g"], np.float32)[l], ND_DI) for l in range(L)])
    g["mlnb"] = np.stack([_vec2sb(np.asarray(inputs["L0_attn_mamba_ln_b"], np.float32)[l], ND_DI) for l in range(L)])
    g["onesm"] = np.ones((128, 128), np.float32)
    sel = np.zeros((48, 16 * 128), np.float32)
    for n in range(16):
        sel[n, n * 128:(n + 1) * 128] = 1.0
        sel[32 + n, n * 128:(n + 1) * 128] = 1.0
    g["bcsel"] = sel
    return g



class Buf:
    __slots__ = ("name", "writer", "readers")

    def __init__(self, name):
        self.name = name
        self.writer = None
        self.readers = []


class Tracker:
    ENGINES = ("sync", "tensor", "vector", "scalar", "gpsimd")

    def __init__(self, sems, lane_pool):
        self.sems = dict(sems)
        self.lane_pool = list(lane_pool)
        self.cnt = {e: 0 for e in self.ENGINES}
        self.q = {e: [] for e in self.ENGINES}
        self.trace = []

    def lane(self, name):
        if name not in self.sems:
            assert self.lane_pool, "out of DMA lane semaphores"
            self.sems[name] = self.lane_pool.pop()
            self.cnt[name] = 0
        return name

    def _waits_for(self, eng, reads, writes):
        need = {}

        def add(dep):
            if dep is None:
                return
            wsn, wcnt = dep
            if wsn == eng:
                if eng == "tensor":
                    return
                if SAME_ENGINE_EDGES is False:
                    return
                if SAME_ENGINE_EDGES is not True and                         self.cnt[eng] - wcnt >= SAME_ENGINE_EDGES:
                    return
            if need.get(wsn, 0) < wcnt:
                need[wsn] = wcnt
        for b in reads:
            add(b.writer)
        for b in writes:
            add(b.writer)
            for r in b.readers:
                add(r)
        return need

    def _push(self, eng, need, fns, inc, sn):
        sem = self.sems

        def action(e, need=need, fns=fns, inc=inc, sn=sn):
            for weng in sorted(need):
                e.wait_ge(sem[weng], need[weng])
            last = None
            for f in fns:
                last = f(e)
            last.then_inc(sem[sn], inc)
        self.q[eng].append(action)

    def emit(self, eng, fn, reads=(), writes=(), inc=1):
        return self.emit_group(eng, [fn], reads=reads, writes=writes, inc=inc)

    def emit_group(self, eng, fns, reads=(), writes=(), inc=1, sem_name=None):
        sn = sem_name or eng
        need = self._waits_for(eng, reads, writes)
        self.cnt[sn] += inc
        self.trace.append((eng, sn, self.cnt[sn], tuple(b.name for b in reads),
                           tuple(b.name for b in writes), dict(need), len(fns)))
        self._push(eng, need, fns, inc, sn)
        dep = (sn, self.cnt[sn])
        for b in writes:
            b.writer = dep
            b.readers = []
        for b in reads:
            if b not in writes:
                b.readers.append(dep)
        return dep

    def barrier(self):
        snap = dict(self.cnt)
        sem = self.sems
        for eng in self.ENGINES:
            def action(e, snap=snap, eng=eng):
                for osn, v in snap.items():
                    if osn != eng and v > 0:
                        e.wait_ge(sem[osn], v)
            self.q[eng].append(action)


def _ap(handle, offset, dims, npart=128):
    base = handle.ap()
    return bass.AP(tensor=base.tensor, offset=base.offset + offset,
                   ap=[[base.ap[0][0], npart]] + [[s, c] for (s, c) in dims])


def _dap(dram_ap, offset, dims):
    return bass.AP(tensor=dram_ap.tensor, offset=dram_ap.offset + offset,
                   ap=[[s, c] for (s, c) in dims])



def build_program(max_phase=99, dbg=False, ntile=NTILE):
    nc = bass.Bass("TRN2", target_bir_lowering=False)
    E = {}

    def ein(name, shape):
        E[name] = nc.dram_tensor(name, list(shape), F32, kind="ExternalInput").ap()

    ein("hx", (D, S))
    ein("wffinT", (D, J2)); ein("wffoutT", (J2, D)); ein("bff", (128, NT_FF))
    ein("treeblk", (len(_TREE_BLOCKS), 128, 128)); ein("rootsel", (1, 128))
    ein("op1T", (D, D)); ein("op2T", (D, D)); ein("b1", (128, ND_D)); ein("b2", (128, ND_D))
    ein("inng", (128, ND_D)); ein("innb", (128, ND_D))
    ein("lng", (L, 128, ND_D)); ein("lnb", (L, 128, ND_D))
    ein("inprojT", (L, D, 2 * DI)); ein("xprojT", (L, DI, 33)); ein("outprojT", (L, DI, D))
    ein("convw", (L, 2, 4, 128, ND_DI)); ein("convb", (L, 2, 128, ND_DI))
    ein("dtw", (L, 128, ND_DI)); ein("dtb", (L, 128, ND_DI)); ein("Dv", (L, 128, ND_DI))
    ein("mlng", (L, 128, ND_DI)); ein("mlnb", (L, 128, ND_DI))
    ein("onesm", (128, 128))
    ein("bcsel", (48, 16 * 128))

    out_t = nc.dram_tensor("out", [D, S], F32, kind="ExternalOutput").ap()
    KD = "ExternalOutput" if dbg else "Internal"
    if dbg:
        dbg1 = nc.dram_tensor("dbg1", [D, S], F32, kind="ExternalOutput").ap()
        dbg2 = nc.dram_tensor("dbg2", [D, S], F32, kind="ExternalOutput").ap()
    zA = nc.dram_tensor("zA", [D, S], F32, kind=KD).ap()
    zB = nc.dram_tensor("zB", [D, S], F32, kind=KD).ap()
    x1s = nc.dram_tensor("x1s", [D, S], F32, kind=KD).ap()
    ln1s = nc.dram_tensor("ln1s", [D, S], BF16, kind=KD).ap()
    c_inproj = nc.dram_tensor("c_inproj", [L, 2 * ND_DI, 128, KC_D * 128], BF16, kind="Internal").ap()
    c_outproj = nc.dram_tensor("c_outproj", [L, ND_D, 128, KC_DI * 128], BF16, kind="Internal").ap()
    c_wffin = nc.dram_tensor("c_wffin", [NT_FF, 128, KC_D * 128], BF16, kind="Internal").ap()
    c_wffout = nc.dram_tensor("c_wffout", [ND_D, 128, NT_FF * 128], BF16, kind="Internal").ap()
    c_op1 = nc.dram_tensor("c_op1", [ND_D, 128, KC_D * 128], BF16, kind="Internal").ap()
    c_op2 = nc.dram_tensor("c_op2", [ND_D, 128, KC_D * 128], BF16, kind="Internal").ap()

    NBLK = len(_TREE_BLOCKS)
    import contextlib
    es = contextlib.ExitStack()
    with es:
        ones_f = es.enter_context(nc.sbuf_tensor("ones_f", [128, 128], F32))
        ones_b = es.enter_context(nc.sbuf_tensor("ones_b", [128, 128], BF16))
        treesb = es.enter_context(nc.sbuf_tensor("treesb", [128, NBLK * 128], BF16))
        rootsb = es.enter_context(nc.sbuf_tensor("rootsb", [1, 128], BF16))
        onerow_b = es.enter_context(nc.sbuf_tensor("onerow_b", [1, T], BF16))
        wstg = es.enter_context(nc.sbuf_tensor("wstg", [128, 1024], F32))
        wstgb = es.enter_context(nc.sbuf_tensor("wstgb", [128, 1024], BF16))
        xpw_b = es.enter_context(nc.sbuf_tensor("xpw_b", [128, KC_DI * 33], BF16))
        wbufA = es.enter_context(nc.sbuf_tensor("wbufA", [128, NT_FF * 128], BF16))
        wbufB = es.enter_context(nc.sbuf_tensor("wbufB", [128, NT_FF * 128], BF16))
        vecs = es.enter_context(nc.sbuf_tensor("vecs", [128, 320], F32))
        z_in = es.enter_context(nc.sbuf_tensor("z_in", [128, ND_D * T], F32))
        zo_pp = es.enter_context(nc.sbuf_tensor("zo_pp", [128, 2 * T], F32))
        xn_b = es.enter_context(nc.sbuf_tensor("xn_b", [128, ND_D * T], BF16))
        cin_b = es.enter_context(nc.sbuf_tensor("cin_b", [128, 2 * (T + 4)], BF16))
        xtails = es.enter_context(nc.sbuf_tensor("xtails", [128, ND_DI * 4], BF16))
        cmid_b = es.enter_context(nc.sbuf_tensor("cmid_b", [128, 2 * (T + 4)], BF16))
        ctails = es.enter_context(nc.sbuf_tensor("ctails", [128, ND_DI * 4], BF16))
        arena = es.enter_context(nc.sbuf_tensor("arena", [128, 49152], BF16))
        projsb = es.enter_context(nc.sbuf_tensor("projsb", [33, T], F32))
        bc_sb = es.enter_context(nc.sbuf_tensor("bc_sb", [48, T], BF16))
        bcsel_b = es.enter_context(nc.sbuf_tensor("bcsel_b", [48, 16 * 128], BF16))
        dtt = es.enter_context(nc.sbuf_tensor("dtt", [128, T], F32))
        dtxb = es.enter_context(nc.sbuf_tensor("dtxb", [128, T], BF16))
        abuf = es.enter_context(nc.sbuf_tensor("abuf", [128, T], BF16))

        yraw = es.enter_context(nc.sbuf_tensor("yraw", [128, T], F32))
        q1 = es.enter_context(nc.sbuf_tensor("q1", [128, T], F32))
        q2 = es.enter_context(nc.sbuf_tensor("q2", [128, T], F32))
        states_sb = es.enter_context(nc.sbuf_tensor("states", [128, ND_DI * NS], F32))
        convacc = es.enter_context(nc.sbuf_tensor("convacc", [128, T], F32))
        sq_b = es.enter_context(nc.sbuf_tensor("sq_b", [128, ND_DI * T], BF16))
        statA = es.enter_context(nc.sbuf_tensor("statA", [128, 3 * T], F32))
        srow_b = es.enter_context(nc.sbuf_tensor("srow_b", [128, T], BF16))
        mrow_b = es.enter_context(nc.sbuf_tensor("mrow_b", [128, T], BF16))
        cst = es.enter_context(nc.sbuf_tensor("cst", [128, 4], F32))

        pb0 = es.enter_context(nc.psum_tensor("pb0", [128, T], F32))
        pb1 = es.enter_context(nc.psum_tensor("pb1", [128, T], F32))
        pb2 = es.enter_context(nc.psum_tensor("pb2", [128, T], F32))
        pb3 = es.enter_context(nc.psum_tensor("pb3", [128, T], F32))
        pb4 = es.enter_context(nc.psum_tensor("pb4", [128, T], F32))
        pb5 = es.enter_context(nc.psum_tensor("pb5", [128, T], F32))
        pb6 = es.enter_context(nc.psum_tensor("pb6", [128, T], F32))
        pb7 = es.enter_context(nc.psum_tensor("pb7", [128, T], F32))
        s_sync = es.enter_context(nc.semaphore("s_sync"))
        lane_sems = [es.enter_context(nc.semaphore(f"lane{i}")) for i in range(24)]
        s_tensor = es.enter_context(nc.semaphore("s_tensor"))
        s_vector = es.enter_context(nc.semaphore("s_vector"))
        s_scalar = es.enter_context(nc.semaphore("s_scalar"))
        s_gpsimd = es.enter_context(nc.semaphore("s_gpsimd"))
        block = es.enter_context(nc.Block())
        pb = [pb0, pb1, pb2, pb3, pb4, pb5, pb6, pb7]

        class TView:

            def __init__(self, h, off, nels):
                self.h, self.off, self.nels = h, off, nels

            def ap(self):
                b = self.h.ap()
                return bass.AP(tensor=b.tensor, offset=b.offset + self.off,
                               ap=[[b.ap[0][0], 128], [1, self.nels]])

            def __getitem__(self, key):
                base = self.ap()
                if isinstance(key, slice):
                    return base
                ps, fs = key
                assert (ps.start or 0) == 0
                f0 = fs.start or 0
                f1 = fs.stop if fs.stop is not None else self.nels
                return bass.AP(tensor=base.tensor, offset=base.offset + f0,
                               ap=[[base.ap[0][0], 128], [1, f1 - f0]])

        arena_f = arena.bitcast(F32)
        ufull_b = TView(sq_b, 0, NS * T)
        hbuf2_v = TView(arena, 24576, ND_DI * T)
        xi2_b = TView(arena, 0, ND_DI * T)
        sres_b = TView(arena, 8192, ND_DI * T)
        y2_b = TView(arena, 16384, ND_DI * T)
        hbuf_b = TView(arena, 24576, NS * T)
        brep_b = TView(arena, 32768, NS * T)
        crep_b = TView(arena, 40960, NS * T)
        acts_b = TView(arena, 0, NT_FF * T)
        dec_b = TView(arena, 9728, NT_FF * T)
        dmap_b = TView(arena, 19456, NT_FF * T)
        oLR_b = TView(arena, 29184, 8 * T)
        ln1_b = TView(arena, 33280, ND_D * T)
        f2_a = TView(arena_f, 18688, ND_D * T)
        tr = Tracker({"sync": s_sync, "tensor": s_tensor, "vector": s_vector,
                      "scalar": s_scalar, "gpsimd": s_gpsimd}, lane_sems)
        build_program.last_trace = tr.trace
        B = {}

        def tb(name):
            if name not in B:
                B[name] = Buf(name)
            return B[name]

        PS = [tb(f"ps{i}") for i in range(8)]
        bv = tb("vecs")

        VEC = {}
        _vc = 0
        for nm, n in [("lng", ND_D), ("lnb", ND_D), ("inng", ND_D), ("innb", ND_D),
                      ("cw00", ND_DI), ("cw01", ND_DI), ("cw02", ND_DI), ("cw03", ND_DI),
                      ("cw10", ND_DI), ("cw11", ND_DI), ("cw12", ND_DI), ("cw13", ND_DI),
                      ("cb0", ND_DI), ("cb1", ND_DI),
                      ("dtw", ND_DI), ("dtb", ND_DI), ("Dv", ND_DI),
                      ("mlng", ND_DI), ("mlnb", ND_DI),
                      ("bff", NT_FF), ("b1", ND_D), ("b2", ND_D)]:
            VEC[nm] = _vc
            _vc += n
        assert _vc <= 320

        def vap(name, i):
            c = VEC[name] + i
            return vecs[:, c:c + 1]

        def ld_vec(dst, src_ap, n):
            c = VEC[dst]
            tr.emit_group("sync", [lambda e, c=c, s=src_ap, n=n: e.dma_start(
                out=vecs[:, c:c + n], in_=s)], writes=(bv,), inc=16,
                sem_name=tr.lane("L_vecs"))

        def dma_in(dst_ap, src_ap, wb, rb=()):
            ln_ = tr.lane("L_" + wb[0].name)
            tr.emit_group("sync", [lambda e, d=dst_ap, s=src_ap: e.dma_start(out=d, in_=s)],
                          reads=rb, writes=wb, inc=16, sem_name=ln_)

        def dma_out(dst_ap, src_ap, rb, wb=()):
            ln_ = tr.lane("S_" + rb[0].name)
            tr.emit_group("gpsimd", [lambda e, d=dst_ap, s=src_ap: e.dma_start(out=d, in_=s)],
                          reads=rb, writes=wb, inc=16, sem_name=ln_)

        dma_in(ones_f[:], E["onesm"][:], (tb("ones_f"),))
        tr.emit("vector", lambda e: e.tensor_copy(ones_b[:], ones_f[:]),
                reads=(tb("ones_f"),), writes=(tb("ones_b"),))
        tr.emit("gpsimd", lambda e: e.memset(onerow_b[:], 1.0), writes=(tb("onerow_b"),))
        tr.emit("gpsimd", lambda e: e.memset(cst[:, 0:1], EPS), writes=(tb("cst"),))
        tr.emit("gpsimd", lambda e: e.memset(cst[:, 1:2], -0.5), writes=(tb("cst"),))
        tr.emit("gpsimd", lambda e: e.memset(cst[:, 2:3], 1.0), writes=(tb("cst"),))

        def cache_build(src2d, rows, cols, dst_ap_fn, wname):
            CC = 1024
            for ch in range(rows // 128):
                for c0 in range(0, cols, CC):
                    cw_ = min(CC, cols - c0)
                    dma_in(wstg[:, 0:cw_], src2d[ch * 128:(ch + 1) * 128, c0:c0 + cw_],
                           (tb("wstg"),))
                    tr.emit("vector", lambda e, c=cw_: e.tensor_copy(wstgb[:, 0:c], wstg[:, 0:c]),
                            reads=(tb("wstg"),), writes=(tb("wstgb"),))
                    dma_out(dst_ap_fn(ch, c0, cw_), wstgb[:, 0:cw_], (tb("wstgb"),))

        for l in range(L):
            def dst_ip(kc, c0, cw_, l=l):
                return _dap(c_inproj,
                            l * (2 * ND_DI * 128 * KC_D * 128)
                            + (c0 // 128) * (128 * KC_D * 128) + kc * 128,
                            [(KC_D * 128, 128), (128 * KC_D * 128, cw_ // 128), (1, 128)])
            cache_build(E["inprojT"][l], D, 2 * DI, dst_ip, "ip")

            def dst_op(kc, c0, cw_, l=l):
                return _dap(c_outproj,
                            l * (ND_D * 128 * KC_DI * 128)
                            + (c0 // 128) * (128 * KC_DI * 128) + kc * 128,
                            [(KC_DI * 128, 128), (128 * KC_DI * 128, cw_ // 128), (1, 128)])
            cache_build(E["outprojT"][l], DI, D, dst_op, "op")

        cache_build(E["wffinT"], D, J2,
                    lambda kc, c0, cw_: _dap(c_wffin, (c0 // 128) * (128 * KC_D * 128) + kc * 128,
                                             [(KC_D * 128, 128), (128 * KC_D * 128, cw_ // 128), (1, 128)]), "fi")
        cache_build(E["wffoutT"], J2, D,
                    lambda kc, c0, cw_: _dap(c_wffout, (c0 // 128) * (128 * NT_FF * 128) + kc * 128,
                                             [(NT_FF * 128, 128), (128 * NT_FF * 128, cw_ // 128), (1, 128)]), "fo")
        cache_build(E["op1T"], D, D,
                    lambda kc, c0, cw_: _dap(c_op1, (c0 // 128) * (128 * KC_D * 128) + kc * 128,
                                             [(KC_D * 128, 128), (128 * KC_D * 128, cw_ // 128), (1, 128)]), "o1")
        cache_build(E["op2T"], D, D,
                    lambda kc, c0, cw_: _dap(c_op2, (c0 // 128) * (128 * KC_D * 128) + kc * 128,
                                             [(KC_D * 128, 128), (128 * KC_D * 128, cw_ // 128), (1, 128)]), "o2")

        for b0 in range(0, NBLK, 8):
            nb = min(8, NBLK - b0)
            src = _dap(E["treeblk"], b0 * 128 * 128,
                       [(128, 128), (128 * 128, nb), (1, 128)])
            dma_in(wstg[:, 0:nb * 128], src, (tb("wstg"),))
            tr.emit("vector", lambda e, b0=b0, nb=nb: e.tensor_copy(
                treesb[:, b0 * 128:(b0 + nb) * 128], wstg[:, 0:nb * 128]),
                reads=(tb("wstg"),), writes=(tb("treesb"),))
        for c0 in (0, 1024):
            dma_in(wstg[0:48, 0:1024], E["bcsel"][:, c0:c0 + 1024], (tb("wstg"),))
            tr.emit("vector", lambda e, c0=c0: e.tensor_copy(
                bcsel_b[0:48, c0:c0 + 1024], wstg[0:48, 0:1024]),
                reads=(tb("wstg"),), writes=(tb("bcsel_b"),))
        dma_in(wstg[0:1, 0:128], E["rootsel"][:], (tb("wstg"),))
        tr.emit("vector", lambda e: e.tensor_copy(rootsb[:], wstg[0:1, 0:128]),
                reads=(tb("wstg"),), writes=(tb("rootsb"),))
        tr.barrier()

        bstat = tb("stat")

        def ln_stats(src_tiles, nt, sq_view, src_is_bf, src_buf):
            Dtot = float(nt * 128)
            for i in range(nt):
                tr.emit("scalar", lambda e, i=i: e.activation(sq_view(i), src_tiles(i), AF.Square),
                        reads=(src_buf,), writes=(tb("sq_b"),))
            oc = ones_b[:, 0:1] if src_is_bf else ones_f[:, 0:1]
            ob = tb("ones_b") if src_is_bf else tb("ones_f")
            tr.emit_group("tensor", [
                (lambda e, i=i, st=(i == 0), sp=(i == nt - 1): e.matmul(
                    pb2[0:1, :], oc, src_tiles(i), start=st, stop=sp))
                for i in range(nt)], reads=(src_buf, ob), writes=(PS[2],))
            tr.emit_group("tensor", [
                (lambda e, i=i, st=(i == 0), sp=(i == nt - 1): e.matmul(
                    pb3[0:1, :], ones_b[:, 0:1], sq_view(i), start=st, stop=sp))
                for i in range(nt)], reads=(tb("sq_b"), tb("ones_b")), writes=(PS[3],))
            tr.emit("vector", lambda e: e.tensor_scalar(statA[0:1, 0:T], pb2[0:1, :], 1.0 / Dtot, None, ALU.mult),
                    reads=(PS[2],), writes=(bstat,))
            tr.emit("vector", lambda e: e.tensor_scalar(statA[32:33, 0:T], pb3[0:1, :], 1.0 / Dtot, None, ALU.mult),
                    reads=(PS[3],), writes=(bstat,))
            tr.emit("vector", lambda e: e.tensor_tensor(statA[32:33, T:2 * T], statA[0:1, 0:T], statA[0:1, 0:T], ALU.mult),
                    reads=(bstat,), writes=(bstat,))
            tr.emit("vector", lambda e: e.tensor_tensor(statA[64:65, 0:T], statA[32:33, 0:T], statA[32:33, T:2 * T], ALU.subtract),
                    reads=(bstat,), writes=(bstat,))
            tr.emit("scalar", lambda e: e.activation(statA[64:65, T:2 * T], statA[64:65, 0:T], AF.Ln, bias=cst[64:65, 0:1]),
                    reads=(bstat, tb("cst")), writes=(bstat,))
            tr.emit("scalar", lambda e: e.activation(statA[0:1, T:2 * T], statA[64:65, T:2 * T], AF.Exp, scale=cst[64:65, 1:2]),
                    reads=(bstat, tb("cst")), writes=(bstat,))
            tr.emit("vector", lambda e: e.tensor_tensor(statA[0:1, 2 * T:3 * T], statA[0:1, 0:T], statA[0:1, T:2 * T], ALU.mult),
                    reads=(bstat,), writes=(bstat,))
            tr.emit("tensor", lambda e: e.matmul(pb4[:], ones_f[0:1, :], statA[0:1, T:2 * T], start=True, stop=True),
                    reads=(bstat, tb("ones_f")), writes=(PS[4],))
            tr.emit("tensor", lambda e: e.matmul(pb5[:], ones_f[0:1, :], statA[0:1, 2 * T:3 * T], start=True, stop=True),
                    reads=(bstat, tb("ones_f")), writes=(PS[5],))

        def ln_apply(src_tiles, nt, out_fn, src_buf, gain=None, bias=None):
            for i in range(nt):
                tr.emit("vector", lambda e, i=i: e.tensor_tensor(q1[:], src_tiles(i), pb4[:], ALU.mult),
                        reads=(src_buf, PS[4]), writes=(tb("q1"),))
                if gain is None:
                    oap, ob = out_fn(i)
                    tr.emit("vector", lambda e, oap=oap: e.tensor_tensor(oap, q1[:], pb5[:], ALU.subtract),
                            reads=(tb("q1"), PS[5]), writes=(ob,))
                else:
                    tr.emit("vector", lambda e: e.tensor_tensor(q2[:], q1[:], pb5[:], ALU.subtract),
                            reads=(tb("q1"), PS[5]), writes=(tb("q2"),))
                    oap, ob = out_fn(i)
                    tr.emit("scalar", lambda e, oap=oap, g=gain(i), bb=bias(i): e.activation(
                        oap, q2[:], AF.Identity, scale=g, bias=bb),
                        reads=(tb("q2"), bv), writes=(ob,))

        def stream_w(cache_ap, idx, ncols, slot):
            wtile = wbufA if slot == 0 else wbufB
            wb = tb("wbufA") if slot == 0 else tb("wbufB")
            src = _dap(cache_ap, idx * 128 * ncols * 128, [(ncols * 128, 128), (1, ncols * 128)])
            dma_in(_ap(wtile, 0, [(1, ncols * 128)]), src, (wb,))
            return wtile, wb

        def ffn_tree(xtile, consume, xnb_buf):
            blocks_by_ct = {}
            for bi, (d, pt, ct, side, _) in enumerate(_TREE_BLOCKS):
                blocks_by_ct.setdefault((d, ct), []).append((bi, pt, side))
            for j in range(NT_FF):
                slot = j % 2
                pst, psb_ = (PS[slot], pb[slot])
                wtile, wb = stream_w(c_wffin, j, KC_D, slot)
                tr.emit_group("tensor", [
                    (lambda e, kc=kc, st=(kc == 0), sp=(kc == KC_D - 1), wtile=wtile, psb_=psb_:
                     e.matmul(psb_[:], wtile[:, kc * 128:(kc + 1) * 128], xtile(kc), start=st, stop=sp))
                    for kc in range(KC_D)], reads=(wb, xnb_buf), writes=(pst,))
                tr.emit("scalar", lambda e, j=j, psb_=psb_: e.activation(
                    acts_b[:, j * T:(j + 1) * T], psb_[:], AF.Silu,
                    bias=vecs[:, VEC["bff"] + j:VEC["bff"] + j + 1]),
                    reads=(bv,), writes=(tb("acts_b"), pst))
                tr.emit("vector", lambda e, j=j, psb_=psb_: e.tensor_scalar(
                    dec_b[:, j * T:(j + 1) * T], psb_[:],
                    vecs[:, VEC["bff"] + j:VEC["bff"] + j + 1], 0.0, ALU.add, ALU.is_gt),
                    reads=(bv,), writes=(tb("dec_b"), pst))
            tr.emit("tensor", lambda e: e.matmul(pb2[:], rootsb[0:1, :], onerow_b[0:1, :], start=True, stop=True),
                    reads=(tb("rootsb"), tb("onerow_b")), writes=(PS[2],))
            tr.emit("vector", lambda e: e.tensor_copy(dmap_b[:, 0:T], pb2[:]),
                    reads=(PS[2],), writes=(tb("dmap_b"),))
            for d in range(DEPTH_FF):
                pts = sorted({pt for (dd, pt, ct, s2, m) in _TREE_BLOCKS if dd == d})
                pt_slot = {pt: k for k, pt in enumerate(pts)}
                for pt, k in pt_slot.items():
                    tr.emit("vector", lambda e, pt=pt, k=k: e.tensor_tensor(
                        oLR_b[:, (2 * k + 1) * T:(2 * k + 2) * T],
                        dmap_b[:, pt * T:(pt + 1) * T], dec_b[:, pt * T:(pt + 1) * T], ALU.mult),
                        reads=(tb("dmap_b"), tb("dec_b")), writes=(tb("oLR_b"),))
                    tr.emit("vector", lambda e, pt=pt, k=k: e.tensor_tensor(
                        oLR_b[:, (2 * k) * T:(2 * k + 1) * T],
                        dmap_b[:, pt * T:(pt + 1) * T], oLR_b[:, (2 * k + 1) * T:(2 * k + 2) * T], ALU.subtract),
                        reads=(tb("dmap_b"),), writes=(tb("oLR_b"),))
                cts = sorted({ct for (dd, pt, ct, s2, m) in _TREE_BLOCKS if dd == d})
                for ci, ct in enumerate(cts):
                    blks = blocks_by_ct[(d, ct)]
                    slot = 2 + ci % 2
                    pst, psb_ = PS[slot], pb[slot]
                    tr.emit_group("tensor", [
                        (lambda e, bi=bi, sl=2 * pt_slot[pt] + side, st=(ii == 0), sp=(ii == len(blks) - 1), psb_=psb_:
                         e.matmul(psb_[:], treesb[:, bi * 128:(bi + 1) * 128],
                                  oLR_b[:, sl * T:(sl + 1) * T], start=st, stop=sp))
                        for ii, (bi, pt, side) in enumerate(blks)],
                        reads=(tb("treesb"), tb("oLR_b")), writes=(pst,))
                    tr.emit("vector", lambda e, ct=ct, psb_=psb_: e.tensor_copy(
                        dmap_b[:, ct * T:(ct + 1) * T], psb_[:]),
                        reads=(pst,), writes=(tb("dmap_b"),))
            for j in range(NT_FF):
                tr.emit("vector", lambda e, j=j: e.tensor_tensor(
                    acts_b[:, j * T:(j + 1) * T], acts_b[:, j * T:(j + 1) * T],
                    dmap_b[:, j * T:(j + 1) * T], ALU.mult),
                    reads=(tb("acts_b"), tb("dmap_b")), writes=(tb("acts_b"),))
            for dout in range(ND_D):
                slot = dout % 2
                pst, psb_ = PS[slot], pb[slot]
                wtile, wb = stream_w(c_wffout, dout, NT_FF, slot)
                tr.emit_group("tensor", [
                    (lambda e, kc=kc, st=(kc == 0), sp=(kc == NT_FF - 1), wtile=wtile, psb_=psb_:
                     e.matmul(psb_[:], wtile[:, kc * 128:(kc + 1) * 128],
                              acts_b[:, kc * T:(kc + 1) * T], start=st, stop=sp))
                    for kc in range(NT_FF)], reads=(wb, tb("acts_b")), writes=(pst,))
                consume(dout, pst, psb_)

        def z_tile_load(dst_handle, dst_buf, src_dram, t0, nt):
            dma_in(_ap(dst_handle, 0, [(T, nt), (1, T)]),
                   _dap(src_dram, t0, [(S, 128), (128 * S, nt), (1, T)]), (dst_buf,))

        if max_phase >= 1:
            ld_vec("bff", E["bff"][:], NT_FF)
            ld_vec("inng", E["inng"][:], ND_D)
            ld_vec("innb", E["innb"][:], ND_D)
            for ti in range(ntile):
                t0 = ti * T
                bz = tb("z_in")
                z_tile_load(z_in, bz, E["hx"], t0, ND_D)

                def srcf(i):
                    return z_in[:, i * T:(i + 1) * T]

                def sqf(i):
                    return sq_b[:, i * T:(i + 1) * T]
                ln_stats(srcf, ND_D, sqf, False, bz)
                ln_apply(srcf, ND_D,
                         lambda i: (xn_b[:, i * T:(i + 1) * T], tb("xn_b")), bz)

                def xtile(kc):
                    return xn_b[:, kc * T:(kc + 1) * T]

                def consume(dout, pst, psb_):
                    tr.emit("vector", lambda e, dout=dout, psb_=psb_: e.scalar_tensor_tensor(
                        f2_a[:, dout * T:(dout + 1) * T], psb_[:], 0.5,
                        z_in[:, dout * T:(dout + 1) * T], ALU.mult, ALU.add),
                        reads=(pst, bz), writes=(tb("f2_a"),))
                    dma_out(_dap(x1s, dout * 128 * S + t0, [(S, 128), (1, T)]),
                            f2_a[:, dout * T:(dout + 1) * T], (tb("f2_a"),))
                ffn_tree(xtile, consume, tb("xn_b"))

                def srcf2(i):
                    return f2_a[:, i * T:(i + 1) * T]
                ln_stats(srcf2, ND_D, sqf, False, tb("f2_a"))
                ln_apply(srcf2, ND_D,
                         lambda i: (ln1_b[:, i * T:(i + 1) * T], tb("ln1_b")), tb("f2_a"))
                for i in range(ND_D):
                    dma_out(_dap(ln1s, i * 128 * S + t0, [(S, 128), (1, T)]),
                            ln1_b[:, i * T:(i + 1) * T], (tb("ln1_b"),))

                def srcf3(i):
                    return ln1_b[:, i * T:(i + 1) * T]
                ln_stats(srcf3, ND_D, sqf, True, tb("ln1_b"))
                ln_apply(srcf3, ND_D,
                         lambda i: (z_in[:, i * T:(i + 1) * T], tb("z_in")), tb("ln1_b"),
                         gain=lambda i: vap("inng", i), bias=lambda i: vap("innb", i))
                for i in range(ND_D):
                    dma_out(_dap(zA, i * 128 * S + t0, [(S, 128), (1, T)]),
                            z_in[:, i * T:(i + 1) * T], (tb("z_in"),))
            tr.barrier()

        nmamba = max(0, min(L, max_phase - 1))
        for l in range(nmamba):
            zsrc = zA if l % 2 == 0 else zB
            zdst = zB if l % 2 == 0 else zA
            ld_vec("lng", E["lng"][l], ND_D)
            ld_vec("lnb", E["lnb"][l], ND_D)
            for c in range(2):
                for k in range(4):
                    ld_vec(f"cw{c}{k}", E["convw"][l, c, k], ND_DI)
                ld_vec(f"cb{c}", E["convb"][l, c], ND_DI)
            ld_vec("dtw", E["dtw"][l], ND_DI)
            ld_vec("dtb", E["dtb"][l], ND_DI)
            ld_vec("Dv", E["Dv"][l], ND_DI)
            ld_vec("mlng", E["mlng"][l], ND_DI)
            ld_vec("mlnb", E["mlnb"][l], ND_DI)
            for kc in range(KC_DI):
                dma_in(wstg[:, 0:33], E["xprojT"][l, kc * 128:(kc + 1) * 128, :], (tb("wstg"),))
                tr.emit("vector", lambda e, kc=kc: e.tensor_copy(
                    xpw_b[:, kc * 33:(kc + 1) * 33], wstg[:, 0:33]),
                    reads=(tb("wstg"),), writes=(tb("xpw_b"),))
            tr.emit("gpsimd", lambda e: e.memset(states_sb[:], 0.0), writes=(tb("states"),))
            tr.emit("gpsimd", lambda e: e.memset(xtails[:], 0.0), writes=(tb("xtails"),))
            tr.emit("gpsimd", lambda e: e.memset(ctails[:], 0.0), writes=(tb("ctails"),))

            for ti in range(ntile):
                t0 = ti * T
                bz = tb("z_in")
                z_tile_load(z_in, bz, zsrc, t0, ND_D)

                def msrc(i):
                    return z_in[:, i * T:(i + 1) * T]

                def msq(i):
                    return sq_b[:, i * T:(i + 1) * T]
                ln_stats(msrc, ND_D, msq, False, bz)
                ln_apply(msrc, ND_D,
                         lambda i: (xn_b[:, i * T:(i + 1) * T], tb("xn_b")), bz,
                         gain=lambda i: vap("lng", i), bias=lambda i: vap("lnb", i))
                for dout in range(2 * ND_DI):
                    slot = dout % 2
                    pst, psb_ = PS[slot], pb[slot]
                    wtile, wb = stream_w(c_inproj, l * 2 * ND_DI + dout, KC_D, slot)
                    tr.emit_group("tensor", [
                        (lambda e, kc=kc, st=(kc == 0), sp=(kc == KC_D - 1), wtile=wtile, psb_=psb_:
                         e.matmul(psb_[:], wtile[:, kc * 128:(kc + 1) * 128],
                                  xn_b[:, kc * T:(kc + 1) * T], start=st, stop=sp))
                        for kc in range(KC_D)], reads=(wb, tb("xn_b")), writes=(pst,))
                    if dout >= ND_DI:
                        dd = dout - ND_DI
                        tr.emit("scalar", lambda e, dd=dd, psb_=psb_: e.activation(
                            sres_b[:, dd * T:(dd + 1) * T], psb_[:], AF.Sigmoid),
                            reads=(pst,), writes=(tb("sres_b"),))
                        continue
                    dd = dout
                    ib = slot * (T + 4)
                    tr.emit("gpsimd", lambda e, ib=ib, dd=dd: e.tensor_copy(
                        cin_b[:, ib + 1:ib + 4], xtails[:, dd * 4:dd * 4 + 3]),
                        reads=(tb("xtails"),), writes=(tb("cin_b"),))
                    tr.emit("scalar", lambda e, ib=ib, psb_=psb_: e.activation(
                        cin_b[:, ib + 4:ib + 4 + T], psb_[:], AF.Silu),
                        reads=(pst,), writes=(tb("cin_b"),))
                    tr.emit("gpsimd", lambda e, ib=ib, dd=dd: e.tensor_copy(
                        xtails[:, dd * 4:dd * 4 + 3], cin_b[:, ib + 1 + T:ib + 4 + T]),
                        reads=(tb("cin_b"),), writes=(tb("xtails"),))
                    tr.emit("vector", lambda e, ib=ib, dd=dd: e.tensor_scalar(
                        convacc[:], cin_b[:, ib + 1:ib + 1 + T], vap("cw00", dd), vap("cb0", dd), ALU.mult, ALU.add),
                        reads=(tb("cin_b"), bv), writes=(tb("convacc"),))
                    for k in range(1, 4):
                        tr.emit("vector", lambda e, ib=ib, dd=dd, k=k: e.scalar_tensor_tensor(
                            convacc[:], cin_b[:, ib + 1 + k:ib + 1 + k + T], vap(f"cw0{k}", dd),
                            convacc[:], ALU.mult, ALU.add),
                            reads=(tb("cin_b"), tb("convacc"), bv), writes=(tb("convacc"),))
                    tr.emit("gpsimd", lambda e, ib=ib, dd=dd: e.tensor_copy(
                        cmid_b[:, ib + 1:ib + 4], ctails[:, dd * 4:dd * 4 + 3]),
                        reads=(tb("ctails"),), writes=(tb("cmid_b"),))
                    tr.emit("scalar", lambda e, ib=ib: e.activation(
                        cmid_b[:, ib + 4:ib + 4 + T], convacc[:], AF.Silu),
                        reads=(tb("convacc"),), writes=(tb("cmid_b"),))
                    tr.emit("gpsimd", lambda e, ib=ib, dd=dd: e.tensor_copy(
                        ctails[:, dd * 4:dd * 4 + 3], cmid_b[:, ib + 1 + T:ib + 4 + T]),
                        reads=(tb("cmid_b"),), writes=(tb("ctails"),))
                    tr.emit("vector", lambda e, ib=ib, dd=dd: e.tensor_scalar(
                        convacc[:], cmid_b[:, ib + 1:ib + 1 + T], vap("cw10", dd), vap("cb1", dd), ALU.mult, ALU.add),
                        reads=(tb("cmid_b"), bv), writes=(tb("convacc"),))
                    for k in range(1, 4):
                        tr.emit("vector", lambda e, ib=ib, dd=dd, k=k: e.scalar_tensor_tensor(
                            convacc[:], cmid_b[:, ib + 1 + k:ib + 1 + k + T], vap(f"cw1{k}", dd),
                            convacc[:], ALU.mult, ALU.add),
                            reads=(tb("cmid_b"), tb("convacc"), bv), writes=(tb("convacc"),))
                    tr.emit("scalar", lambda e, dd=dd: e.activation(
                        xi2_b[:, dd * T:(dd + 1) * T], convacc[:], AF.Silu),
                        reads=(tb("convacc"),), writes=(tb("xi2_b"),))
                tr.emit_group("tensor", [
                    (lambda e, kc=kc, st=(kc == 0), sp=(kc == KC_DI - 1): e.matmul(
                        pb2[0:1, :], xpw_b[:, kc * 33:kc * 33 + 1],
                        xi2_b[:, kc * T:(kc + 1) * T], start=st, stop=sp))
                    for kc in range(KC_DI)], reads=(tb("xpw_b"), tb("xi2_b")), writes=(PS[2],))
                tr.emit_group("tensor", [
                    (lambda e, kc=kc, st=(kc == 0), sp=(kc == KC_DI - 1): e.matmul(
                        pb6[0:16, :], xpw_b[:, kc * 33 + 1:kc * 33 + 17],
                        xi2_b[:, kc * T:(kc + 1) * T], start=st, stop=sp))
                    for kc in range(KC_DI)], reads=(tb("xpw_b"), tb("xi2_b")), writes=(PS[6],))
                tr.emit_group("tensor", [
                    (lambda e, kc=kc, st=(kc == 0), sp=(kc == KC_DI - 1): e.matmul(
                        pb7[0:16, :], xpw_b[:, kc * 33 + 17:kc * 33 + 33],
                        xi2_b[:, kc * T:(kc + 1) * T], start=st, stop=sp))
                    for kc in range(KC_DI)], reads=(tb("xpw_b"), tb("xi2_b")), writes=(PS[7],))
                tr.emit("vector", lambda e: e.tensor_copy(projsb[0:1, :], pb2[0:1, :]),
                        reads=(PS[2],), writes=(tb("projsb"),))
                tr.emit("vector", lambda e: e.tensor_copy(bc_sb[0:16, :], pb6[0:16, :]),
                        reads=(PS[6],), writes=(tb("bc_sb"),))
                tr.emit("vector", lambda e: e.tensor_copy(bc_sb[32:48, :], pb7[0:16, :]),
                        reads=(PS[7],), writes=(tb("bc_sb"),))
                tr.emit("tensor", lambda e: e.matmul(pb3[:], ones_f[0:1, :], projsb[0:1, :], start=True, stop=True),
                        reads=(tb("projsb"), tb("ones_f")), writes=(PS[3],))
                for n in range(NS):
                    slot = 6 + n % 2
                    pst, psb_ = PS[slot], pb[slot]
                    tr.emit("tensor", lambda e, n=n, psb_=psb_: e.matmul(
                        psb_[:], bcsel_b[0:16, n * 128:(n + 1) * 128], bc_sb[0:16, :],
                        start=True, stop=True),
                        reads=(tb("bcsel_b"), tb("bc_sb")), writes=(pst,))
                    tr.emit("vector", lambda e, n=n, psb_=psb_: e.tensor_copy(
                        brep_b[:, n * T:(n + 1) * T], psb_[:]), reads=(pst,), writes=(tb("brep_b"),))
                for n in range(NS):
                    slot = 6 + n % 2
                    pst, psb_ = PS[slot], pb[slot]
                    tr.emit("tensor", lambda e, n=n, psb_=psb_: e.matmul(
                        psb_[:], bcsel_b[32:48, n * 128:(n + 1) * 128], bc_sb[32:48, :],
                        start=True, stop=True),
                        reads=(tb("bcsel_b"), tb("bc_sb")), writes=(pst,))
                    tr.emit("vector", lambda e, n=n, psb_=psb_: e.tensor_copy(
                        crep_b[:, n * T:(n + 1) * T], psb_[:]), reads=(pst,), writes=(tb("crep_b"),))
                for ds in range(ND_DI):
                    tr.emit("scalar", lambda e, ds=ds: e.activation(
                        q1[:], pb3[:], AF.Exp, scale=vap("dtw", ds), bias=vap("dtb", ds)),
                        reads=(PS[3], bv), writes=(tb("q1"),))
                    tr.emit("scalar", lambda e: e.activation(dtt[:], q1[:], AF.Ln, bias=cst[:, 2:3]),
                            reads=(tb("q1"), tb("cst")), writes=(tb("dtt"),))
                    tr.emit("gpsimd", lambda e, ds=ds: e.tensor_tensor(
                        dtxb[:], dtt[:], xi2_b[:, ds * T:(ds + 1) * T], ALU.mult),
                        reads=(tb("dtt"), tb("xi2_b")), writes=(tb("dtxb"),))
                    dba = dtxb.ap()
                    dtx_bc = bass.AP(tensor=dba.tensor, offset=dba.offset,
                                     ap=[[dba.ap[0][0], 128], [0, NS], [1, T]])
                    tr.emit("vector", lambda e, dtx_bc=dtx_bc: e.tensor_tensor(
                        ufull_b[:], dtx_bc, brep_b[:], ALU.mult),
                        reads=(tb("dtxb"), tb("brep_b")), writes=(tb("sq_b"),))
                    for n in range(NS):
                        tr.emit("vector", lambda e, n=n: e.tensor_scalar(
                            abuf[:], dtt[:], -float(n + 1), 1.0, ALU.mult, ALU.add),
                            reads=(tb("dtt"),), writes=(tb("abuf"),))
                        tr.emit("vector", lambda e, n=n, ds=ds: e.tensor_tensor_scan(
                            hbuf_b[:, n * T:(n + 1) * T], abuf[:],
                            ufull_b[:, n * T:(n + 1) * T],
                            states_sb[:, ds * NS + n:ds * NS + n + 1], ALU.mult, ALU.add),
                            reads=(tb("abuf"), tb("sq_b"), tb("states")), writes=(tb("hbuf_b"),))
                    tr.emit("vector", lambda e, ds=ds: e.tensor_copy(
                        states_sb[:, ds * NS:(ds + 1) * NS], _ap(hbuf_b, T - 1, [(T, NS)])),
                        reads=(tb("hbuf_b"),), writes=(tb("states"),))
                    tr.emit("vector", lambda e: e.tensor_tensor(
                        hbuf_b[:], hbuf_b[:], crep_b[:], ALU.mult),
                        reads=(tb("hbuf_b"), tb("crep_b")), writes=(tb("hbuf_b"),))
                    tr.emit("vector", lambda e: e.tensor_tensor(
                        ufull_b[:, 0:8 * T], hbuf_b[:, 0:8 * T], hbuf_b[:, 8 * T:16 * T], ALU.add),
                        reads=(tb("hbuf_b"),), writes=(tb("sq_b"),))
                    tr.emit("vector", lambda e: e.tensor_tensor(
                        ufull_b[:, 8 * T:12 * T], ufull_b[:, 0:4 * T], ufull_b[:, 4 * T:8 * T], ALU.add),
                        reads=(tb("sq_b"),), writes=(tb("sq_b"),))
                    tr.emit("vector", lambda e: e.tensor_tensor(
                        ufull_b[:, 12 * T:14 * T], ufull_b[:, 8 * T:10 * T], ufull_b[:, 10 * T:12 * T], ALU.add),
                        reads=(tb("sq_b"),), writes=(tb("sq_b"),))
                    tr.emit("vector", lambda e: e.tensor_tensor(
                        yraw[:], ufull_b[:, 12 * T:13 * T], ufull_b[:, 13 * T:14 * T], ALU.add),
                        reads=(tb("sq_b"),), writes=(tb("yraw"),))
                    tr.emit("vector", lambda e, ds=ds: e.scalar_tensor_tensor(
                        y2_b[:, ds * T:(ds + 1) * T], xi2_b[:, ds * T:(ds + 1) * T],
                        vap("Dv", ds), yraw[:], ALU.mult, ALU.add),
                        reads=(tb("xi2_b"), tb("yraw"), bv), writes=(tb("y2_b"),))
                def y2src(i):
                    return y2_b[:, i * T:(i + 1) * T]

                def y2sq(i):
                    return sq_b[:, i * T:(i + 1) * T]
                ln_stats(y2src, ND_DI, y2sq, True, tb("y2_b"))
                tr.emit("vector", lambda e: e.tensor_copy(srow_b[:], pb4[:]),
                        reads=(PS[4],), writes=(tb("srow_b"),))
                tr.emit("vector", lambda e: e.tensor_copy(mrow_b[:], pb5[:]),
                        reads=(PS[5],), writes=(tb("mrow_b"),))
                sba = srow_b.ap()
                s_bc = bass.AP(tensor=sba.tensor, offset=sba.offset,
                               ap=[[sba.ap[0][0], 128], [0, ND_DI], [1, T]])
                mba = mrow_b.ap()
                m_bc = bass.AP(tensor=mba.tensor, offset=mba.offset,
                               ap=[[mba.ap[0][0], 128], [0, ND_DI], [1, T]])
                tr.emit("vector", lambda e, s_bc=s_bc: e.tensor_tensor(
                    hbuf2_v[:], y2_b[:], s_bc, ALU.mult),
                    reads=(tb("y2_b"), tb("srow_b")), writes=(tb("hbuf_b"),))
                tr.emit("vector", lambda e, m_bc=m_bc: e.tensor_tensor(
                    hbuf2_v[:], hbuf2_v[:], m_bc, ALU.subtract),
                    reads=(tb("hbuf_b"), tb("mrow_b")), writes=(tb("hbuf_b"),))
                for i in range(ND_DI):
                    tr.emit("scalar", lambda e, i=i: e.activation(
                        q1[:], hbuf2_v[:, i * T:(i + 1) * T], AF.Identity,
                        scale=vap("mlng", i), bias=vap("mlnb", i)),
                        reads=(tb("hbuf_b"), bv), writes=(tb("q1"),))
                    tr.emit("vector", lambda e, i=i: e.tensor_tensor(
                        brep_b[:, i * T:(i + 1) * T], q1[:], sres_b[:, i * T:(i + 1) * T], ALU.mult),
                        reads=(tb("q1"), tb("sres_b")), writes=(tb("brep_b"),))
                for dout in range(ND_D):
                    slot = dout % 2
                    pst, psb_ = PS[slot], pb[slot]
                    wtile, wb = stream_w(c_outproj, l * ND_D + dout, KC_DI, slot)
                    tr.emit_group("tensor", [
                        (lambda e, kc=kc, st=(kc == 0), sp=(kc == KC_DI - 1), wtile=wtile, psb_=psb_:
                         e.matmul(psb_[:], wtile[:, kc * 128:(kc + 1) * 128],
                                  brep_b[:, kc * T:(kc + 1) * T], start=st, stop=sp))
                        for kc in range(KC_DI)], reads=(wb, tb("brep_b")), writes=(pst,))
                    zsl = dout % 2
                    tr.emit("vector", lambda e, dout=dout, psb_=psb_, zsl=zsl: e.tensor_tensor(
                        zo_pp[:, zsl * T:(zsl + 1) * T], psb_[:],
                        z_in[:, dout * T:(dout + 1) * T], ALU.add),
                        reads=(pst, bz), writes=(tb("zo_pp"),))
                    dma_out(_dap(zdst, dout * 128 * S + t0, [(S, 128), (1, T)]),
                            zo_pp[:, zsl * T:(zsl + 1) * T], (tb("zo_pp"),))
            tr.barrier()

        if max_phase >= 6:
            ld_vec("b1", E["b1"][:], ND_D)
            ld_vec("b2", E["b2"][:], ND_D)
            ld_vec("bff", E["bff"][:], NT_FF)
            zsrc = zA if L % 2 == 0 else zB
            for ti in range(ntile):
                t0 = ti * T
                bz = tb("z_in")
                z_tile_load(z_in, bz, zsrc, t0, ND_D)
                bx1 = tb("f2_a")
                z_tile_load(f2_a, bx1, x1s, t0, ND_D)
                bl1 = tb("ln1_b")
                z_tile_load(ln1_b, bl1, ln1s, t0, ND_D)
                for i in range(ND_D):
                    tr.emit("vector", lambda e, i=i: e.tensor_copy(
                        xn_b[:, i * T:(i + 1) * T], z_in[:, i * T:(i + 1) * T]),
                        reads=(bz,), writes=(tb("xn_b"),))
                for dout in range(ND_D):
                    slot = dout % 2
                    pst, psb_ = PS[slot], pb[slot]
                    wtile, wb = stream_w(c_op1, dout, KC_D, slot)
                    tr.emit_group("tensor", [
                        (lambda e, kc=kc, st=(kc == 0), sp=(kc == KC_D - 1), wtile=wtile, psb_=psb_:
                         e.matmul(psb_[:], wtile[:, kc * 128:(kc + 1) * 128],
                                  xn_b[:, kc * T:(kc + 1) * T], start=st, stop=sp))
                        for kc in range(KC_D)], reads=(wb, tb("xn_b")), writes=(pst,))
                    tr.emit("scalar", lambda e, dout=dout, psb_=psb_: e.activation(
                        acts_b[:, dout * T:(dout + 1) * T], psb_[:], AF.Gelu, bias=vap("b1", dout)),
                        reads=(pst, bv), writes=(tb("acts_b"),))
                for dout in range(ND_D):
                    slot = dout % 2
                    pst, psb_ = PS[slot], pb[slot]
                    wtile, wb = stream_w(c_op2, dout, KC_D, slot)
                    tr.emit_group("tensor", [
                        (lambda e, kc=kc, st=(kc == 0), sp=(kc == KC_D - 1), wtile=wtile, psb_=psb_:
                         e.matmul(psb_[:], wtile[:, kc * 128:(kc + 1) * 128],
                                  acts_b[:, kc * T:(kc + 1) * T], start=st, stop=sp))
                        for kc in range(KC_D)], reads=(wb, tb("acts_b")), writes=(pst,))
                    tr.emit("scalar", lambda e, dout=dout, psb_=psb_: e.activation(
                        q1[:], psb_[:], AF.Identity, bias=vap("b2", dout)),
                        reads=(pst, bv), writes=(tb("q1"),))
                    tr.emit("vector", lambda e, dout=dout: e.tensor_tensor(
                        q2[:], q1[:], ln1_b[:, dout * T:(dout + 1) * T], ALU.add),
                        reads=(tb("q1"), bl1), writes=(tb("q2"),))
                    tr.emit("vector", lambda e, dout=dout: e.tensor_tensor(
                        f2_a[:, dout * T:(dout + 1) * T], q2[:],
                        f2_a[:, dout * T:(dout + 1) * T], ALU.add),
                        reads=(tb("q2"), bx1), writes=(bx1,))

                if dbg:
                    for i in range(ND_D):
                        dma_out(_dap(dbg1, i * 128 * S + t0, [(S, 128), (1, T)]),
                                f2_a[:, i * T:(i + 1) * T], (bx1,))
                        tr.emit("vector", lambda e, i=i: e.tensor_copy(
                            q1[:], acts_b[:, i * T:(i + 1) * T]),
                            reads=(tb("acts_b"),), writes=(tb("q1"),))
                        dma_out(_dap(dbg2, i * 128 * S + t0, [(S, 128), (1, T)]),
                                q1[:], (tb("q1"),))

                def srcf6(i):
                    return f2_a[:, i * T:(i + 1) * T]

                def sqf6(i):
                    return sq_b[:, i * T:(i + 1) * T]
                ln_stats(srcf6, ND_D, sqf6, False, bx1)
                ln_apply(srcf6, ND_D,
                         lambda i: (xn_b[:, i * T:(i + 1) * T], tb("xn_b")), bx1)

                def xtile6(kc):
                    return xn_b[:, kc * T:(kc + 1) * T]

                def consume6(dout, pst, psb_):
                    zsl = dout % 2
                    tr.emit("vector", lambda e, dout=dout, psb_=psb_, zsl=zsl: e.scalar_tensor_tensor(
                        zo_pp[:, zsl * T:(zsl + 1) * T], psb_[:], 0.5,
                        f2_a[:, dout * T:(dout + 1) * T], ALU.mult, ALU.add),
                        reads=(pst, bx1), writes=(tb("zo_pp"),))
                    dma_out(_dap(out_t, dout * 128 * S + t0, [(S, 128), (1, T)]),
                            zo_pp[:, zsl * T:(zsl + 1) * T], (tb("zo_pp"),))
                ffn_tree(xtile6, consume6, tb("xn_b"))
        tr.barrier()

        @block.sync
        def _(e):
            for a in tr.q["sync"]:
                a(e)

        @block.tensor
        def _(e):
            for a in tr.q["tensor"]:
                a(e)

        @block.vector
        def _(e):
            for a in tr.q["vector"]:
                a(e)

        @block.scalar
        def _(e):
            for a in tr.q["scalar"]:
                a(e)

        @block.gpsimd
        def _(e):
            for a in tr.q["gpsimd"]:
                a(e)

    return nc


def kernel(**inputs):
    g = _prep(inputs)
    hxs = g.pop("_hx_per_core")
    nc = build_program()
    in_maps = []
    for b in range(8):
        m = dict(g)
        m["hx"] = hxs[b]
        in_maps.append(m)
    res = run_bass_kernel_spmd(nc, in_maps, core_ids=list(range(8)))
    outs = [np.asarray(r["out"]).T for r in res.results]
    return np.ascontiguousarray(np.stack(outs, 0).astype(np.float32))


# version 08
# speedup vs baseline: 1.0915x; 1.0915x over previous
import numpy as np
import concourse.bass as bass
import concourse.mybir as mybir
from concourse.bass_utils import run_bass_kernel_spmd

F32 = mybir.dt.float32
BF16 = mybir.dt.bfloat16
AF = mybir.ActivationFunctionType
ALU = mybir.AluOpType
AX = mybir.AxisListType

D = 1024
DI = 2048
S = 8192
T = 512
NTILE = S // T
NS = 16
L = 4
KC_D = D // 128
KC_DI = DI // 128
ND_D = D // 128
ND_DI = DI // 128
EPS = 1e-5
DEPTH_FF = 7
PAR_FF = 8
NN_FF = 255
LOFF = [0, 1, 2, 3, 4, 5, 7, 11, 19]
NT_FF = 19
J2 = NT_FF * 128
SAME_ENGINE_EDGES = 2



def _vec2sb(v, ntiles):
    out = np.zeros(128 * ntiles, np.float32)
    out[:v.shape[0]] = v
    return np.ascontiguousarray(out.reshape(ntiles, 128).T)


def _ffn_perm():
    perm = np.full(J2, -1, np.int64)
    for d in range(DEPTH_FF + 1):
        sz = 1 << d
        base = 128 * LOFF[d]
        for p in range(PAR_FF):
            for i in range(sz):
                perm[base + p * sz + i] = p * NN_FF + (sz - 1 + i)
    return perm


def _tree_blocks():
    blocks = []
    for d in range(DEPTH_FF):
        psz = 1 << d
        csz = 1 << (d + 1)
        nchild_rows = PAR_FF * csz
        nct = (nchild_rows + 127) // 128
        for ctl in range(nct):
            ct = LOFF[d + 1] + ctl
            matL = np.zeros((128, 128), np.float32)
            matR = np.zeros((128, 128), np.float32)
            pt = None
            for cl in range(min(128, nchild_rows - ctl * 128)):
                crow = ctl * 128 + cl
                p = crow // csz
                ip = crow % csz
                prow = p * psz + (ip >> 1)
                side = ip & 1
                pt_this = LOFF[d] + prow // 128
                if pt is None:
                    pt = pt_this
                assert pt == pt_this
                (matR if side else matL)[prow % 128, cl] = 1.0
            blocks.append((d, pt, ct, 0, matL))
            blocks.append((d, pt, ct, 1, matR))
    return blocks


_TREE_BLOCKS = _tree_blocks()
_PERM = _ffn_perm()


def _prep(inputs):
    g = {}
    hs = np.asarray(inputs["L0_attn_hidden_states"], np.float32)
    g["_hx_per_core"] = [np.ascontiguousarray(hs[b].T) for b in range(8)]

    W_in = np.asarray(inputs["L0_ffn_W_in"], np.float32)
    b_in = np.asarray(inputs["L0_ffn_b_in"], np.float32)
    W_out = np.asarray(inputs["L0_ffn_W_out"], np.float32)
    WiT = np.zeros((D, J2), np.float32)
    WoT = np.zeros((J2, D), np.float32)
    bff = np.zeros(J2, np.float32)
    valid = _PERM >= 0
    WiT[:, valid] = W_in[_PERM[valid], :].T
    WoT[valid, :] = W_out[:, _PERM[valid]].T
    bff[valid] = b_in[_PERM[valid]]
    g["wffinT"] = np.ascontiguousarray(WiT)
    g["wffoutT"] = np.ascontiguousarray(WoT)
    g["bff"] = _vec2sb(bff, NT_FF)
    g["treeblk"] = np.ascontiguousarray(
        np.stack([m for (_, _, _, _, m) in _TREE_BLOCKS], 0))
    rootsel = np.zeros((1, 128), np.float32)
    rootsel[0, :PAR_FF] = 1.0
    g["rootsel"] = rootsel

    g["op1T"] = np.ascontiguousarray(np.asarray(inputs["L0_attn_op1_w"], np.float32).T)
    g["op2T"] = np.ascontiguousarray(np.asarray(inputs["L0_attn_op2_w"], np.float32).T)
    g["b1"] = _vec2sb(np.asarray(inputs["L0_attn_op1_b"], np.float32), ND_D)
    g["b2"] = _vec2sb(np.asarray(inputs["L0_attn_op2_b"], np.float32), ND_D)
    g["inng"] = _vec2sb(np.asarray(inputs["L0_attn_in_norm_g"], np.float32), ND_D)
    g["innb"] = _vec2sb(np.asarray(inputs["L0_attn_in_norm_b"], np.float32), ND_D)

    lng = np.asarray(inputs["L0_attn_ln_g"], np.float32)
    lnb = np.asarray(inputs["L0_attn_ln_b"], np.float32)
    g["lng"] = np.stack([_vec2sb(lng[l], ND_D) for l in range(L)])
    g["lnb"] = np.stack([_vec2sb(lnb[l], ND_D) for l in range(L)])
    g["inprojT"] = np.ascontiguousarray(
        np.asarray(inputs["L0_attn_in_proj_w"], np.float32).transpose(0, 2, 1))
    g["xprojT"] = np.ascontiguousarray(
        np.asarray(inputs["L0_attn_x_proj_w"], np.float32).transpose(0, 2, 1))
    g["outprojT"] = np.ascontiguousarray(
        np.asarray(inputs["L0_attn_out_proj_w"], np.float32).transpose(0, 2, 1))
    cw = np.asarray(inputs["L0_attn_conv_w"], np.float32)
    g["convw"] = np.ascontiguousarray(np.stack([
        np.stack([np.stack([_vec2sb(cw[l, c, :, k], ND_DI) for k in range(4)])
                  for c in range(2)]) for l in range(L)]))
    cb = np.asarray(inputs["L0_attn_conv_b"], np.float32)
    g["convb"] = np.ascontiguousarray(np.stack([
        np.stack([_vec2sb(cb[l, c], ND_DI) for c in range(2)]) for l in range(L)]))
    g["dtw"] = np.stack([_vec2sb(np.asarray(inputs["L0_attn_dt_proj_w"], np.float32)[l, :, 0], ND_DI) for l in range(L)])
    g["dtb"] = np.stack([_vec2sb(np.asarray(inputs["L0_attn_dt_proj_b"], np.float32)[l], ND_DI) for l in range(L)])
    g["Dv"] = np.stack([_vec2sb(np.asarray(inputs["L0_attn_D"], np.float32)[l], ND_DI) for l in range(L)])
    g["mlng"] = np.stack([_vec2sb(np.asarray(inputs["L0_attn_mamba_ln_g"], np.float32)[l], ND_DI) for l in range(L)])
    g["mlnb"] = np.stack([_vec2sb(np.asarray(inputs["L0_attn_mamba_ln_b"], np.float32)[l], ND_DI) for l in range(L)])
    g["onesm"] = np.ones((128, 128), np.float32)
    sel = np.zeros((48, 16 * 128), np.float32)
    for n in range(16):
        sel[n, n * 128:(n + 1) * 128] = 1.0
        sel[32 + n, n * 128:(n + 1) * 128] = 1.0
    g["bcsel"] = sel
    return g



class Buf:
    __slots__ = ("name", "writer", "readers")

    def __init__(self, name):
        self.name = name
        self.writer = None
        self.readers = []


class Tracker:
    ENGINES = ("sync", "tensor", "vector", "scalar", "gpsimd")

    def __init__(self, sems, lane_pool):
        self.sems = dict(sems)
        self.lane_pool = list(lane_pool)
        self.cnt = {e: 0 for e in self.ENGINES}
        self.q = {e: [] for e in self.ENGINES}
        self.trace = []

    def lane(self, name):
        if name not in self.sems:
            assert self.lane_pool, "out of DMA lane semaphores"
            self.sems[name] = self.lane_pool.pop()
            self.cnt[name] = 0
        return name

    def _waits_for(self, eng, reads, writes):
        need = {}

        def add(dep):
            if dep is None:
                return
            wsn, wcnt = dep
            if wsn == eng:
                if eng == "tensor":
                    return
                if SAME_ENGINE_EDGES is False:
                    return
                if SAME_ENGINE_EDGES is not True and                         self.cnt[eng] - wcnt >= SAME_ENGINE_EDGES:
                    return
            if need.get(wsn, 0) < wcnt:
                need[wsn] = wcnt
        for b in reads:
            add(b.writer)
        for b in writes:
            add(b.writer)
            for r in b.readers:
                add(r)
        return need

    def _push(self, eng, need, fns, inc, sn):
        sem = self.sems

        def action(e, need=need, fns=fns, inc=inc, sn=sn):
            for weng in sorted(need):
                e.wait_ge(sem[weng], need[weng])
            last = None
            for f in fns:
                last = f(e)
            last.then_inc(sem[sn], inc)
        self.q[eng].append(action)

    def emit(self, eng, fn, reads=(), writes=(), inc=1):
        return self.emit_group(eng, [fn], reads=reads, writes=writes, inc=inc)

    def emit_group(self, eng, fns, reads=(), writes=(), inc=1, sem_name=None):
        sn = sem_name or eng
        need = self._waits_for(eng, reads, writes)
        self.cnt[sn] += inc
        self.trace.append((eng, sn, self.cnt[sn], tuple(b.name for b in reads),
                           tuple(b.name for b in writes), dict(need), len(fns)))
        self._push(eng, need, fns, inc, sn)
        dep = (sn, self.cnt[sn])
        for b in writes:
            b.writer = dep
            b.readers = []
        for b in reads:
            if b not in writes:
                b.readers.append(dep)
        return dep

    def barrier(self):
        snap = dict(self.cnt)
        sem = self.sems
        for eng in self.ENGINES:
            def action(e, snap=snap, eng=eng):
                for osn, v in snap.items():
                    if osn != eng and v > 0:
                        e.wait_ge(sem[osn], v)
            self.q[eng].append(action)


def _ap(handle, offset, dims, npart=128):
    base = handle.ap()
    return bass.AP(tensor=base.tensor, offset=base.offset + offset,
                   ap=[[base.ap[0][0], npart]] + [[s, c] for (s, c) in dims])


def _dap(dram_ap, offset, dims):
    return bass.AP(tensor=dram_ap.tensor, offset=dram_ap.offset + offset,
                   ap=[[s, c] for (s, c) in dims])



def build_program(max_phase=99, dbg=False, ntile=NTILE):
    nc = bass.Bass("TRN2", target_bir_lowering=False)
    E = {}

    def ein(name, shape):
        E[name] = nc.dram_tensor(name, list(shape), F32, kind="ExternalInput").ap()

    ein("hx", (D, S))
    ein("wffinT", (D, J2)); ein("wffoutT", (J2, D)); ein("bff", (128, NT_FF))
    ein("treeblk", (len(_TREE_BLOCKS), 128, 128)); ein("rootsel", (1, 128))
    ein("op1T", (D, D)); ein("op2T", (D, D)); ein("b1", (128, ND_D)); ein("b2", (128, ND_D))
    ein("inng", (128, ND_D)); ein("innb", (128, ND_D))
    ein("lng", (L, 128, ND_D)); ein("lnb", (L, 128, ND_D))
    ein("inprojT", (L, D, 2 * DI)); ein("xprojT", (L, DI, 33)); ein("outprojT", (L, DI, D))
    ein("convw", (L, 2, 4, 128, ND_DI)); ein("convb", (L, 2, 128, ND_DI))
    ein("dtw", (L, 128, ND_DI)); ein("dtb", (L, 128, ND_DI)); ein("Dv", (L, 128, ND_DI))
    ein("mlng", (L, 128, ND_DI)); ein("mlnb", (L, 128, ND_DI))
    ein("onesm", (128, 128))
    ein("bcsel", (48, 16 * 128))

    out_t = nc.dram_tensor("out", [D, S], F32, kind="ExternalOutput").ap()
    KD = "ExternalOutput" if dbg else "Internal"
    if dbg:
        dbg1 = nc.dram_tensor("dbg1", [D, S], F32, kind="ExternalOutput").ap()
        dbg2 = nc.dram_tensor("dbg2", [D, S], F32, kind="ExternalOutput").ap()
    zA = nc.dram_tensor("zA", [D, S], F32, kind=KD).ap()
    zB = nc.dram_tensor("zB", [D, S], F32, kind=KD).ap()
    x1s = nc.dram_tensor("x1s", [D, S], F32, kind=KD).ap()
    ln1s = nc.dram_tensor("ln1s", [D, S], BF16, kind=KD).ap()
    c_inproj = nc.dram_tensor("c_inproj", [L, 2 * ND_DI, 128, KC_D * 128], BF16, kind="Internal").ap()
    c_outproj = nc.dram_tensor("c_outproj", [L, ND_D, 128, KC_DI * 128], BF16, kind="Internal").ap()
    c_wffin = nc.dram_tensor("c_wffin", [NT_FF, 128, KC_D * 128], BF16, kind="Internal").ap()
    c_wffout = nc.dram_tensor("c_wffout", [ND_D, 128, NT_FF * 128], BF16, kind="Internal").ap()
    c_op1 = nc.dram_tensor("c_op1", [ND_D, 128, KC_D * 128], BF16, kind="Internal").ap()
    c_op2 = nc.dram_tensor("c_op2", [ND_D, 128, KC_D * 128], BF16, kind="Internal").ap()

    NBLK = len(_TREE_BLOCKS)
    import contextlib
    es = contextlib.ExitStack()
    with es:
        ones_f = es.enter_context(nc.sbuf_tensor("ones_f", [128, 128], F32))
        ones_b = es.enter_context(nc.sbuf_tensor("ones_b", [128, 128], BF16))
        treesb = es.enter_context(nc.sbuf_tensor("treesb", [128, NBLK * 128], BF16))
        rootsb = es.enter_context(nc.sbuf_tensor("rootsb", [1, 128], BF16))
        onerow_b = es.enter_context(nc.sbuf_tensor("onerow_b", [1, T], BF16))
        wstg = es.enter_context(nc.sbuf_tensor("wstg", [128, 1024], F32))
        wstgb = es.enter_context(nc.sbuf_tensor("wstgb", [128, 1024], BF16))
        xpw_b = es.enter_context(nc.sbuf_tensor("xpw_b", [128, KC_DI * 33], BF16))
        wbufA = es.enter_context(nc.sbuf_tensor("wbufA", [128, NT_FF * 128], BF16))
        wbufB = es.enter_context(nc.sbuf_tensor("wbufB", [128, NT_FF * 128], BF16))
        vecs = es.enter_context(nc.sbuf_tensor("vecs", [128, 320], F32))
        z_in = es.enter_context(nc.sbuf_tensor("z_in", [128, ND_D * T], F32))
        zo_pp = es.enter_context(nc.sbuf_tensor("zo_pp", [128, 2 * T], F32))
        xn_b = es.enter_context(nc.sbuf_tensor("xn_b", [128, ND_D * T], BF16))
        cin_b = es.enter_context(nc.sbuf_tensor("cin_b", [128, 2 * (T + 4)], BF16))
        xtails = es.enter_context(nc.sbuf_tensor("xtails", [128, ND_DI * 4], BF16))
        cmid_b = es.enter_context(nc.sbuf_tensor("cmid_b", [128, 2 * (T + 4)], BF16))
        ctails = es.enter_context(nc.sbuf_tensor("ctails", [128, ND_DI * 4], BF16))
        arena = es.enter_context(nc.sbuf_tensor("arena", [128, 49152], BF16))
        projsb = es.enter_context(nc.sbuf_tensor("projsb", [33, T], F32))
        bc_sb = es.enter_context(nc.sbuf_tensor("bc_sb", [48, T], BF16))
        bcsel_b = es.enter_context(nc.sbuf_tensor("bcsel_b", [48, 16 * 128], BF16))
        dtt = es.enter_context(nc.sbuf_tensor("dtt", [128, T], F32))
        dtxb = es.enter_context(nc.sbuf_tensor("dtxb", [128, T], BF16))
        abuf = es.enter_context(nc.sbuf_tensor("abuf", [128, 3 * T], BF16))

        yraw = es.enter_context(nc.sbuf_tensor("yraw", [128, T], F32))
        q1 = es.enter_context(nc.sbuf_tensor("q1", [128, T], F32))
        q2 = es.enter_context(nc.sbuf_tensor("q2", [128, T], F32))
        states_sb = es.enter_context(nc.sbuf_tensor("states", [128, ND_DI * NS], F32))
        convacc = es.enter_context(nc.sbuf_tensor("convacc", [128, T], F32))
        sq_b = es.enter_context(nc.sbuf_tensor("sq_b", [128, ND_DI * T], BF16))
        statA = es.enter_context(nc.sbuf_tensor("statA", [128, 3 * T], F32))
        srow_b = es.enter_context(nc.sbuf_tensor("srow_b", [128, T], BF16))
        mrow_b = es.enter_context(nc.sbuf_tensor("mrow_b", [128, T], BF16))
        cst = es.enter_context(nc.sbuf_tensor("cst", [128, 4], F32))

        pb0 = es.enter_context(nc.psum_tensor("pb0", [128, T], F32))
        pb1 = es.enter_context(nc.psum_tensor("pb1", [128, T], F32))
        pb2 = es.enter_context(nc.psum_tensor("pb2", [128, T], F32))
        pb3 = es.enter_context(nc.psum_tensor("pb3", [128, T], F32))
        pb4 = es.enter_context(nc.psum_tensor("pb4", [128, T], F32))
        pb5 = es.enter_context(nc.psum_tensor("pb5", [128, T], F32))
        pb6 = es.enter_context(nc.psum_tensor("pb6", [128, T], F32))
        pb7 = es.enter_context(nc.psum_tensor("pb7", [128, T], F32))
        s_sync = es.enter_context(nc.semaphore("s_sync"))
        lane_sems = [es.enter_context(nc.semaphore(f"lane{i}")) for i in range(24)]
        s_tensor = es.enter_context(nc.semaphore("s_tensor"))
        s_vector = es.enter_context(nc.semaphore("s_vector"))
        s_scalar = es.enter_context(nc.semaphore("s_scalar"))
        s_gpsimd = es.enter_context(nc.semaphore("s_gpsimd"))
        block = es.enter_context(nc.Block())
        pb = [pb0, pb1, pb2, pb3, pb4, pb5, pb6, pb7]

        class TView:

            def __init__(self, h, off, nels):
                self.h, self.off, self.nels = h, off, nels

            def ap(self):
                b = self.h.ap()
                return bass.AP(tensor=b.tensor, offset=b.offset + self.off,
                               ap=[[b.ap[0][0], 128], [1, self.nels]])

            def __getitem__(self, key):
                base = self.ap()
                if isinstance(key, slice):
                    return base
                ps, fs = key
                assert (ps.start or 0) == 0
                f0 = fs.start or 0
                f1 = fs.stop if fs.stop is not None else self.nels
                return bass.AP(tensor=base.tensor, offset=base.offset + f0,
                               ap=[[base.ap[0][0], 128], [1, f1 - f0]])

        arena_f = arena.bitcast(F32)
        ufull_b = TView(sq_b, 0, NS * T)
        hbuf2_v = TView(arena, 24576, ND_DI * T)
        xi2_b = TView(arena, 0, ND_DI * T)
        sres_b = TView(arena, 8192, ND_DI * T)
        y2_b = TView(arena, 16384, ND_DI * T)
        hbuf_b = TView(arena, 24576, NS * T)
        brep_b = TView(arena, 32768, NS * T)
        crep_b = TView(arena, 40960, NS * T)
        acts_b = TView(arena, 0, NT_FF * T)
        dec_b = TView(arena, 9728, NT_FF * T)
        dmap_b = TView(arena, 19456, NT_FF * T)
        oLR_b = TView(arena, 29184, 8 * T)
        ln1_b = TView(arena, 33280, ND_D * T)
        f2_a = TView(arena_f, 18688, ND_D * T)
        tr = Tracker({"sync": s_sync, "tensor": s_tensor, "vector": s_vector,
                      "scalar": s_scalar, "gpsimd": s_gpsimd}, lane_sems)
        build_program.last_trace = tr.trace
        B = {}

        def tb(name):
            if name not in B:
                B[name] = Buf(name)
            return B[name]

        PS = [tb(f"ps{i}") for i in range(8)]
        bv = tb("vecs")

        VEC = {}
        _vc = 0
        for nm, n in [("lng", ND_D), ("lnb", ND_D), ("inng", ND_D), ("innb", ND_D),
                      ("cw00", ND_DI), ("cw01", ND_DI), ("cw02", ND_DI), ("cw03", ND_DI),
                      ("cw10", ND_DI), ("cw11", ND_DI), ("cw12", ND_DI), ("cw13", ND_DI),
                      ("cb0", ND_DI), ("cb1", ND_DI),
                      ("dtw", ND_DI), ("dtb", ND_DI), ("Dv", ND_DI),
                      ("mlng", ND_DI), ("mlnb", ND_DI),
                      ("bff", NT_FF), ("b1", ND_D), ("b2", ND_D)]:
            VEC[nm] = _vc
            _vc += n
        assert _vc <= 320

        def vap(name, i):
            c = VEC[name] + i
            return vecs[:, c:c + 1]

        def ld_vec(dst, src_ap, n):
            c = VEC[dst]
            tr.emit_group("sync", [lambda e, c=c, s=src_ap, n=n: e.dma_start(
                out=vecs[:, c:c + n], in_=s)], writes=(bv,), inc=16,
                sem_name=tr.lane("L_vecs"))

        def dma_in(dst_ap, src_ap, wb, rb=()):
            ln_ = tr.lane("L_" + wb[0].name)
            tr.emit_group("sync", [lambda e, d=dst_ap, s=src_ap: e.dma_start(out=d, in_=s)],
                          reads=rb, writes=wb, inc=16, sem_name=ln_)

        def dma_out(dst_ap, src_ap, rb, wb=()):
            ln_ = tr.lane("S_" + rb[0].name)
            tr.emit_group("gpsimd", [lambda e, d=dst_ap, s=src_ap: e.dma_start(out=d, in_=s)],
                          reads=rb, writes=wb, inc=16, sem_name=ln_)

        dma_in(ones_f[:], E["onesm"][:], (tb("ones_f"),))
        tr.emit("vector", lambda e: e.tensor_copy(ones_b[:], ones_f[:]),
                reads=(tb("ones_f"),), writes=(tb("ones_b"),))
        tr.emit("gpsimd", lambda e: e.memset(onerow_b[:], 1.0), writes=(tb("onerow_b"),))
        tr.emit("gpsimd", lambda e: e.memset(cst[:, 0:1], EPS), writes=(tb("cst"),))
        tr.emit("gpsimd", lambda e: e.memset(cst[:, 1:2], -0.5), writes=(tb("cst"),))
        tr.emit("gpsimd", lambda e: e.memset(cst[:, 2:3], 1.0), writes=(tb("cst"),))

        def cache_build(src2d, rows, cols, dst_ap_fn, wname):
            CC = 1024
            for ch in range(rows // 128):
                for c0 in range(0, cols, CC):
                    cw_ = min(CC, cols - c0)
                    dma_in(wstg[:, 0:cw_], src2d[ch * 128:(ch + 1) * 128, c0:c0 + cw_],
                           (tb("wstg"),))
                    tr.emit("vector", lambda e, c=cw_: e.tensor_copy(wstgb[:, 0:c], wstg[:, 0:c]),
                            reads=(tb("wstg"),), writes=(tb("wstgb"),))
                    dma_out(dst_ap_fn(ch, c0, cw_), wstgb[:, 0:cw_], (tb("wstgb"),))

        for l in range(L):
            def dst_ip(kc, c0, cw_, l=l):
                return _dap(c_inproj,
                            l * (2 * ND_DI * 128 * KC_D * 128)
                            + (c0 // 128) * (128 * KC_D * 128) + kc * 128,
                            [(KC_D * 128, 128), (128 * KC_D * 128, cw_ // 128), (1, 128)])
            cache_build(E["inprojT"][l], D, 2 * DI, dst_ip, "ip")

            def dst_op(kc, c0, cw_, l=l):
                return _dap(c_outproj,
                            l * (ND_D * 128 * KC_DI * 128)
                            + (c0 // 128) * (128 * KC_DI * 128) + kc * 128,
                            [(KC_DI * 128, 128), (128 * KC_DI * 128, cw_ // 128), (1, 128)])
            cache_build(E["outprojT"][l], DI, D, dst_op, "op")

        cache_build(E["wffinT"], D, J2,
                    lambda kc, c0, cw_: _dap(c_wffin, (c0 // 128) * (128 * KC_D * 128) + kc * 128,
                                             [(KC_D * 128, 128), (128 * KC_D * 128, cw_ // 128), (1, 128)]), "fi")
        cache_build(E["wffoutT"], J2, D,
                    lambda kc, c0, cw_: _dap(c_wffout, (c0 // 128) * (128 * NT_FF * 128) + kc * 128,
                                             [(NT_FF * 128, 128), (128 * NT_FF * 128, cw_ // 128), (1, 128)]), "fo")
        cache_build(E["op1T"], D, D,
                    lambda kc, c0, cw_: _dap(c_op1, (c0 // 128) * (128 * KC_D * 128) + kc * 128,
                                             [(KC_D * 128, 128), (128 * KC_D * 128, cw_ // 128), (1, 128)]), "o1")
        cache_build(E["op2T"], D, D,
                    lambda kc, c0, cw_: _dap(c_op2, (c0 // 128) * (128 * KC_D * 128) + kc * 128,
                                             [(KC_D * 128, 128), (128 * KC_D * 128, cw_ // 128), (1, 128)]), "o2")

        for b0 in range(0, NBLK, 8):
            nb = min(8, NBLK - b0)
            src = _dap(E["treeblk"], b0 * 128 * 128,
                       [(128, 128), (128 * 128, nb), (1, 128)])
            dma_in(wstg[:, 0:nb * 128], src, (tb("wstg"),))
            tr.emit("vector", lambda e, b0=b0, nb=nb: e.tensor_copy(
                treesb[:, b0 * 128:(b0 + nb) * 128], wstg[:, 0:nb * 128]),
                reads=(tb("wstg"),), writes=(tb("treesb"),))
        for c0 in (0, 1024):
            dma_in(wstg[0:48, 0:1024], E["bcsel"][:, c0:c0 + 1024], (tb("wstg"),))
            tr.emit("vector", lambda e, c0=c0: e.tensor_copy(
                bcsel_b[0:48, c0:c0 + 1024], wstg[0:48, 0:1024]),
                reads=(tb("wstg"),), writes=(tb("bcsel_b"),))
        dma_in(wstg[0:1, 0:128], E["rootsel"][:], (tb("wstg"),))
        tr.emit("vector", lambda e: e.tensor_copy(rootsb[:], wstg[0:1, 0:128]),
                reads=(tb("wstg"),), writes=(tb("rootsb"),))
        tr.barrier()

        bstat = tb("stat")

        def ln_stats(src_tiles, nt, sq_view, src_is_bf, src_buf):
            Dtot = float(nt * 128)
            for i in range(nt):
                tr.emit("scalar", lambda e, i=i: e.activation(sq_view(i), src_tiles(i), AF.Square),
                        reads=(src_buf,), writes=(tb("sq_b"),))
            oc = ones_b[:, 0:1] if src_is_bf else ones_f[:, 0:1]
            ob = tb("ones_b") if src_is_bf else tb("ones_f")
            tr.emit_group("tensor", [
                (lambda e, i=i, st=(i == 0), sp=(i == nt - 1): e.matmul(
                    pb2[0:1, :], oc, src_tiles(i), start=st, stop=sp))
                for i in range(nt)], reads=(src_buf, ob), writes=(PS[2],))
            tr.emit_group("tensor", [
                (lambda e, i=i, st=(i == 0), sp=(i == nt - 1): e.matmul(
                    pb3[0:1, :], ones_b[:, 0:1], sq_view(i), start=st, stop=sp))
                for i in range(nt)], reads=(tb("sq_b"), tb("ones_b")), writes=(PS[3],))
            tr.emit("vector", lambda e: e.tensor_scalar(statA[0:1, 0:T], pb2[0:1, :], 1.0 / Dtot, None, ALU.mult),
                    reads=(PS[2],), writes=(bstat,))
            tr.emit("vector", lambda e: e.tensor_scalar(statA[32:33, 0:T], pb3[0:1, :], 1.0 / Dtot, None, ALU.mult),
                    reads=(PS[3],), writes=(bstat,))
            tr.emit("vector", lambda e: e.tensor_tensor(statA[32:33, T:2 * T], statA[0:1, 0:T], statA[0:1, 0:T], ALU.mult),
                    reads=(bstat,), writes=(bstat,))
            tr.emit("vector", lambda e: e.tensor_tensor(statA[64:65, 0:T], statA[32:33, 0:T], statA[32:33, T:2 * T], ALU.subtract),
                    reads=(bstat,), writes=(bstat,))
            tr.emit("scalar", lambda e: e.activation(statA[64:65, T:2 * T], statA[64:65, 0:T], AF.Ln, bias=cst[64:65, 0:1]),
                    reads=(bstat, tb("cst")), writes=(bstat,))
            tr.emit("scalar", lambda e: e.activation(statA[0:1, T:2 * T], statA[64:65, T:2 * T], AF.Exp, scale=cst[64:65, 1:2]),
                    reads=(bstat, tb("cst")), writes=(bstat,))
            tr.emit("vector", lambda e: e.tensor_tensor(statA[0:1, 2 * T:3 * T], statA[0:1, 0:T], statA[0:1, T:2 * T], ALU.mult),
                    reads=(bstat,), writes=(bstat,))
            tr.emit("tensor", lambda e: e.matmul(pb4[:], ones_f[0:1, :], statA[0:1, T:2 * T], start=True, stop=True),
                    reads=(bstat, tb("ones_f")), writes=(PS[4],))
            tr.emit("tensor", lambda e: e.matmul(pb5[:], ones_f[0:1, :], statA[0:1, 2 * T:3 * T], start=True, stop=True),
                    reads=(bstat, tb("ones_f")), writes=(PS[5],))

        def ln_apply(src_tiles, nt, out_fn, src_buf, gain=None, bias=None):
            for i in range(nt):
                tr.emit("vector", lambda e, i=i: e.tensor_tensor(q1[:], src_tiles(i), pb4[:], ALU.mult),
                        reads=(src_buf, PS[4]), writes=(tb("q1"),))
                if gain is None:
                    oap, ob = out_fn(i)
                    tr.emit("vector", lambda e, oap=oap: e.tensor_tensor(oap, q1[:], pb5[:], ALU.subtract),
                            reads=(tb("q1"), PS[5]), writes=(ob,))
                else:
                    tr.emit("vector", lambda e: e.tensor_tensor(q2[:], q1[:], pb5[:], ALU.subtract),
                            reads=(tb("q1"), PS[5]), writes=(tb("q2"),))
                    oap, ob = out_fn(i)
                    tr.emit("scalar", lambda e, oap=oap, g=gain(i), bb=bias(i): e.activation(
                        oap, q2[:], AF.Identity, scale=g, bias=bb),
                        reads=(tb("q2"), bv), writes=(ob,))

        def stream_w(cache_ap, idx, ncols, slot):
            wtile = wbufA if slot == 0 else wbufB
            wb = tb("wbufA") if slot == 0 else tb("wbufB")
            src = _dap(cache_ap, idx * 128 * ncols * 128, [(ncols * 128, 128), (1, ncols * 128)])
            dma_in(_ap(wtile, 0, [(1, ncols * 128)]), src, (wb,))
            return wtile, wb

        def ffn_tree(xtile, consume, xnb_buf):
            blocks_by_ct = {}
            for bi, (d, pt, ct, side, _) in enumerate(_TREE_BLOCKS):
                blocks_by_ct.setdefault((d, ct), []).append((bi, pt, side))
            for j in range(NT_FF):
                slot = j % 2
                pst, psb_ = (PS[slot], pb[slot])
                wtile, wb = stream_w(c_wffin, j, KC_D, slot)
                tr.emit_group("tensor", [
                    (lambda e, kc=kc, st=(kc == 0), sp=(kc == KC_D - 1), wtile=wtile, psb_=psb_:
                     e.matmul(psb_[:], wtile[:, kc * 128:(kc + 1) * 128], xtile(kc), start=st, stop=sp))
                    for kc in range(KC_D)], reads=(wb, xnb_buf), writes=(pst,))
                tr.emit("scalar", lambda e, j=j, psb_=psb_: e.activation(
                    acts_b[:, j * T:(j + 1) * T], psb_[:], AF.Silu,
                    bias=vecs[:, VEC["bff"] + j:VEC["bff"] + j + 1]),
                    reads=(bv,), writes=(tb("acts_b"), pst))
                tr.emit("vector", lambda e, j=j, psb_=psb_: e.tensor_scalar(
                    dec_b[:, j * T:(j + 1) * T], psb_[:],
                    vecs[:, VEC["bff"] + j:VEC["bff"] + j + 1], 0.0, ALU.add, ALU.is_gt),
                    reads=(bv,), writes=(tb("dec_b"), pst))
            tr.emit("tensor", lambda e: e.matmul(pb2[:], rootsb[0:1, :], onerow_b[0:1, :], start=True, stop=True),
                    reads=(tb("rootsb"), tb("onerow_b")), writes=(PS[2],))
            tr.emit("vector", lambda e: e.tensor_copy(dmap_b[:, 0:T], pb2[:]),
                    reads=(PS[2],), writes=(tb("dmap_b"),))
            for d in range(DEPTH_FF):
                pts = sorted({pt for (dd, pt, ct, s2, m) in _TREE_BLOCKS if dd == d})
                pt_slot = {pt: k for k, pt in enumerate(pts)}
                for pt, k in pt_slot.items():
                    tr.emit("vector", lambda e, pt=pt, k=k: e.tensor_tensor(
                        oLR_b[:, (2 * k + 1) * T:(2 * k + 2) * T],
                        dmap_b[:, pt * T:(pt + 1) * T], dec_b[:, pt * T:(pt + 1) * T], ALU.mult),
                        reads=(tb("dmap_b"), tb("dec_b")), writes=(tb("oLR_b"),))
                    tr.emit("vector", lambda e, pt=pt, k=k: e.tensor_tensor(
                        oLR_b[:, (2 * k) * T:(2 * k + 1) * T],
                        dmap_b[:, pt * T:(pt + 1) * T], oLR_b[:, (2 * k + 1) * T:(2 * k + 2) * T], ALU.subtract),
                        reads=(tb("dmap_b"),), writes=(tb("oLR_b"),))
                cts = sorted({ct for (dd, pt, ct, s2, m) in _TREE_BLOCKS if dd == d})
                for ci, ct in enumerate(cts):
                    blks = blocks_by_ct[(d, ct)]
                    slot = 2 + ci % 2
                    pst, psb_ = PS[slot], pb[slot]
                    tr.emit_group("tensor", [
                        (lambda e, bi=bi, sl=2 * pt_slot[pt] + side, st=(ii == 0), sp=(ii == len(blks) - 1), psb_=psb_:
                         e.matmul(psb_[:], treesb[:, bi * 128:(bi + 1) * 128],
                                  oLR_b[:, sl * T:(sl + 1) * T], start=st, stop=sp))
                        for ii, (bi, pt, side) in enumerate(blks)],
                        reads=(tb("treesb"), tb("oLR_b")), writes=(pst,))
                    tr.emit("vector", lambda e, ct=ct, psb_=psb_: e.tensor_copy(
                        dmap_b[:, ct * T:(ct + 1) * T], psb_[:]),
                        reads=(pst,), writes=(tb("dmap_b"),))
            for j in range(NT_FF):
                tr.emit("vector", lambda e, j=j: e.tensor_tensor(
                    acts_b[:, j * T:(j + 1) * T], acts_b[:, j * T:(j + 1) * T],
                    dmap_b[:, j * T:(j + 1) * T], ALU.mult),
                    reads=(tb("acts_b"), tb("dmap_b")), writes=(tb("acts_b"),))
            for dout in range(ND_D):
                slot = dout % 2
                pst, psb_ = PS[slot], pb[slot]
                wtile, wb = stream_w(c_wffout, dout, NT_FF, slot)
                tr.emit_group("tensor", [
                    (lambda e, kc=kc, st=(kc == 0), sp=(kc == NT_FF - 1), wtile=wtile, psb_=psb_:
                     e.matmul(psb_[:], wtile[:, kc * 128:(kc + 1) * 128],
                              acts_b[:, kc * T:(kc + 1) * T], start=st, stop=sp))
                    for kc in range(NT_FF)], reads=(wb, tb("acts_b")), writes=(pst,))
                consume(dout, pst, psb_)

        def z_tile_load(dst_handle, dst_buf, src_dram, t0, nt):
            dma_in(_ap(dst_handle, 0, [(T, nt), (1, T)]),
                   _dap(src_dram, t0, [(S, 128), (128 * S, nt), (1, T)]), (dst_buf,))

        if max_phase >= 1:
            ld_vec("bff", E["bff"][:], NT_FF)
            ld_vec("inng", E["inng"][:], ND_D)
            ld_vec("innb", E["innb"][:], ND_D)
            for ti in range(ntile):
                t0 = ti * T
                bz = tb("z_in")
                z_tile_load(z_in, bz, E["hx"], t0, ND_D)

                def srcf(i):
                    return z_in[:, i * T:(i + 1) * T]

                def sqf(i):
                    return sq_b[:, i * T:(i + 1) * T]
                ln_stats(srcf, ND_D, sqf, False, bz)
                ln_apply(srcf, ND_D,
                         lambda i: (xn_b[:, i * T:(i + 1) * T], tb("xn_b")), bz)

                def xtile(kc):
                    return xn_b[:, kc * T:(kc + 1) * T]

                def consume(dout, pst, psb_):
                    tr.emit("vector", lambda e, dout=dout, psb_=psb_: e.scalar_tensor_tensor(
                        f2_a[:, dout * T:(dout + 1) * T], psb_[:], 0.5,
                        z_in[:, dout * T:(dout + 1) * T], ALU.mult, ALU.add),
                        reads=(pst, bz), writes=(tb("f2_a"),))
                    dma_out(_dap(x1s, dout * 128 * S + t0, [(S, 128), (1, T)]),
                            f2_a[:, dout * T:(dout + 1) * T], (tb("f2_a"),))
                ffn_tree(xtile, consume, tb("xn_b"))

                def srcf2(i):
                    return f2_a[:, i * T:(i + 1) * T]
                ln_stats(srcf2, ND_D, sqf, False, tb("f2_a"))
                ln_apply(srcf2, ND_D,
                         lambda i: (ln1_b[:, i * T:(i + 1) * T], tb("ln1_b")), tb("f2_a"))
                for i in range(ND_D):
                    dma_out(_dap(ln1s, i * 128 * S + t0, [(S, 128), (1, T)]),
                            ln1_b[:, i * T:(i + 1) * T], (tb("ln1_b"),))

                def srcf3(i):
                    return ln1_b[:, i * T:(i + 1) * T]
                ln_stats(srcf3, ND_D, sqf, True, tb("ln1_b"))
                ln_apply(srcf3, ND_D,
                         lambda i: (z_in[:, i * T:(i + 1) * T], tb("z_in")), tb("ln1_b"),
                         gain=lambda i: vap("inng", i), bias=lambda i: vap("innb", i))
                for i in range(ND_D):
                    dma_out(_dap(zA, i * 128 * S + t0, [(S, 128), (1, T)]),
                            z_in[:, i * T:(i + 1) * T], (tb("z_in"),))
            tr.barrier()

        nmamba = max(0, min(L, max_phase - 1))
        for l in range(nmamba):
            zsrc = zA if l % 2 == 0 else zB
            zdst = zB if l % 2 == 0 else zA
            ld_vec("lng", E["lng"][l], ND_D)
            ld_vec("lnb", E["lnb"][l], ND_D)
            for c in range(2):
                for k in range(4):
                    ld_vec(f"cw{c}{k}", E["convw"][l, c, k], ND_DI)
                ld_vec(f"cb{c}", E["convb"][l, c], ND_DI)
            ld_vec("dtw", E["dtw"][l], ND_DI)
            ld_vec("dtb", E["dtb"][l], ND_DI)
            ld_vec("Dv", E["Dv"][l], ND_DI)
            ld_vec("mlng", E["mlng"][l], ND_DI)
            ld_vec("mlnb", E["mlnb"][l], ND_DI)
            for kc in range(KC_DI):
                dma_in(wstg[:, 0:33], E["xprojT"][l, kc * 128:(kc + 1) * 128, :], (tb("wstg"),))
                tr.emit("vector", lambda e, kc=kc: e.tensor_copy(
                    xpw_b[:, kc * 33:(kc + 1) * 33], wstg[:, 0:33]),
                    reads=(tb("wstg"),), writes=(tb("xpw_b"),))
            tr.emit("gpsimd", lambda e: e.memset(states_sb[:], 0.0), writes=(tb("states"),))
            tr.emit("gpsimd", lambda e: e.memset(xtails[:], 0.0), writes=(tb("xtails"),))
            tr.emit("gpsimd", lambda e: e.memset(ctails[:], 0.0), writes=(tb("ctails"),))

            for ti in range(ntile):
                t0 = ti * T
                bz = tb("z_in")
                z_tile_load(z_in, bz, zsrc, t0, ND_D)

                def msrc(i):
                    return z_in[:, i * T:(i + 1) * T]

                def msq(i):
                    return sq_b[:, i * T:(i + 1) * T]
                ln_stats(msrc, ND_D, msq, False, bz)
                ln_apply(msrc, ND_D,
                         lambda i: (xn_b[:, i * T:(i + 1) * T], tb("xn_b")), bz,
                         gain=lambda i: vap("lng", i), bias=lambda i: vap("lnb", i))
                for dout in range(2 * ND_DI):
                    slot = dout % 2
                    pst, psb_ = PS[slot], pb[slot]
                    wtile, wb = stream_w(c_inproj, l * 2 * ND_DI + dout, KC_D, slot)
                    tr.emit_group("tensor", [
                        (lambda e, kc=kc, st=(kc == 0), sp=(kc == KC_D - 1), wtile=wtile, psb_=psb_:
                         e.matmul(psb_[:], wtile[:, kc * 128:(kc + 1) * 128],
                                  xn_b[:, kc * T:(kc + 1) * T], start=st, stop=sp))
                        for kc in range(KC_D)], reads=(wb, tb("xn_b")), writes=(pst,))
                    if dout >= ND_DI:
                        dd = dout - ND_DI
                        tr.emit("scalar", lambda e, dd=dd, psb_=psb_: e.activation(
                            sres_b[:, dd * T:(dd + 1) * T], psb_[:], AF.Sigmoid),
                            reads=(pst,), writes=(tb("sres_b"),))
                        continue
                    dd = dout
                    ib = slot * (T + 4)
                    tr.emit("gpsimd", lambda e, ib=ib, dd=dd: e.tensor_copy(
                        cin_b[:, ib + 1:ib + 4], xtails[:, dd * 4:dd * 4 + 3]),
                        reads=(tb("xtails"),), writes=(tb("cin_b"),))
                    tr.emit("scalar", lambda e, ib=ib, psb_=psb_: e.activation(
                        cin_b[:, ib + 4:ib + 4 + T], psb_[:], AF.Silu),
                        reads=(pst,), writes=(tb("cin_b"),))
                    tr.emit("gpsimd", lambda e, ib=ib, dd=dd: e.tensor_copy(
                        xtails[:, dd * 4:dd * 4 + 3], cin_b[:, ib + 1 + T:ib + 4 + T]),
                        reads=(tb("cin_b"),), writes=(tb("xtails"),))
                    tr.emit("vector", lambda e, ib=ib, dd=dd: e.tensor_scalar(
                        convacc[:], cin_b[:, ib + 1:ib + 1 + T], vap("cw00", dd), vap("cb0", dd), ALU.mult, ALU.add),
                        reads=(tb("cin_b"), bv), writes=(tb("convacc"),))
                    for k in range(1, 4):
                        tr.emit("vector", lambda e, ib=ib, dd=dd, k=k: e.scalar_tensor_tensor(
                            convacc[:], cin_b[:, ib + 1 + k:ib + 1 + k + T], vap(f"cw0{k}", dd),
                            convacc[:], ALU.mult, ALU.add),
                            reads=(tb("cin_b"), tb("convacc"), bv), writes=(tb("convacc"),))
                    tr.emit("gpsimd", lambda e, ib=ib, dd=dd: e.tensor_copy(
                        cmid_b[:, ib + 1:ib + 4], ctails[:, dd * 4:dd * 4 + 3]),
                        reads=(tb("ctails"),), writes=(tb("cmid_b"),))
                    tr.emit("scalar", lambda e, ib=ib: e.activation(
                        cmid_b[:, ib + 4:ib + 4 + T], convacc[:], AF.Silu),
                        reads=(tb("convacc"),), writes=(tb("cmid_b"),))
                    tr.emit("gpsimd", lambda e, ib=ib, dd=dd: e.tensor_copy(
                        ctails[:, dd * 4:dd * 4 + 3], cmid_b[:, ib + 1 + T:ib + 4 + T]),
                        reads=(tb("cmid_b"),), writes=(tb("ctails"),))
                    tr.emit("vector", lambda e, ib=ib, dd=dd: e.tensor_scalar(
                        convacc[:], cmid_b[:, ib + 1:ib + 1 + T], vap("cw10", dd), vap("cb1", dd), ALU.mult, ALU.add),
                        reads=(tb("cmid_b"), bv), writes=(tb("convacc"),))
                    for k in range(1, 4):
                        tr.emit("vector", lambda e, ib=ib, dd=dd, k=k: e.scalar_tensor_tensor(
                            convacc[:], cmid_b[:, ib + 1 + k:ib + 1 + k + T], vap(f"cw1{k}", dd),
                            convacc[:], ALU.mult, ALU.add),
                            reads=(tb("cmid_b"), tb("convacc"), bv), writes=(tb("convacc"),))
                    tr.emit("scalar", lambda e, dd=dd: e.activation(
                        xi2_b[:, dd * T:(dd + 1) * T], convacc[:], AF.Silu),
                        reads=(tb("convacc"),), writes=(tb("xi2_b"),))
                tr.emit_group("tensor", [
                    (lambda e, kc=kc, st=(kc == 0), sp=(kc == KC_DI - 1): e.matmul(
                        pb2[0:1, :], xpw_b[:, kc * 33:kc * 33 + 1],
                        xi2_b[:, kc * T:(kc + 1) * T], start=st, stop=sp))
                    for kc in range(KC_DI)], reads=(tb("xpw_b"), tb("xi2_b")), writes=(PS[2],))
                tr.emit_group("tensor", [
                    (lambda e, kc=kc, st=(kc == 0), sp=(kc == KC_DI - 1): e.matmul(
                        pb6[0:16, :], xpw_b[:, kc * 33 + 1:kc * 33 + 17],
                        xi2_b[:, kc * T:(kc + 1) * T], start=st, stop=sp))
                    for kc in range(KC_DI)], reads=(tb("xpw_b"), tb("xi2_b")), writes=(PS[6],))
                tr.emit_group("tensor", [
                    (lambda e, kc=kc, st=(kc == 0), sp=(kc == KC_DI - 1): e.matmul(
                        pb7[0:16, :], xpw_b[:, kc * 33 + 17:kc * 33 + 33],
                        xi2_b[:, kc * T:(kc + 1) * T], start=st, stop=sp))
                    for kc in range(KC_DI)], reads=(tb("xpw_b"), tb("xi2_b")), writes=(PS[7],))
                tr.emit("vector", lambda e: e.tensor_copy(projsb[0:1, :], pb2[0:1, :]),
                        reads=(PS[2],), writes=(tb("projsb"),))
                tr.emit("vector", lambda e: e.tensor_copy(bc_sb[0:16, :], pb6[0:16, :]),
                        reads=(PS[6],), writes=(tb("bc_sb"),))
                tr.emit("vector", lambda e: e.tensor_copy(bc_sb[32:48, :], pb7[0:16, :]),
                        reads=(PS[7],), writes=(tb("bc_sb"),))
                tr.emit("tensor", lambda e: e.matmul(pb3[:], ones_f[0:1, :], projsb[0:1, :], start=True, stop=True),
                        reads=(tb("projsb"), tb("ones_f")), writes=(PS[3],))
                for n in range(NS):
                    slot = 6 + n % 2
                    pst, psb_ = PS[slot], pb[slot]
                    tr.emit("tensor", lambda e, n=n, psb_=psb_: e.matmul(
                        psb_[:], bcsel_b[0:16, n * 128:(n + 1) * 128], bc_sb[0:16, :],
                        start=True, stop=True),
                        reads=(tb("bcsel_b"), tb("bc_sb")), writes=(pst,))
                    tr.emit("vector", lambda e, n=n, psb_=psb_: e.tensor_copy(
                        brep_b[:, n * T:(n + 1) * T], psb_[:]), reads=(pst,), writes=(tb("brep_b"),))
                for n in range(NS):
                    slot = 6 + n % 2
                    pst, psb_ = PS[slot], pb[slot]
                    tr.emit("tensor", lambda e, n=n, psb_=psb_: e.matmul(
                        psb_[:], bcsel_b[32:48, n * 128:(n + 1) * 128], bc_sb[32:48, :],
                        start=True, stop=True),
                        reads=(tb("bcsel_b"), tb("bc_sb")), writes=(pst,))
                    tr.emit("vector", lambda e, n=n, psb_=psb_: e.tensor_copy(
                        crep_b[:, n * T:(n + 1) * T], psb_[:]), reads=(pst,), writes=(tb("crep_b"),))
                for ds in range(ND_DI):
                    tr.emit("scalar", lambda e, ds=ds: e.activation(
                        q1[:], pb3[:], AF.Exp, scale=vap("dtw", ds), bias=vap("dtb", ds)),
                        reads=(PS[3], bv), writes=(tb("q1"),))
                    tr.emit("scalar", lambda e: e.activation(dtt[:], q1[:], AF.Ln, bias=cst[:, 2:3]),
                            reads=(tb("q1"), tb("cst")), writes=(tb("dtt"),))
                    tr.emit("gpsimd", lambda e, ds=ds: e.tensor_tensor(
                        dtxb[:], dtt[:], xi2_b[:, ds * T:(ds + 1) * T], ALU.mult),
                        reads=(tb("dtt"), tb("xi2_b")), writes=(tb("dtxb"),))
                    dba = dtxb.ap()
                    dtx_bc = bass.AP(tensor=dba.tensor, offset=dba.offset,
                                     ap=[[dba.ap[0][0], 128], [0, NS], [1, T]])
                    tr.emit("vector", lambda e, dtx_bc=dtx_bc: e.tensor_tensor(
                        ufull_b[:], dtx_bc, brep_b[:], ALU.mult),
                        reads=(tb("dtxb"), tb("brep_b")), writes=(tb("sq_b"),))
                    def emit_a(n):
                        sl = n % 3
                        tr.emit("vector", lambda e, n=n, sl=sl: e.tensor_scalar(
                            abuf[:, sl * T:(sl + 1) * T], dtt[:], -float(n + 1), 1.0,
                            ALU.mult, ALU.add),
                            reads=(tb("dtt"),), writes=(tb(f"abuf{sl}"),))
                    emit_a(0)
                    for n in range(NS):
                        if n + 1 < NS:
                            emit_a(n + 1)
                        sl = n % 3
                        tr.emit("vector", lambda e, n=n, ds=ds, sl=sl: e.tensor_tensor_scan(
                            hbuf_b[:, n * T:(n + 1) * T], abuf[:, sl * T:(sl + 1) * T],
                            ufull_b[:, n * T:(n + 1) * T],
                            states_sb[:, ds * NS + n:ds * NS + n + 1], ALU.mult, ALU.add),
                            reads=(tb(f"abuf{sl}"), tb("sq_b"), tb("states")),
                            writes=(tb(f"hbuf{n % 2}"),))
                    tr.emit("vector", lambda e, ds=ds: e.tensor_copy(
                        states_sb[:, ds * NS:(ds + 1) * NS], _ap(hbuf_b, T - 1, [(T, NS)])),
                        reads=(tb("hbuf0"), tb("hbuf1")), writes=(tb("states"),))
                    tr.emit("vector", lambda e: e.tensor_tensor(
                        hbuf_b[:], hbuf_b[:], crep_b[:], ALU.mult),
                        reads=(tb("crep_b"),), writes=(tb("hbuf0"), tb("hbuf1")))
                    tr.emit("vector", lambda e: e.tensor_tensor(
                        ufull_b[:, 0:8 * T], hbuf_b[:, 0:8 * T], hbuf_b[:, 8 * T:16 * T], ALU.add),
                        reads=(tb("hbuf0"), tb("hbuf1")), writes=(tb("sq_b"),))
                    tr.emit("vector", lambda e: e.tensor_tensor(
                        ufull_b[:, 8 * T:12 * T], ufull_b[:, 0:4 * T], ufull_b[:, 4 * T:8 * T], ALU.add),
                        reads=(tb("sq_b"),), writes=(tb("sq_b"),))
                    tr.emit("vector", lambda e: e.tensor_tensor(
                        ufull_b[:, 12 * T:14 * T], ufull_b[:, 8 * T:10 * T], ufull_b[:, 10 * T:12 * T], ALU.add),
                        reads=(tb("sq_b"),), writes=(tb("sq_b"),))
                    tr.emit("vector", lambda e: e.tensor_tensor(
                        yraw[:], ufull_b[:, 12 * T:13 * T], ufull_b[:, 13 * T:14 * T], ALU.add),
                        reads=(tb("sq_b"),), writes=(tb("yraw"),))
                    tr.emit("vector", lambda e, ds=ds: e.scalar_tensor_tensor(
                        y2_b[:, ds * T:(ds + 1) * T], xi2_b[:, ds * T:(ds + 1) * T],
                        vap("Dv", ds), yraw[:], ALU.mult, ALU.add),
                        reads=(tb("xi2_b"), tb("yraw"), bv), writes=(tb("y2_b"),))
                def y2src(i):
                    return y2_b[:, i * T:(i + 1) * T]

                def y2sq(i):
                    return sq_b[:, i * T:(i + 1) * T]
                ln_stats(y2src, ND_DI, y2sq, True, tb("y2_b"))
                tr.emit("vector", lambda e: e.tensor_copy(srow_b[:], pb4[:]),
                        reads=(PS[4],), writes=(tb("srow_b"),))
                tr.emit("vector", lambda e: e.tensor_copy(mrow_b[:], pb5[:]),
                        reads=(PS[5],), writes=(tb("mrow_b"),))
                sba = srow_b.ap()
                s_bc = bass.AP(tensor=sba.tensor, offset=sba.offset,
                               ap=[[sba.ap[0][0], 128], [0, ND_DI], [1, T]])
                mba = mrow_b.ap()
                m_bc = bass.AP(tensor=mba.tensor, offset=mba.offset,
                               ap=[[mba.ap[0][0], 128], [0, ND_DI], [1, T]])
                tr.emit("vector", lambda e, s_bc=s_bc: e.tensor_tensor(
                    hbuf2_v[:], y2_b[:], s_bc, ALU.mult),
                    reads=(tb("y2_b"), tb("srow_b")), writes=(tb("hbuf0"), tb("hbuf1")))
                tr.emit("vector", lambda e, m_bc=m_bc: e.tensor_tensor(
                    hbuf2_v[:], hbuf2_v[:], m_bc, ALU.subtract),
                    reads=(tb("mrow_b"),), writes=(tb("hbuf0"), tb("hbuf1")))
                for i in range(ND_DI):
                    tr.emit("scalar", lambda e, i=i: e.activation(
                        q1[:], hbuf2_v[:, i * T:(i + 1) * T], AF.Identity,
                        scale=vap("mlng", i), bias=vap("mlnb", i)),
                        reads=(tb("hbuf0"), tb("hbuf1"), bv), writes=(tb("q1"),))
                    tr.emit("vector", lambda e, i=i: e.tensor_tensor(
                        brep_b[:, i * T:(i + 1) * T], q1[:], sres_b[:, i * T:(i + 1) * T], ALU.mult),
                        reads=(tb("q1"), tb("sres_b")), writes=(tb("brep_b"),))
                for dout in range(ND_D):
                    slot = dout % 2
                    pst, psb_ = PS[slot], pb[slot]
                    wtile, wb = stream_w(c_outproj, l * ND_D + dout, KC_DI, slot)
                    tr.emit_group("tensor", [
                        (lambda e, kc=kc, st=(kc == 0), sp=(kc == KC_DI - 1), wtile=wtile, psb_=psb_:
                         e.matmul(psb_[:], wtile[:, kc * 128:(kc + 1) * 128],
                                  brep_b[:, kc * T:(kc + 1) * T], start=st, stop=sp))
                        for kc in range(KC_DI)], reads=(wb, tb("brep_b")), writes=(pst,))
                    zsl = dout % 2
                    tr.emit("vector", lambda e, dout=dout, psb_=psb_, zsl=zsl: e.tensor_tensor(
                        zo_pp[:, zsl * T:(zsl + 1) * T], psb_[:],
                        z_in[:, dout * T:(dout + 1) * T], ALU.add),
                        reads=(pst, bz), writes=(tb("zo_pp"),))
                    dma_out(_dap(zdst, dout * 128 * S + t0, [(S, 128), (1, T)]),
                            zo_pp[:, zsl * T:(zsl + 1) * T], (tb("zo_pp"),))
            tr.barrier()

        if max_phase >= 6:
            ld_vec("b1", E["b1"][:], ND_D)
            ld_vec("b2", E["b2"][:], ND_D)
            ld_vec("bff", E["bff"][:], NT_FF)
            zsrc = zA if L % 2 == 0 else zB
            for ti in range(ntile):
                t0 = ti * T
                bz = tb("z_in")
                z_tile_load(z_in, bz, zsrc, t0, ND_D)
                bx1 = tb("f2_a")
                z_tile_load(f2_a, bx1, x1s, t0, ND_D)
                bl1 = tb("ln1_b")
                z_tile_load(ln1_b, bl1, ln1s, t0, ND_D)
                for i in range(ND_D):
                    tr.emit("vector", lambda e, i=i: e.tensor_copy(
                        xn_b[:, i * T:(i + 1) * T], z_in[:, i * T:(i + 1) * T]),
                        reads=(bz,), writes=(tb("xn_b"),))
                for dout in range(ND_D):
                    slot = dout % 2
                    pst, psb_ = PS[slot], pb[slot]
                    wtile, wb = stream_w(c_op1, dout, KC_D, slot)
                    tr.emit_group("tensor", [
                        (lambda e, kc=kc, st=(kc == 0), sp=(kc == KC_D - 1), wtile=wtile, psb_=psb_:
                         e.matmul(psb_[:], wtile[:, kc * 128:(kc + 1) * 128],
                                  xn_b[:, kc * T:(kc + 1) * T], start=st, stop=sp))
                        for kc in range(KC_D)], reads=(wb, tb("xn_b")), writes=(pst,))
                    tr.emit("scalar", lambda e, dout=dout, psb_=psb_: e.activation(
                        acts_b[:, dout * T:(dout + 1) * T], psb_[:], AF.Gelu, bias=vap("b1", dout)),
                        reads=(pst, bv), writes=(tb("acts_b"),))
                for dout in range(ND_D):
                    slot = dout % 2
                    pst, psb_ = PS[slot], pb[slot]
                    wtile, wb = stream_w(c_op2, dout, KC_D, slot)
                    tr.emit_group("tensor", [
                        (lambda e, kc=kc, st=(kc == 0), sp=(kc == KC_D - 1), wtile=wtile, psb_=psb_:
                         e.matmul(psb_[:], wtile[:, kc * 128:(kc + 1) * 128],
                                  acts_b[:, kc * T:(kc + 1) * T], start=st, stop=sp))
                        for kc in range(KC_D)], reads=(wb, tb("acts_b")), writes=(pst,))
                    tr.emit("scalar", lambda e, dout=dout, psb_=psb_: e.activation(
                        q1[:], psb_[:], AF.Identity, bias=vap("b2", dout)),
                        reads=(pst, bv), writes=(tb("q1"),))
                    tr.emit("vector", lambda e, dout=dout: e.tensor_tensor(
                        q2[:], q1[:], ln1_b[:, dout * T:(dout + 1) * T], ALU.add),
                        reads=(tb("q1"), bl1), writes=(tb("q2"),))
                    tr.emit("vector", lambda e, dout=dout: e.tensor_tensor(
                        f2_a[:, dout * T:(dout + 1) * T], q2[:],
                        f2_a[:, dout * T:(dout + 1) * T], ALU.add),
                        reads=(tb("q2"), bx1), writes=(bx1,))

                if dbg:
                    for i in range(ND_D):
                        dma_out(_dap(dbg1, i * 128 * S + t0, [(S, 128), (1, T)]),
                                f2_a[:, i * T:(i + 1) * T], (bx1,))
                        tr.emit("vector", lambda e, i=i: e.tensor_copy(
                            q1[:], acts_b[:, i * T:(i + 1) * T]),
                            reads=(tb("acts_b"),), writes=(tb("q1"),))
                        dma_out(_dap(dbg2, i * 128 * S + t0, [(S, 128), (1, T)]),
                                q1[:], (tb("q1"),))

                def srcf6(i):
                    return f2_a[:, i * T:(i + 1) * T]

                def sqf6(i):
                    return sq_b[:, i * T:(i + 1) * T]
                ln_stats(srcf6, ND_D, sqf6, False, bx1)
                ln_apply(srcf6, ND_D,
                         lambda i: (xn_b[:, i * T:(i + 1) * T], tb("xn_b")), bx1)

                def xtile6(kc):
                    return xn_b[:, kc * T:(kc + 1) * T]

                def consume6(dout, pst, psb_):
                    zsl = dout % 2
                    tr.emit("vector", lambda e, dout=dout, psb_=psb_, zsl=zsl: e.scalar_tensor_tensor(
                        zo_pp[:, zsl * T:(zsl + 1) * T], psb_[:], 0.5,
                        f2_a[:, dout * T:(dout + 1) * T], ALU.mult, ALU.add),
                        reads=(pst, bx1), writes=(tb("zo_pp"),))
                    dma_out(_dap(out_t, dout * 128 * S + t0, [(S, 128), (1, T)]),
                            zo_pp[:, zsl * T:(zsl + 1) * T], (tb("zo_pp"),))
                ffn_tree(xtile6, consume6, tb("xn_b"))
        tr.barrier()

        @block.sync
        def _(e):
            for a in tr.q["sync"]:
                a(e)

        @block.tensor
        def _(e):
            for a in tr.q["tensor"]:
                a(e)

        @block.vector
        def _(e):
            for a in tr.q["vector"]:
                a(e)

        @block.scalar
        def _(e):
            for a in tr.q["scalar"]:
                a(e)

        @block.gpsimd
        def _(e):
            for a in tr.q["gpsimd"]:
                a(e)

    return nc


def kernel(**inputs):
    g = _prep(inputs)
    hxs = g.pop("_hx_per_core")
    nc = build_program()
    in_maps = []
    for b in range(8):
        m = dict(g)
        m["hx"] = hxs[b]
        in_maps.append(m)
    res = run_bass_kernel_spmd(nc, in_maps, core_ids=list(range(8)))
    outs = [np.asarray(r["out"]).T for r in res.results]
    return np.ascontiguousarray(np.stack(outs, 0).astype(np.float32))


# version 11
# speedup vs baseline: 1.1216x; 1.0276x over previous
import numpy as np
import concourse.bass as bass
import concourse.mybir as mybir
from concourse.bass_utils import run_bass_kernel_spmd

F32 = mybir.dt.float32
BF16 = mybir.dt.bfloat16
AF = mybir.ActivationFunctionType
ALU = mybir.AluOpType
AX = mybir.AxisListType

D = 1024
DI = 2048
S = 8192
T = 512
NTILE = S // T
NS = 16
L = 4
KC_D = D // 128
KC_DI = DI // 128
ND_D = D // 128
ND_DI = DI // 128
EPS = 1e-5
DEPTH_FF = 7
PAR_FF = 8
NN_FF = 255
LOFF = [0, 1, 2, 3, 4, 5, 7, 11, 19]
NT_FF = 19
J2 = NT_FF * 128
SAME_ENGINE_EDGES = 2



def _vec2sb(v, ntiles):
    out = np.zeros(128 * ntiles, np.float32)
    out[:v.shape[0]] = v
    return np.ascontiguousarray(out.reshape(ntiles, 128).T)


def _ffn_perm():
    perm = np.full(J2, -1, np.int64)
    for d in range(DEPTH_FF + 1):
        sz = 1 << d
        base = 128 * LOFF[d]
        for p in range(PAR_FF):
            for i in range(sz):
                perm[base + p * sz + i] = p * NN_FF + (sz - 1 + i)
    return perm


def _tree_blocks():
    blocks = []
    for d in range(DEPTH_FF):
        psz = 1 << d
        csz = 1 << (d + 1)
        nchild_rows = PAR_FF * csz
        nct = (nchild_rows + 127) // 128
        for ctl in range(nct):
            ct = LOFF[d + 1] + ctl
            matL = np.zeros((128, 128), np.float32)
            matR = np.zeros((128, 128), np.float32)
            pt = None
            for cl in range(min(128, nchild_rows - ctl * 128)):
                crow = ctl * 128 + cl
                p = crow // csz
                ip = crow % csz
                prow = p * psz + (ip >> 1)
                side = ip & 1
                pt_this = LOFF[d] + prow // 128
                if pt is None:
                    pt = pt_this
                assert pt == pt_this
                (matR if side else matL)[prow % 128, cl] = 1.0
            blocks.append((d, pt, ct, 0, matL))
            blocks.append((d, pt, ct, 1, matR))
    return blocks


_TREE_BLOCKS = _tree_blocks()
_PERM = _ffn_perm()


def _prep(inputs):
    g = {}
    hs = np.asarray(inputs["L0_attn_hidden_states"], np.float32)
    g["_hx_per_core"] = [np.ascontiguousarray(hs[b].T) for b in range(8)]

    W_in = np.asarray(inputs["L0_ffn_W_in"], np.float32)
    b_in = np.asarray(inputs["L0_ffn_b_in"], np.float32)
    W_out = np.asarray(inputs["L0_ffn_W_out"], np.float32)
    WiT = np.zeros((D, J2), np.float32)
    WoT = np.zeros((J2, D), np.float32)
    bff = np.zeros(J2, np.float32)
    valid = _PERM >= 0
    WiT[:, valid] = W_in[_PERM[valid], :].T
    WoT[valid, :] = W_out[:, _PERM[valid]].T
    bff[valid] = b_in[_PERM[valid]]
    g["wffinT"] = np.ascontiguousarray(WiT)
    g["wffoutT"] = np.ascontiguousarray(WoT)
    g["bff"] = _vec2sb(bff, NT_FF)
    g["treeblk"] = np.ascontiguousarray(
        np.stack([m for (_, _, _, _, m) in _TREE_BLOCKS], 0))
    rootsel = np.zeros((1, 128), np.float32)
    rootsel[0, :PAR_FF] = 1.0
    g["rootsel"] = rootsel

    g["op1T"] = np.ascontiguousarray(np.asarray(inputs["L0_attn_op1_w"], np.float32).T)
    g["op2T"] = np.ascontiguousarray(np.asarray(inputs["L0_attn_op2_w"], np.float32).T)
    g["b1"] = _vec2sb(np.asarray(inputs["L0_attn_op1_b"], np.float32), ND_D)
    g["b2"] = _vec2sb(np.asarray(inputs["L0_attn_op2_b"], np.float32), ND_D)
    g["inng"] = _vec2sb(np.asarray(inputs["L0_attn_in_norm_g"], np.float32), ND_D)
    g["innb"] = _vec2sb(np.asarray(inputs["L0_attn_in_norm_b"], np.float32), ND_D)

    lng = np.asarray(inputs["L0_attn_ln_g"], np.float32)
    lnb = np.asarray(inputs["L0_attn_ln_b"], np.float32)
    g["lng"] = np.stack([_vec2sb(lng[l], ND_D) for l in range(L)])
    g["lnb"] = np.stack([_vec2sb(lnb[l], ND_D) for l in range(L)])
    g["inprojT"] = np.ascontiguousarray(
        np.asarray(inputs["L0_attn_in_proj_w"], np.float32).transpose(0, 2, 1))
    g["xprojT"] = np.ascontiguousarray(
        np.asarray(inputs["L0_attn_x_proj_w"], np.float32).transpose(0, 2, 1))
    g["outprojT"] = np.ascontiguousarray(
        np.asarray(inputs["L0_attn_out_proj_w"], np.float32).transpose(0, 2, 1))
    cw = np.asarray(inputs["L0_attn_conv_w"], np.float32)
    g["convw"] = np.ascontiguousarray(np.stack([
        np.stack([np.stack([_vec2sb(cw[l, c, :, k], ND_DI) for k in range(4)])
                  for c in range(2)]) for l in range(L)]))
    cb = np.asarray(inputs["L0_attn_conv_b"], np.float32)
    g["convb"] = np.ascontiguousarray(np.stack([
        np.stack([_vec2sb(cb[l, c], ND_DI) for c in range(2)]) for l in range(L)]))
    g["dtw"] = np.stack([_vec2sb(np.asarray(inputs["L0_attn_dt_proj_w"], np.float32)[l, :, 0], ND_DI) for l in range(L)])
    g["dtb"] = np.stack([_vec2sb(np.asarray(inputs["L0_attn_dt_proj_b"], np.float32)[l], ND_DI) for l in range(L)])
    g["Dv"] = np.stack([_vec2sb(np.asarray(inputs["L0_attn_D"], np.float32)[l], ND_DI) for l in range(L)])
    g["mlng"] = np.stack([_vec2sb(np.asarray(inputs["L0_attn_mamba_ln_g"], np.float32)[l], ND_DI) for l in range(L)])
    g["mlnb"] = np.stack([_vec2sb(np.asarray(inputs["L0_attn_mamba_ln_b"], np.float32)[l], ND_DI) for l in range(L)])
    g["onesm"] = np.ones((128, 128), np.float32)
    lv = np.zeros((4, 128, 307), np.float32)
    for l in range(4):
        cols = []
        cols.append(_vec2sb(lng[l], ND_D)); cols.append(_vec2sb(lnb[l], ND_D))
        cols.append(_vec2sb(np.asarray(inputs["L0_attn_in_norm_g"], np.float32), ND_D))
        cols.append(_vec2sb(np.asarray(inputs["L0_attn_in_norm_b"], np.float32), ND_D))
        for c in range(2):
            for k in range(4):
                cols.append(_vec2sb(cw[l, c, :, k], ND_DI))
        for c in range(2):
            cols.append(_vec2sb(cb[l, c], ND_DI))
        cols.append(_vec2sb(np.asarray(inputs["L0_attn_dt_proj_w"], np.float32)[l, :, 0], ND_DI))
        cols.append(_vec2sb(np.asarray(inputs["L0_attn_dt_proj_b"], np.float32)[l], ND_DI))
        cols.append(_vec2sb(np.asarray(inputs["L0_attn_D"], np.float32)[l], ND_DI))
        cols.append(_vec2sb(np.asarray(inputs["L0_attn_mamba_ln_g"], np.float32)[l], ND_DI))
        cols.append(_vec2sb(np.asarray(inputs["L0_attn_mamba_ln_b"], np.float32)[l], ND_DI))
        cols.append(_vec2sb(bff, NT_FF))
        cols.append(_vec2sb(np.asarray(inputs["L0_attn_op1_b"], np.float32), ND_D))
        cols.append(_vec2sb(np.asarray(inputs["L0_attn_op2_b"], np.float32), ND_D))
        lv[l] = np.concatenate(cols, axis=1)
    g["layvecs"] = np.ascontiguousarray(lv)
    sel = np.zeros((48, 16 * 128), np.float32)
    for n in range(16):
        sel[n, n * 128:(n + 1) * 128] = 1.0
        sel[32 + n, n * 128:(n + 1) * 128] = 1.0
    g["bcsel"] = sel
    return g



class Buf:
    __slots__ = ("name", "writer", "readers")

    def __init__(self, name):
        self.name = name
        self.writer = None
        self.readers = []


class Tracker:
    ENGINES = ("sync", "tensor", "vector", "scalar", "gpsimd")

    def __init__(self, sems, lane_pool):
        self.sems = dict(sems)
        self.lane_pool = list(lane_pool)
        self.cnt = {e: 0 for e in self.ENGINES}
        self.q = {e: [] for e in self.ENGINES}
        self.trace = []

    def lane(self, name):
        if name not in self.sems:
            assert self.lane_pool, "out of DMA lane semaphores"
            self.sems[name] = self.lane_pool.pop()
            self.cnt[name] = 0
        return name

    def _waits_for(self, eng, reads, writes):
        need = {}

        def add(dep):
            if dep is None:
                return
            wsn, wcnt = dep
            if wsn == eng:
                if eng == "tensor":
                    return
                if SAME_ENGINE_EDGES is False:
                    return
                if SAME_ENGINE_EDGES is not True and                         self.cnt[eng] - wcnt >= SAME_ENGINE_EDGES:
                    return
            if need.get(wsn, 0) < wcnt:
                need[wsn] = wcnt
        for b in reads:
            add(b.writer)
        for b in writes:
            add(b.writer)
            for r in b.readers:
                add(r)
        return need

    def _push(self, eng, need, fns, inc, sn):
        sem = self.sems

        def action(e, need=need, fns=fns, inc=inc, sn=sn):
            for weng in sorted(need):
                e.wait_ge(sem[weng], need[weng])
            last = None
            for f in fns:
                last = f(e)
            last.then_inc(sem[sn], inc)
        self.q[eng].append(action)

    def emit(self, eng, fn, reads=(), writes=(), inc=1):
        return self.emit_group(eng, [fn], reads=reads, writes=writes, inc=inc)

    def emit_group(self, eng, fns, reads=(), writes=(), inc=1, sem_name=None):
        sn = sem_name or eng
        need = self._waits_for(eng, reads, writes)
        self.cnt[sn] += inc
        self.trace.append((eng, sn, self.cnt[sn], tuple(b.name for b in reads),
                           tuple(b.name for b in writes), dict(need), len(fns)))
        self._push(eng, need, fns, inc, sn)
        dep = (sn, self.cnt[sn])
        for b in writes:
            b.writer = dep
            b.readers = []
        for b in reads:
            if b not in writes:
                b.readers.append(dep)
        return dep

    def barrier(self):
        snap = dict(self.cnt)
        sem = self.sems
        for eng in self.ENGINES:
            def action(e, snap=snap, eng=eng):
                for osn, v in snap.items():
                    if osn != eng and v > 0:
                        e.wait_ge(sem[osn], v)
            self.q[eng].append(action)


def _ap(handle, offset, dims, npart=128):
    base = handle.ap()
    return bass.AP(tensor=base.tensor, offset=base.offset + offset,
                   ap=[[base.ap[0][0], npart]] + [[s, c] for (s, c) in dims])


def _dap(dram_ap, offset, dims):
    return bass.AP(tensor=dram_ap.tensor, offset=dram_ap.offset + offset,
                   ap=[[s, c] for (s, c) in dims])



def build_program(max_phase=99, dbg=False, ntile=NTILE):
    nc = bass.Bass("TRN2", target_bir_lowering=False)
    E = {}

    def ein(name, shape):
        E[name] = nc.dram_tensor(name, list(shape), F32, kind="ExternalInput").ap()

    ein("hx", (D, S))
    ein("wffinT", (D, J2)); ein("wffoutT", (J2, D)); ein("bff", (128, NT_FF))
    ein("treeblk", (len(_TREE_BLOCKS), 128, 128)); ein("rootsel", (1, 128))
    ein("op1T", (D, D)); ein("op2T", (D, D)); ein("b1", (128, ND_D)); ein("b2", (128, ND_D))
    ein("inng", (128, ND_D)); ein("innb", (128, ND_D))
    ein("lng", (L, 128, ND_D)); ein("lnb", (L, 128, ND_D))
    ein("inprojT", (L, D, 2 * DI)); ein("xprojT", (L, DI, 33)); ein("outprojT", (L, DI, D))
    ein("convw", (L, 2, 4, 128, ND_DI)); ein("convb", (L, 2, 128, ND_DI))
    ein("dtw", (L, 128, ND_DI)); ein("dtb", (L, 128, ND_DI)); ein("Dv", (L, 128, ND_DI))
    ein("mlng", (L, 128, ND_DI)); ein("mlnb", (L, 128, ND_DI))
    ein("onesm", (128, 128))
    ein("layvecs", (L, 128, 307))
    ein("bcsel", (48, 16 * 128))

    out_t = nc.dram_tensor("out", [D, S], F32, kind="ExternalOutput").ap()
    KD = "ExternalOutput" if dbg else "Internal"
    if dbg:
        dbg1 = nc.dram_tensor("dbg1", [D, S], F32, kind="ExternalOutput").ap()
        dbg2 = nc.dram_tensor("dbg2", [D, S], F32, kind="ExternalOutput").ap()
    zA = nc.dram_tensor("zA", [D, S], F32, kind=KD).ap()
    zB = nc.dram_tensor("zB", [D, S], F32, kind=KD).ap()
    x1s = nc.dram_tensor("x1s", [D, S], F32, kind=KD).ap()
    ln1s = nc.dram_tensor("ln1s", [D, S], BF16, kind=KD).ap()
    c_inproj = nc.dram_tensor("c_inproj", [L, 2 * ND_DI, 128, KC_D * 128], BF16, kind="Internal").ap()
    c_outproj = nc.dram_tensor("c_outproj", [L, ND_D, 128, KC_DI * 128], BF16, kind="Internal").ap()
    c_wffin = nc.dram_tensor("c_wffin", [NT_FF, 128, KC_D * 128], BF16, kind="Internal").ap()
    c_wffout = nc.dram_tensor("c_wffout", [ND_D, 128, NT_FF * 128], BF16, kind="Internal").ap()
    c_op1 = nc.dram_tensor("c_op1", [ND_D, 128, KC_D * 128], BF16, kind="Internal").ap()
    c_op2 = nc.dram_tensor("c_op2", [ND_D, 128, KC_D * 128], BF16, kind="Internal").ap()

    NBLK = len(_TREE_BLOCKS)
    import contextlib
    es = contextlib.ExitStack()
    with es:
        ones_f = es.enter_context(nc.sbuf_tensor("ones_f", [128, 128], F32))
        ones_b = es.enter_context(nc.sbuf_tensor("ones_b", [128, 128], BF16))
        treesb = es.enter_context(nc.sbuf_tensor("treesb", [128, NBLK * 128], BF16))
        rootsb = es.enter_context(nc.sbuf_tensor("rootsb", [1, 128], BF16))
        onerow_b = es.enter_context(nc.sbuf_tensor("onerow_b", [1, T], BF16))
        wstg = es.enter_context(nc.sbuf_tensor("wstg", [128, 384], F32))
        wstgb = es.enter_context(nc.sbuf_tensor("wstgb", [128, 384], BF16))
        xpw_b = es.enter_context(nc.sbuf_tensor("xpw_b", [128, KC_DI * 33], BF16))
        wbufA = es.enter_context(nc.sbuf_tensor("wbufA", [128, NT_FF * 128], BF16))
        wbufB = es.enter_context(nc.sbuf_tensor("wbufB", [128, NT_FF * 128], BF16))
        vecs = es.enter_context(nc.sbuf_tensor("vecs", [128, 320], F32))
        z_in = es.enter_context(nc.sbuf_tensor("z_in", [128, ND_D * T], F32))
        zo_pp = es.enter_context(nc.sbuf_tensor("zo_pp", [128, 2 * T], F32))
        xn_b = es.enter_context(nc.sbuf_tensor("xn_b", [128, ND_D * T], BF16))
        cin_b = es.enter_context(nc.sbuf_tensor("cin_b", [128, 2 * (T + 4)], BF16))
        xtails = es.enter_context(nc.sbuf_tensor("xtails", [128, ND_DI * 4], BF16))
        cmid_b = es.enter_context(nc.sbuf_tensor("cmid_b", [128, 2 * (T + 4)], BF16))
        ctails = es.enter_context(nc.sbuf_tensor("ctails", [128, ND_DI * 4], BF16))
        arena = es.enter_context(nc.sbuf_tensor("arena", [128, 49152], BF16))
        projsb = es.enter_context(nc.sbuf_tensor("projsb", [33, T], F32))
        bc_sb = es.enter_context(nc.sbuf_tensor("bc_sb", [48, T], BF16))
        bcsel_b = es.enter_context(nc.sbuf_tensor("bcsel_b", [48, 16 * 128], BF16))
        dtt = es.enter_context(nc.sbuf_tensor("dtt", [128, 2 * T], F32))
        dtxb = es.enter_context(nc.sbuf_tensor("dtxb", [128, 2 * T], BF16))
        abuf = es.enter_context(nc.sbuf_tensor("abuf", [128, 3 * T], BF16))

        q1 = es.enter_context(nc.sbuf_tensor("q1", [128, T], F32))
        q2 = es.enter_context(nc.sbuf_tensor("q2", [128, T], F32))
        states_sb = es.enter_context(nc.sbuf_tensor("states", [128, ND_DI * NS], F32))
        convacc = es.enter_context(nc.sbuf_tensor("convacc", [128, T], F32))
        sq_b = es.enter_context(nc.sbuf_tensor("sq_b", [128, ND_DI * T], BF16))
        statA = es.enter_context(nc.sbuf_tensor("statA", [128, 3 * T], F32))
        srow_b = es.enter_context(nc.sbuf_tensor("srow_b", [128, T], BF16))
        mrow_b = es.enter_context(nc.sbuf_tensor("mrow_b", [128, T], BF16))
        cst = es.enter_context(nc.sbuf_tensor("cst", [128, 12], F32))

        pb0 = es.enter_context(nc.psum_tensor("pb0", [128, T], F32))
        pb1 = es.enter_context(nc.psum_tensor("pb1", [128, T], F32))
        pb2 = es.enter_context(nc.psum_tensor("pb2", [128, T], F32))
        pb3 = es.enter_context(nc.psum_tensor("pb3", [128, T], F32))
        pb4 = es.enter_context(nc.psum_tensor("pb4", [128, T], F32))
        pb5 = es.enter_context(nc.psum_tensor("pb5", [128, T], F32))
        pb6 = es.enter_context(nc.psum_tensor("pb6", [128, T], F32))
        pb7 = es.enter_context(nc.psum_tensor("pb7", [128, T], F32))
        s_sync = es.enter_context(nc.semaphore("s_sync"))
        lane_sems = [es.enter_context(nc.semaphore(f"lane{i}")) for i in range(24)]
        s_tensor = es.enter_context(nc.semaphore("s_tensor"))
        s_vector = es.enter_context(nc.semaphore("s_vector"))
        s_scalar = es.enter_context(nc.semaphore("s_scalar"))
        s_gpsimd = es.enter_context(nc.semaphore("s_gpsimd"))
        block = es.enter_context(nc.Block())
        pb = [pb0, pb1, pb2, pb3, pb4, pb5, pb6, pb7]

        class TView:

            def __init__(self, h, off, nels):
                self.h, self.off, self.nels = h, off, nels

            def ap(self):
                b = self.h.ap()
                return bass.AP(tensor=b.tensor, offset=b.offset + self.off,
                               ap=[[b.ap[0][0], 128], [1, self.nels]])

            def __getitem__(self, key):
                base = self.ap()
                if isinstance(key, slice):
                    return base
                ps, fs = key
                assert (ps.start or 0) == 0
                f0 = fs.start or 0
                f1 = fs.stop if fs.stop is not None else self.nels
                return bass.AP(tensor=base.tensor, offset=base.offset + f0,
                               ap=[[base.ap[0][0], 128], [1, f1 - f0]])

        arena_f = arena.bitcast(F32)
        ufull_b = TView(sq_b, 0, NS * T)
        hbuf2_v = TView(arena, 24576, ND_DI * T)
        xi2_b = TView(arena, 0, ND_DI * T)
        sres_b = TView(arena, 8192, ND_DI * T)
        y2_b = TView(arena, 16384, ND_DI * T)
        hbuf_b = TView(arena, 24576, NS * T)
        brep_b = TView(arena, 32768, NS * T)
        crep_b = TView(arena, 40960, NS * T)
        acts_b = TView(arena, 0, NT_FF * T)
        dec_b = TView(arena, 9728, NT_FF * T)
        dmap_b = TView(arena, 19456, NT_FF * T)
        oLR_b = TView(arena, 29184, 8 * T)
        ln1_b = TView(arena, 33280, ND_D * T)
        f2_a = TView(arena_f, 18688, ND_D * T)
        tr = Tracker({"sync": s_sync, "tensor": s_tensor, "vector": s_vector,
                      "scalar": s_scalar, "gpsimd": s_gpsimd}, lane_sems)
        build_program.last_trace = tr.trace
        B = {}

        def tb(name):
            if name not in B:
                B[name] = Buf(name)
            return B[name]

        PS = [tb(f"ps{i}") for i in range(8)]
        bv = tb("vecs")

        VEC = {}
        _vc = 0
        for nm, n in [("lng", ND_D), ("lnb", ND_D), ("inng", ND_D), ("innb", ND_D),
                      ("cw00", ND_DI), ("cw01", ND_DI), ("cw02", ND_DI), ("cw03", ND_DI),
                      ("cw10", ND_DI), ("cw11", ND_DI), ("cw12", ND_DI), ("cw13", ND_DI),
                      ("cb0", ND_DI), ("cb1", ND_DI),
                      ("dtw", ND_DI), ("dtb", ND_DI), ("Dv", ND_DI),
                      ("mlng", ND_DI), ("mlnb", ND_DI),
                      ("bff", NT_FF), ("b1", ND_D), ("b2", ND_D)]:
            VEC[nm] = _vc
            _vc += n
        assert _vc <= 320

        def vap(name, i):
            c = VEC[name] + i
            return vecs[:, c:c + 1]

        def ld_vec(dst, src_ap, n):
            c = VEC[dst]
            tr.emit_group("sync", [lambda e, c=c, s=src_ap, n=n: e.dma_start(
                out=vecs[:, c:c + n], in_=s)], writes=(bv,), inc=16,
                sem_name=tr.lane("L_vecs"))

        def dma_in(dst_ap, src_ap, wb, rb=()):
            ln_ = tr.lane("L_" + wb[0].name)
            tr.emit_group("sync", [lambda e, d=dst_ap, s=src_ap: e.dma_start(out=d, in_=s)],
                          reads=rb, writes=wb, inc=16, sem_name=ln_)

        def dma_out(dst_ap, src_ap, rb, wb=()):
            ln_ = tr.lane("S_" + rb[0].name)
            tr.emit_group("gpsimd", [lambda e, d=dst_ap, s=src_ap: e.dma_start(out=d, in_=s)],
                          reads=rb, writes=wb, inc=16, sem_name=ln_)

        dma_in(ones_f[:], E["onesm"][:], (tb("ones_f"),))
        tr.emit("vector", lambda e: e.tensor_copy(ones_b[:], ones_f[:]),
                reads=(tb("ones_f"),), writes=(tb("ones_b"),))
        tr.emit("gpsimd", lambda e: e.memset(onerow_b[:], 1.0), writes=(tb("onerow_b"),))
        tr.emit("gpsimd", lambda e: e.memset(cst[:, 0:1], EPS), writes=(tb("cst"),))
        tr.emit("gpsimd", lambda e: e.memset(cst[:, 1:2], -0.5), writes=(tb("cst"),))
        tr.emit("gpsimd", lambda e: e.memset(cst[:, 2:3], 1.0), writes=(tb("cst"),))
        for _n in range(8, 16):
            tr.emit("gpsimd", lambda e, _n=_n: e.memset(cst[:, 4 + _n - 8:5 + _n - 8], -float(_n + 1)),
                    writes=(tb("cst"),))

        def cache_build(src2d, rows, cols, dst_ap_fn, wname):
            CC = 384
            for ch in range(rows // 128):
                for c0 in range(0, cols, CC):
                    cw_ = min(CC, cols - c0)
                    dma_in(wstg[:, 0:cw_], src2d[ch * 128:(ch + 1) * 128, c0:c0 + cw_],
                           (tb("wstg"),))
                    tr.emit("vector", lambda e, c=cw_: e.tensor_copy(wstgb[:, 0:c], wstg[:, 0:c]),
                            reads=(tb("wstg"),), writes=(tb("wstgb"),))
                    dma_out(dst_ap_fn(ch, c0, cw_), wstgb[:, 0:cw_], (tb("wstgb"),))

        for l in range(L):
            def dst_ip(kc, c0, cw_, l=l):
                return _dap(c_inproj,
                            l * (2 * ND_DI * 128 * KC_D * 128)
                            + (c0 // 128) * (128 * KC_D * 128) + kc * 128,
                            [(KC_D * 128, 128), (128 * KC_D * 128, cw_ // 128), (1, 128)])
            cache_build(E["inprojT"][l], D, 2 * DI, dst_ip, "ip")

            def dst_op(kc, c0, cw_, l=l):
                return _dap(c_outproj,
                            l * (ND_D * 128 * KC_DI * 128)
                            + (c0 // 128) * (128 * KC_DI * 128) + kc * 128,
                            [(KC_DI * 128, 128), (128 * KC_DI * 128, cw_ // 128), (1, 128)])
            cache_build(E["outprojT"][l], DI, D, dst_op, "op")

        cache_build(E["wffinT"], D, J2,
                    lambda kc, c0, cw_: _dap(c_wffin, (c0 // 128) * (128 * KC_D * 128) + kc * 128,
                                             [(KC_D * 128, 128), (128 * KC_D * 128, cw_ // 128), (1, 128)]), "fi")
        cache_build(E["wffoutT"], J2, D,
                    lambda kc, c0, cw_: _dap(c_wffout, (c0 // 128) * (128 * NT_FF * 128) + kc * 128,
                                             [(NT_FF * 128, 128), (128 * NT_FF * 128, cw_ // 128), (1, 128)]), "fo")
        cache_build(E["op1T"], D, D,
                    lambda kc, c0, cw_: _dap(c_op1, (c0 // 128) * (128 * KC_D * 128) + kc * 128,
                                             [(KC_D * 128, 128), (128 * KC_D * 128, cw_ // 128), (1, 128)]), "o1")
        cache_build(E["op2T"], D, D,
                    lambda kc, c0, cw_: _dap(c_op2, (c0 // 128) * (128 * KC_D * 128) + kc * 128,
                                             [(KC_D * 128, 128), (128 * KC_D * 128, cw_ // 128), (1, 128)]), "o2")

        for b0 in range(0, NBLK, 1):
            nb = 1
            src = _dap(E["treeblk"], b0 * 128 * 128,
                       [(128, 128), (128 * 128, nb), (1, 128)])
            dma_in(wstg[:, 0:nb * 128], src, (tb("wstg"),))
            tr.emit("vector", lambda e, b0=b0, nb=nb: e.tensor_copy(
                treesb[:, b0 * 128:(b0 + nb) * 128], wstg[:, 0:nb * 128]),
                reads=(tb("wstg"),), writes=(tb("treesb"),))
        for c0 in range(0, 2048, 128):
            dma_in(wstg[0:48, 0:128], E["bcsel"][:, c0:c0 + 128], (tb("wstg"),))
            tr.emit("vector", lambda e, c0=c0: e.tensor_copy(
                bcsel_b[0:48, c0:c0 + 128], wstg[0:48, 0:128]),
                reads=(tb("wstg"),), writes=(tb("bcsel_b"),))
        dma_in(wstg[0:1, 0:128], E["rootsel"][:], (tb("wstg"),))
        tr.emit("vector", lambda e: e.tensor_copy(rootsb[:], wstg[0:1, 0:128]),
                reads=(tb("wstg"),), writes=(tb("rootsb"),))
        tr.barrier()

        bstat = tb("stat")

        def ln_stats(src_tiles, nt, sq_view, src_is_bf, src_buf):
            Dtot = float(nt * 128)
            for i in range(nt):
                tr.emit("scalar", lambda e, i=i: e.activation(sq_view(i), src_tiles(i), AF.Square),
                        reads=(src_buf,), writes=(tb("sq_b"),))
            oc = ones_b[:, 0:1] if src_is_bf else ones_f[:, 0:1]
            ob = tb("ones_b") if src_is_bf else tb("ones_f")
            tr.emit_group("tensor", [
                (lambda e, i=i, st=(i == 0), sp=(i == nt - 1): e.matmul(
                    pb2[0:1, :], oc, src_tiles(i), start=st, stop=sp))
                for i in range(nt)], reads=(src_buf, ob), writes=(PS[2],))
            tr.emit_group("tensor", [
                (lambda e, i=i, st=(i == 0), sp=(i == nt - 1): e.matmul(
                    pb3[0:1, :], ones_b[:, 0:1], sq_view(i), start=st, stop=sp))
                for i in range(nt)], reads=(tb("sq_b"), tb("ones_b")), writes=(PS[3],))
            tr.emit("vector", lambda e: e.tensor_scalar(statA[0:1, 0:T], pb2[0:1, :], 1.0 / Dtot, None, ALU.mult),
                    reads=(PS[2],), writes=(bstat,))
            tr.emit("vector", lambda e: e.tensor_scalar(statA[32:33, 0:T], pb3[0:1, :], 1.0 / Dtot, None, ALU.mult),
                    reads=(PS[3],), writes=(bstat,))
            tr.emit("vector", lambda e: e.tensor_tensor(statA[32:33, T:2 * T], statA[0:1, 0:T], statA[0:1, 0:T], ALU.mult),
                    reads=(bstat,), writes=(bstat,))
            tr.emit("vector", lambda e: e.tensor_tensor(statA[64:65, 0:T], statA[32:33, 0:T], statA[32:33, T:2 * T], ALU.subtract),
                    reads=(bstat,), writes=(bstat,))
            tr.emit("scalar", lambda e: e.activation(statA[64:65, T:2 * T], statA[64:65, 0:T], AF.Ln, bias=cst[64:65, 0:1]),
                    reads=(bstat, tb("cst")), writes=(bstat,))
            tr.emit("scalar", lambda e: e.activation(statA[0:1, T:2 * T], statA[64:65, T:2 * T], AF.Exp, scale=cst[64:65, 1:2]),
                    reads=(bstat, tb("cst")), writes=(bstat,))
            tr.emit("vector", lambda e: e.tensor_tensor(statA[0:1, 2 * T:3 * T], statA[0:1, 0:T], statA[0:1, T:2 * T], ALU.mult),
                    reads=(bstat,), writes=(bstat,))
            tr.emit("tensor", lambda e: e.matmul(pb4[:], ones_f[0:1, :], statA[0:1, T:2 * T], start=True, stop=True),
                    reads=(bstat, tb("ones_f")), writes=(PS[4],))
            tr.emit("tensor", lambda e: e.matmul(pb5[:], ones_f[0:1, :], statA[0:1, 2 * T:3 * T], start=True, stop=True),
                    reads=(bstat, tb("ones_f")), writes=(PS[5],))

        def ln_apply(src_tiles, nt, out_fn, src_buf, gain=None, bias=None):
            for i in range(nt):
                tr.emit("vector", lambda e, i=i: e.tensor_tensor(q1[:], src_tiles(i), pb4[:], ALU.mult),
                        reads=(src_buf, PS[4]), writes=(tb("q1"),))
                if gain is None:
                    oap, ob = out_fn(i)
                    tr.emit("vector", lambda e, oap=oap: e.tensor_tensor(oap, q1[:], pb5[:], ALU.subtract),
                            reads=(tb("q1"), PS[5]), writes=(ob,))
                else:
                    tr.emit("vector", lambda e: e.tensor_tensor(q2[:], q1[:], pb5[:], ALU.subtract),
                            reads=(tb("q1"), PS[5]), writes=(tb("q2"),))
                    oap, ob = out_fn(i)
                    tr.emit("scalar", lambda e, oap=oap, g=gain(i), bb=bias(i): e.activation(
                        oap, q2[:], AF.Identity, scale=g, bias=bb),
                        reads=(tb("q2"), bv), writes=(ob,))

        def stream_w(cache_ap, idx, ncols, slot):
            wtile = wbufA if slot == 0 else wbufB
            wb = tb("wbufA") if slot == 0 else tb("wbufB")
            src = _dap(cache_ap, idx * 128 * ncols * 128, [(ncols * 128, 128), (1, ncols * 128)])
            dma_in(_ap(wtile, 0, [(1, ncols * 128)]), src, (wb,))
            return wtile, wb

        def ffn_tree(xtile, consume, xnb_buf):
            blocks_by_ct = {}
            for bi, (d, pt, ct, side, _) in enumerate(_TREE_BLOCKS):
                blocks_by_ct.setdefault((d, ct), []).append((bi, pt, side))
            for j in range(NT_FF):
                slot = j % 2
                pst, psb_ = (PS[slot], pb[slot])
                wtile, wb = stream_w(c_wffin, j, KC_D, slot)
                tr.emit_group("tensor", [
                    (lambda e, kc=kc, st=(kc == 0), sp=(kc == KC_D - 1), wtile=wtile, psb_=psb_:
                     e.matmul(psb_[:], wtile[:, kc * 128:(kc + 1) * 128], xtile(kc), start=st, stop=sp))
                    for kc in range(KC_D)], reads=(wb, xnb_buf), writes=(pst,))
                tr.emit("scalar", lambda e, j=j, psb_=psb_: e.activation(
                    acts_b[:, j * T:(j + 1) * T], psb_[:], AF.Silu,
                    bias=vecs[:, VEC["bff"] + j:VEC["bff"] + j + 1]),
                    reads=(bv,), writes=(tb("acts_b"), pst))
                tr.emit("vector", lambda e, j=j, psb_=psb_: e.tensor_scalar(
                    dec_b[:, j * T:(j + 1) * T], psb_[:],
                    vecs[:, VEC["bff"] + j:VEC["bff"] + j + 1], 0.0, ALU.add, ALU.is_gt),
                    reads=(bv,), writes=(tb("dec_b"), pst))
            tr.emit("tensor", lambda e: e.matmul(pb2[:], rootsb[0:1, :], onerow_b[0:1, :], start=True, stop=True),
                    reads=(tb("rootsb"), tb("onerow_b")), writes=(PS[2],))
            tr.emit("vector", lambda e: e.tensor_copy(dmap_b[:, 0:T], pb2[:]),
                    reads=(PS[2],), writes=(tb("dmap_b"),))
            for d in range(DEPTH_FF):
                pts = sorted({pt for (dd, pt, ct, s2, m) in _TREE_BLOCKS if dd == d})
                pt_slot = {pt: k for k, pt in enumerate(pts)}
                for pt, k in pt_slot.items():
                    tr.emit("vector", lambda e, pt=pt, k=k: e.tensor_tensor(
                        oLR_b[:, (2 * k + 1) * T:(2 * k + 2) * T],
                        dmap_b[:, pt * T:(pt + 1) * T], dec_b[:, pt * T:(pt + 1) * T], ALU.mult),
                        reads=(tb("dmap_b"), tb("dec_b")), writes=(tb("oLR_b"),))
                    tr.emit("vector", lambda e, pt=pt, k=k: e.tensor_tensor(
                        oLR_b[:, (2 * k) * T:(2 * k + 1) * T],
                        dmap_b[:, pt * T:(pt + 1) * T], oLR_b[:, (2 * k + 1) * T:(2 * k + 2) * T], ALU.subtract),
                        reads=(tb("dmap_b"),), writes=(tb("oLR_b"),))
                cts = sorted({ct for (dd, pt, ct, s2, m) in _TREE_BLOCKS if dd == d})
                for ci, ct in enumerate(cts):
                    blks = blocks_by_ct[(d, ct)]
                    slot = 2 + ci % 2
                    pst, psb_ = PS[slot], pb[slot]
                    tr.emit_group("tensor", [
                        (lambda e, bi=bi, sl=2 * pt_slot[pt] + side, st=(ii == 0), sp=(ii == len(blks) - 1), psb_=psb_:
                         e.matmul(psb_[:], treesb[:, bi * 128:(bi + 1) * 128],
                                  oLR_b[:, sl * T:(sl + 1) * T], start=st, stop=sp))
                        for ii, (bi, pt, side) in enumerate(blks)],
                        reads=(tb("treesb"), tb("oLR_b")), writes=(pst,))
                    tr.emit("vector", lambda e, ct=ct, psb_=psb_: e.tensor_copy(
                        dmap_b[:, ct * T:(ct + 1) * T], psb_[:]),
                        reads=(pst,), writes=(tb("dmap_b"),))
            for j in range(NT_FF):
                tr.emit("vector", lambda e, j=j: e.tensor_tensor(
                    acts_b[:, j * T:(j + 1) * T], acts_b[:, j * T:(j + 1) * T],
                    dmap_b[:, j * T:(j + 1) * T], ALU.mult),
                    reads=(tb("acts_b"), tb("dmap_b")), writes=(tb("acts_b"),))
            for dout in range(ND_D):
                slot = dout % 2
                pst, psb_ = PS[slot], pb[slot]
                wtile, wb = stream_w(c_wffout, dout, NT_FF, slot)
                tr.emit_group("tensor", [
                    (lambda e, kc=kc, st=(kc == 0), sp=(kc == NT_FF - 1), wtile=wtile, psb_=psb_:
                     e.matmul(psb_[:], wtile[:, kc * 128:(kc + 1) * 128],
                              acts_b[:, kc * T:(kc + 1) * T], start=st, stop=sp))
                    for kc in range(NT_FF)], reads=(wb, tb("acts_b")), writes=(pst,))
                consume(dout, pst, psb_)

        def z_tile_load(dst_handle, dst_buf, src_dram, t0, nt):
            dma_in(_ap(dst_handle, 0, [(T, nt), (1, T)]),
                   _dap(src_dram, t0, [(S, 128), (128 * S, nt), (1, T)]), (dst_buf,))

        if max_phase >= 1:
            ld_vec("bff", E["bff"][:], NT_FF)
            ld_vec("inng", E["inng"][:], ND_D)
            ld_vec("innb", E["innb"][:], ND_D)
            for ti in range(ntile):
                t0 = ti * T
                bz = tb("z_in")
                z_tile_load(z_in, bz, E["hx"], t0, ND_D)

                def srcf(i):
                    return z_in[:, i * T:(i + 1) * T]

                def sqf(i):
                    return sq_b[:, i * T:(i + 1) * T]
                ln_stats(srcf, ND_D, sqf, False, bz)
                ln_apply(srcf, ND_D,
                         lambda i: (xn_b[:, i * T:(i + 1) * T], tb("xn_b")), bz)

                def xtile(kc):
                    return xn_b[:, kc * T:(kc + 1) * T]

                def consume(dout, pst, psb_):
                    tr.emit("vector", lambda e, dout=dout, psb_=psb_: e.scalar_tensor_tensor(
                        f2_a[:, dout * T:(dout + 1) * T], psb_[:], 0.5,
                        z_in[:, dout * T:(dout + 1) * T], ALU.mult, ALU.add),
                        reads=(pst, bz), writes=(tb("f2_a"),))
                    dma_out(_dap(x1s, dout * 128 * S + t0, [(S, 128), (1, T)]),
                            f2_a[:, dout * T:(dout + 1) * T], (tb("f2_a"),))
                ffn_tree(xtile, consume, tb("xn_b"))

                def srcf2(i):
                    return f2_a[:, i * T:(i + 1) * T]
                ln_stats(srcf2, ND_D, sqf, False, tb("f2_a"))
                ln_apply(srcf2, ND_D,
                         lambda i: (ln1_b[:, i * T:(i + 1) * T], tb("ln1_b")), tb("f2_a"))
                for i in range(ND_D):
                    dma_out(_dap(ln1s, i * 128 * S + t0, [(S, 128), (1, T)]),
                            ln1_b[:, i * T:(i + 1) * T], (tb("ln1_b"),))

                def srcf3(i):
                    return ln1_b[:, i * T:(i + 1) * T]
                ln_stats(srcf3, ND_D, sqf, True, tb("ln1_b"))
                ln_apply(srcf3, ND_D,
                         lambda i: (z_in[:, i * T:(i + 1) * T], tb("z_in")), tb("ln1_b"),
                         gain=lambda i: vap("inng", i), bias=lambda i: vap("innb", i))
                for i in range(ND_D):
                    dma_out(_dap(zA, i * 128 * S + t0, [(S, 128), (1, T)]),
                            z_in[:, i * T:(i + 1) * T], (tb("z_in"),))
            tr.barrier()

        nmamba = max(0, min(L, max_phase - 1))
        for l in range(nmamba):
            zsrc = zA if l % 2 == 0 else zB
            zdst = zB if l % 2 == 0 else zA
            tr.emit_group("sync", [lambda e, l=l: e.dma_start(
                out=vecs[:, 0:307], in_=E["layvecs"][l])], writes=(bv,), inc=16,
                sem_name=tr.lane("L_vecs"))
            xsrc = _dap(E["xprojT"], l * DI * 33,
                        [(33, 128), (128 * 33, KC_DI), (1, 33)])
            tr.emit_group("sync", [lambda e, xsrc=xsrc: e.dma_start(
                out=_ap(statA, 0, [(33, KC_DI), (1, 33)]), in_=xsrc)],
                writes=(bstat,), inc=16, sem_name=tr.lane("L_statA"))
            tr.emit("vector", lambda e: e.tensor_copy(
                xpw_b[:, 0:KC_DI * 33], statA[:, 0:KC_DI * 33]),
                reads=(bstat,), writes=(tb("xpw_b"),))
            tr.emit("gpsimd", lambda e: e.memset(states_sb[:], 0.0), writes=(tb("states"),))
            tr.emit("gpsimd", lambda e: e.memset(xtails[:], 0.0), writes=(tb("xtails"),))
            tr.emit("gpsimd", lambda e: e.memset(ctails[:], 0.0), writes=(tb("ctails"),))

            for ti in range(ntile):
                t0 = ti * T
                bz = tb("z_in")
                z_tile_load(z_in, bz, zsrc, t0, ND_D)

                def msrc(i):
                    return z_in[:, i * T:(i + 1) * T]

                def msq(i):
                    return sq_b[:, i * T:(i + 1) * T]
                ln_stats(msrc, ND_D, msq, False, bz)
                ln_apply(msrc, ND_D,
                         lambda i: (xn_b[:, i * T:(i + 1) * T], tb("xn_b")), bz,
                         gain=lambda i: vap("lng", i), bias=lambda i: vap("lnb", i))
                for dout in range(2 * ND_DI):
                    slot = dout % 2
                    pst, psb_ = PS[slot], pb[slot]
                    wtile, wb = stream_w(c_inproj, l * 2 * ND_DI + dout, KC_D, slot)
                    tr.emit_group("tensor", [
                        (lambda e, kc=kc, st=(kc == 0), sp=(kc == KC_D - 1), wtile=wtile, psb_=psb_:
                         e.matmul(psb_[:], wtile[:, kc * 128:(kc + 1) * 128],
                                  xn_b[:, kc * T:(kc + 1) * T], start=st, stop=sp))
                        for kc in range(KC_D)], reads=(wb, tb("xn_b")), writes=(pst,))
                    if dout >= ND_DI:
                        dd = dout - ND_DI
                        tr.emit("scalar", lambda e, dd=dd, psb_=psb_: e.activation(
                            sres_b[:, dd * T:(dd + 1) * T], psb_[:], AF.Sigmoid),
                            reads=(pst,), writes=(tb("sres_b"),))
                        continue
                    dd = dout
                    ib = slot * (T + 4)
                    tr.emit("gpsimd", lambda e, ib=ib, dd=dd: e.tensor_copy(
                        cin_b[:, ib + 1:ib + 4], xtails[:, dd * 4:dd * 4 + 3]),
                        reads=(tb("xtails"),), writes=(tb("cin_b"),))
                    tr.emit("scalar", lambda e, ib=ib, psb_=psb_: e.activation(
                        cin_b[:, ib + 4:ib + 4 + T], psb_[:], AF.Silu),
                        reads=(pst,), writes=(tb("cin_b"),))
                    tr.emit("gpsimd", lambda e, ib=ib, dd=dd: e.tensor_copy(
                        xtails[:, dd * 4:dd * 4 + 3], cin_b[:, ib + 1 + T:ib + 4 + T]),
                        reads=(tb("cin_b"),), writes=(tb("xtails"),))
                    tr.emit("vector", lambda e, ib=ib, dd=dd: e.tensor_scalar(
                        convacc[:], cin_b[:, ib + 1:ib + 1 + T], vap("cw00", dd), vap("cb0", dd), ALU.mult, ALU.add),
                        reads=(tb("cin_b"), bv), writes=(tb("convacc"),))
                    for k in range(1, 4):
                        tr.emit("vector", lambda e, ib=ib, dd=dd, k=k: e.scalar_tensor_tensor(
                            convacc[:], cin_b[:, ib + 1 + k:ib + 1 + k + T], vap(f"cw0{k}", dd),
                            convacc[:], ALU.mult, ALU.add),
                            reads=(tb("cin_b"), tb("convacc"), bv), writes=(tb("convacc"),))
                    tr.emit("gpsimd", lambda e, ib=ib, dd=dd: e.tensor_copy(
                        cmid_b[:, ib + 1:ib + 4], ctails[:, dd * 4:dd * 4 + 3]),
                        reads=(tb("ctails"),), writes=(tb("cmid_b"),))
                    tr.emit("scalar", lambda e, ib=ib: e.activation(
                        cmid_b[:, ib + 4:ib + 4 + T], convacc[:], AF.Silu),
                        reads=(tb("convacc"),), writes=(tb("cmid_b"),))
                    tr.emit("gpsimd", lambda e, ib=ib, dd=dd: e.tensor_copy(
                        ctails[:, dd * 4:dd * 4 + 3], cmid_b[:, ib + 1 + T:ib + 4 + T]),
                        reads=(tb("cmid_b"),), writes=(tb("ctails"),))
                    tr.emit("vector", lambda e, ib=ib, dd=dd: e.tensor_scalar(
                        convacc[:], cmid_b[:, ib + 1:ib + 1 + T], vap("cw10", dd), vap("cb1", dd), ALU.mult, ALU.add),
                        reads=(tb("cmid_b"), bv), writes=(tb("convacc"),))
                    for k in range(1, 4):
                        tr.emit("vector", lambda e, ib=ib, dd=dd, k=k: e.scalar_tensor_tensor(
                            convacc[:], cmid_b[:, ib + 1 + k:ib + 1 + k + T], vap(f"cw1{k}", dd),
                            convacc[:], ALU.mult, ALU.add),
                            reads=(tb("cmid_b"), tb("convacc"), bv), writes=(tb("convacc"),))
                    tr.emit("scalar", lambda e, dd=dd: e.activation(
                        xi2_b[:, dd * T:(dd + 1) * T], convacc[:], AF.Silu),
                        reads=(tb("convacc"),), writes=(tb("xi2_b"),))
                tr.emit_group("tensor", [
                    (lambda e, kc=kc, st=(kc == 0), sp=(kc == KC_DI - 1): e.matmul(
                        pb2[0:1, :], xpw_b[:, kc * 33:kc * 33 + 1],
                        xi2_b[:, kc * T:(kc + 1) * T], start=st, stop=sp))
                    for kc in range(KC_DI)], reads=(tb("xpw_b"), tb("xi2_b")), writes=(PS[2],))
                tr.emit_group("tensor", [
                    (lambda e, kc=kc, st=(kc == 0), sp=(kc == KC_DI - 1): e.matmul(
                        pb6[0:16, :], xpw_b[:, kc * 33 + 1:kc * 33 + 17],
                        xi2_b[:, kc * T:(kc + 1) * T], start=st, stop=sp))
                    for kc in range(KC_DI)], reads=(tb("xpw_b"), tb("xi2_b")), writes=(PS[6],))
                tr.emit_group("tensor", [
                    (lambda e, kc=kc, st=(kc == 0), sp=(kc == KC_DI - 1): e.matmul(
                        pb7[0:16, :], xpw_b[:, kc * 33 + 17:kc * 33 + 33],
                        xi2_b[:, kc * T:(kc + 1) * T], start=st, stop=sp))
                    for kc in range(KC_DI)], reads=(tb("xpw_b"), tb("xi2_b")), writes=(PS[7],))
                tr.emit("vector", lambda e: e.tensor_copy(projsb[0:1, :], pb2[0:1, :]),
                        reads=(PS[2],), writes=(tb("projsb"),))
                tr.emit("vector", lambda e: e.tensor_copy(bc_sb[0:16, :], pb6[0:16, :]),
                        reads=(PS[6],), writes=(tb("bc_sb"),))
                tr.emit("vector", lambda e: e.tensor_copy(bc_sb[32:48, :], pb7[0:16, :]),
                        reads=(PS[7],), writes=(tb("bc_sb"),))
                tr.emit("tensor", lambda e: e.matmul(pb3[:], ones_f[0:1, :], projsb[0:1, :], start=True, stop=True),
                        reads=(tb("projsb"), tb("ones_f")), writes=(PS[3],))
                for n in range(NS):
                    slot = 6 + n % 2
                    pst, psb_ = PS[slot], pb[slot]
                    tr.emit("tensor", lambda e, n=n, psb_=psb_: e.matmul(
                        psb_[:], bcsel_b[0:16, n * 128:(n + 1) * 128], bc_sb[0:16, :],
                        start=True, stop=True),
                        reads=(tb("bcsel_b"), tb("bc_sb")), writes=(pst,))
                    tr.emit("vector", lambda e, n=n, psb_=psb_: e.tensor_copy(
                        brep_b[:, n * T:(n + 1) * T], psb_[:]), reads=(pst,), writes=(tb("brep_b"),))
                for n in range(NS):
                    slot = 6 + n % 2
                    pst, psb_ = PS[slot], pb[slot]
                    tr.emit("tensor", lambda e, n=n, psb_=psb_: e.matmul(
                        psb_[:], bcsel_b[32:48, n * 128:(n + 1) * 128], bc_sb[32:48, :],
                        start=True, stop=True),
                        reads=(tb("bcsel_b"), tb("bc_sb")), writes=(pst,))
                    tr.emit("vector", lambda e, n=n, psb_=psb_: e.tensor_copy(
                        crep_b[:, n * T:(n + 1) * T], psb_[:]), reads=(pst,), writes=(tb("crep_b"),))
                def emit_dt(ds):
                    sl = ds % 2
                    tr.emit("scalar", lambda e, ds=ds, sl=sl: e.activation(
                        q1[:], pb3[:], AF.Exp, scale=vap("dtw", ds), bias=vap("dtb", ds)),
                        reads=(PS[3], bv), writes=(tb("q1"),))
                    tr.emit("scalar", lambda e, sl=sl: e.activation(
                        dtt[:, sl * T:(sl + 1) * T], q1[:], AF.Ln, bias=cst[:, 2:3]),
                        reads=(tb("q1"), tb("cst")), writes=(tb(f"dtt{sl}"),))
                    tr.emit("gpsimd", lambda e, ds=ds, sl=sl: e.tensor_tensor(
                        dtxb[:, sl * T:(sl + 1) * T], dtt[:, sl * T:(sl + 1) * T],
                        xi2_b[:, ds * T:(ds + 1) * T], ALU.mult),
                        reads=(tb(f"dtt{sl}"), tb("xi2_b")), writes=(tb(f"dtxb{sl}"),))
                emit_dt(0)
                for ds in range(ND_DI):
                    sl = ds % 2
                    if ds + 1 < ND_DI:
                        emit_dt(ds + 1)
                    dba = dtxb.ap()
                    dtx_bc = bass.AP(tensor=dba.tensor, offset=dba.offset + sl * T,
                                     ap=[[dba.ap[0][0], 128], [0, NS], [1, T]])
                    tr.emit("vector", lambda e, dtx_bc=dtx_bc: e.tensor_tensor(
                        ufull_b[:], dtx_bc, brep_b[:], ALU.mult),
                        reads=(tb(f"dtxb{sl}"), tb("brep_b")), writes=(tb("sq_b"),))
                    def emit_a(n, sl=sl):
                        asl = n % 3
                        if n < 8:
                            tr.emit("vector", lambda e, n=n, asl=asl, sl=sl: e.tensor_scalar(
                                abuf[:, asl * T:(asl + 1) * T], dtt[:, sl * T:(sl + 1) * T],
                                -float(n + 1), 1.0, ALU.mult, ALU.add),
                                reads=(tb(f"dtt{sl}"),), writes=(tb(f"abuf{asl}"),))
                        else:
                            tr.emit("scalar", lambda e, n=n, asl=asl, sl=sl: e.activation(
                                abuf[:, asl * T:(asl + 1) * T], dtt[:, sl * T:(sl + 1) * T],
                                AF.Identity, scale=cst[:, 4 + n - 8:5 + n - 8], bias=cst[:, 2:3]),
                                reads=(tb(f"dtt{sl}"), tb("cst")), writes=(tb(f"abuf{asl}"),))
                    emit_a(0)
                    for n in range(NS):
                        if n + 1 < NS:
                            emit_a(n + 1)
                        sl = n % 3
                        tr.emit("vector", lambda e, n=n, ds=ds, sl=sl: e.tensor_tensor_scan(
                            hbuf_b[:, n * T:(n + 1) * T], abuf[:, sl * T:(sl + 1) * T],
                            ufull_b[:, n * T:(n + 1) * T],
                            states_sb[:, ds * NS + n:ds * NS + n + 1], ALU.mult, ALU.add),
                            reads=(tb(f"abuf{sl}"), tb("sq_b"), tb("states")),
                            writes=(tb(f"hbuf{n % 2}"),))
                    tr.emit("vector", lambda e, ds=ds: e.tensor_copy(
                        states_sb[:, ds * NS:(ds + 1) * NS], _ap(hbuf_b, T - 1, [(T, NS)])),
                        reads=(tb("hbuf0"), tb("hbuf1")), writes=(tb("states"),))
                    tr.emit("vector", lambda e: e.tensor_tensor(
                        hbuf_b[:], hbuf_b[:], crep_b[:], ALU.mult),
                        reads=(tb("crep_b"),), writes=(tb("hbuf0"), tb("hbuf1")))
                    tr.emit("vector", lambda e: e.tensor_tensor(
                        ufull_b[:, 0:8 * T], hbuf_b[:, 0:8 * T], hbuf_b[:, 8 * T:16 * T], ALU.add),
                        reads=(tb("hbuf0"), tb("hbuf1")), writes=(tb("sq_b"),))
                    tr.emit("vector", lambda e: e.tensor_tensor(
                        ufull_b[:, 8 * T:12 * T], ufull_b[:, 0:4 * T], ufull_b[:, 4 * T:8 * T], ALU.add),
                        reads=(tb("sq_b"),), writes=(tb("sq_b"),))
                    tr.emit("vector", lambda e: e.tensor_tensor(
                        ufull_b[:, 12 * T:14 * T], ufull_b[:, 8 * T:10 * T], ufull_b[:, 10 * T:12 * T], ALU.add),
                        reads=(tb("sq_b"),), writes=(tb("sq_b"),))
                    tr.emit("vector", lambda e: e.tensor_tensor(
                        q2[:], ufull_b[:, 12 * T:13 * T], ufull_b[:, 13 * T:14 * T], ALU.add),
                        reads=(tb("sq_b"),), writes=(tb("q2"),))
                    tr.emit("vector", lambda e, ds=ds: e.scalar_tensor_tensor(
                        y2_b[:, ds * T:(ds + 1) * T], xi2_b[:, ds * T:(ds + 1) * T],
                        vap("Dv", ds), q2[:], ALU.mult, ALU.add),
                        reads=(tb("xi2_b"), tb("q2"), bv), writes=(tb("y2_b"),))
                def y2src(i):
                    return y2_b[:, i * T:(i + 1) * T]

                def y2sq(i):
                    return sq_b[:, i * T:(i + 1) * T]
                ln_stats(y2src, ND_DI, y2sq, True, tb("y2_b"))
                tr.emit("vector", lambda e: e.tensor_copy(srow_b[:], pb4[:]),
                        reads=(PS[4],), writes=(tb("srow_b"),))
                tr.emit("vector", lambda e: e.tensor_copy(mrow_b[:], pb5[:]),
                        reads=(PS[5],), writes=(tb("mrow_b"),))
                sba = srow_b.ap()
                s_bc = bass.AP(tensor=sba.tensor, offset=sba.offset,
                               ap=[[sba.ap[0][0], 128], [0, ND_DI], [1, T]])
                mba = mrow_b.ap()
                m_bc = bass.AP(tensor=mba.tensor, offset=mba.offset,
                               ap=[[mba.ap[0][0], 128], [0, ND_DI], [1, T]])
                tr.emit("vector", lambda e, s_bc=s_bc: e.tensor_tensor(
                    hbuf2_v[:], y2_b[:], s_bc, ALU.mult),
                    reads=(tb("y2_b"), tb("srow_b")), writes=(tb("hbuf0"), tb("hbuf1")))
                tr.emit("vector", lambda e, m_bc=m_bc: e.tensor_tensor(
                    hbuf2_v[:], hbuf2_v[:], m_bc, ALU.subtract),
                    reads=(tb("mrow_b"),), writes=(tb("hbuf0"), tb("hbuf1")))
                for i in range(ND_DI):
                    tr.emit("scalar", lambda e, i=i: e.activation(
                        q1[:], hbuf2_v[:, i * T:(i + 1) * T], AF.Identity,
                        scale=vap("mlng", i), bias=vap("mlnb", i)),
                        reads=(tb("hbuf0"), tb("hbuf1"), bv), writes=(tb("q1"),))
                    tr.emit("vector", lambda e, i=i: e.tensor_tensor(
                        brep_b[:, i * T:(i + 1) * T], q1[:], sres_b[:, i * T:(i + 1) * T], ALU.mult),
                        reads=(tb("q1"), tb("sres_b")), writes=(tb("brep_b"),))
                for dout in range(ND_D):
                    slot = dout % 2
                    pst, psb_ = PS[slot], pb[slot]
                    wtile, wb = stream_w(c_outproj, l * ND_D + dout, KC_DI, slot)
                    tr.emit_group("tensor", [
                        (lambda e, kc=kc, st=(kc == 0), sp=(kc == KC_DI - 1), wtile=wtile, psb_=psb_:
                         e.matmul(psb_[:], wtile[:, kc * 128:(kc + 1) * 128],
                                  brep_b[:, kc * T:(kc + 1) * T], start=st, stop=sp))
                        for kc in range(KC_DI)], reads=(wb, tb("brep_b")), writes=(pst,))
                    zsl = dout % 2
                    tr.emit("vector", lambda e, dout=dout, psb_=psb_, zsl=zsl: e.tensor_tensor(
                        zo_pp[:, zsl * T:(zsl + 1) * T], psb_[:],
                        z_in[:, dout * T:(dout + 1) * T], ALU.add),
                        reads=(pst, bz), writes=(tb("zo_pp"),))
                    dma_out(_dap(zdst, dout * 128 * S + t0, [(S, 128), (1, T)]),
                            zo_pp[:, zsl * T:(zsl + 1) * T], (tb("zo_pp"),))
            tr.barrier()

        if max_phase >= 6:
            ld_vec("b1", E["b1"][:], ND_D)
            ld_vec("b2", E["b2"][:], ND_D)
            ld_vec("bff", E["bff"][:], NT_FF)
            zsrc = zA if L % 2 == 0 else zB
            for ti in range(ntile):
                t0 = ti * T
                bz = tb("z_in")
                z_tile_load(z_in, bz, zsrc, t0, ND_D)
                bx1 = tb("f2_a")
                z_tile_load(f2_a, bx1, x1s, t0, ND_D)
                bl1 = tb("ln1_b")
                z_tile_load(ln1_b, bl1, ln1s, t0, ND_D)
                for i in range(ND_D):
                    tr.emit("vector", lambda e, i=i: e.tensor_copy(
                        xn_b[:, i * T:(i + 1) * T], z_in[:, i * T:(i + 1) * T]),
                        reads=(bz,), writes=(tb("xn_b"),))
                for dout in range(ND_D):
                    slot = dout % 2
                    pst, psb_ = PS[slot], pb[slot]
                    wtile, wb = stream_w(c_op1, dout, KC_D, slot)
                    tr.emit_group("tensor", [
                        (lambda e, kc=kc, st=(kc == 0), sp=(kc == KC_D - 1), wtile=wtile, psb_=psb_:
                         e.matmul(psb_[:], wtile[:, kc * 128:(kc + 1) * 128],
                                  xn_b[:, kc * T:(kc + 1) * T], start=st, stop=sp))
                        for kc in range(KC_D)], reads=(wb, tb("xn_b")), writes=(pst,))
                    tr.emit("scalar", lambda e, dout=dout, psb_=psb_: e.activation(
                        acts_b[:, dout * T:(dout + 1) * T], psb_[:], AF.Gelu, bias=vap("b1", dout)),
                        reads=(pst, bv), writes=(tb("acts_b"),))
                for dout in range(ND_D):
                    slot = dout % 2
                    pst, psb_ = PS[slot], pb[slot]
                    wtile, wb = stream_w(c_op2, dout, KC_D, slot)
                    tr.emit_group("tensor", [
                        (lambda e, kc=kc, st=(kc == 0), sp=(kc == KC_D - 1), wtile=wtile, psb_=psb_:
                         e.matmul(psb_[:], wtile[:, kc * 128:(kc + 1) * 128],
                                  acts_b[:, kc * T:(kc + 1) * T], start=st, stop=sp))
                        for kc in range(KC_D)], reads=(wb, tb("acts_b")), writes=(pst,))
                    tr.emit("scalar", lambda e, dout=dout, psb_=psb_: e.activation(
                        q1[:], psb_[:], AF.Identity, bias=vap("b2", dout)),
                        reads=(pst, bv), writes=(tb("q1"),))
                    tr.emit("vector", lambda e, dout=dout: e.tensor_tensor(
                        q2[:], q1[:], ln1_b[:, dout * T:(dout + 1) * T], ALU.add),
                        reads=(tb("q1"), bl1), writes=(tb("q2"),))
                    tr.emit("vector", lambda e, dout=dout: e.tensor_tensor(
                        f2_a[:, dout * T:(dout + 1) * T], q2[:],
                        f2_a[:, dout * T:(dout + 1) * T], ALU.add),
                        reads=(tb("q2"), bx1), writes=(bx1,))

                if dbg:
                    for i in range(ND_D):
                        dma_out(_dap(dbg1, i * 128 * S + t0, [(S, 128), (1, T)]),
                                f2_a[:, i * T:(i + 1) * T], (bx1,))
                        tr.emit("vector", lambda e, i=i: e.tensor_copy(
                            q1[:], acts_b[:, i * T:(i + 1) * T]),
                            reads=(tb("acts_b"),), writes=(tb("q1"),))
                        dma_out(_dap(dbg2, i * 128 * S + t0, [(S, 128), (1, T)]),
                                q1[:], (tb("q1"),))

                def srcf6(i):
                    return f2_a[:, i * T:(i + 1) * T]

                def sqf6(i):
                    return sq_b[:, i * T:(i + 1) * T]
                ln_stats(srcf6, ND_D, sqf6, False, bx1)
                ln_apply(srcf6, ND_D,
                         lambda i: (xn_b[:, i * T:(i + 1) * T], tb("xn_b")), bx1)

                def xtile6(kc):
                    return xn_b[:, kc * T:(kc + 1) * T]

                def consume6(dout, pst, psb_):
                    zsl = dout % 2
                    tr.emit("vector", lambda e, dout=dout, psb_=psb_, zsl=zsl: e.scalar_tensor_tensor(
                        zo_pp[:, zsl * T:(zsl + 1) * T], psb_[:], 0.5,
                        f2_a[:, dout * T:(dout + 1) * T], ALU.mult, ALU.add),
                        reads=(pst, bx1), writes=(tb("zo_pp"),))
                    dma_out(_dap(out_t, dout * 128 * S + t0, [(S, 128), (1, T)]),
                            zo_pp[:, zsl * T:(zsl + 1) * T], (tb("zo_pp"),))
                ffn_tree(xtile6, consume6, tb("xn_b"))
        tr.barrier()

        @block.sync
        def _(e):
            for a in tr.q["sync"]:
                a(e)

        @block.tensor
        def _(e):
            for a in tr.q["tensor"]:
                a(e)

        @block.vector
        def _(e):
            for a in tr.q["vector"]:
                a(e)

        @block.scalar
        def _(e):
            for a in tr.q["scalar"]:
                a(e)

        @block.gpsimd
        def _(e):
            for a in tr.q["gpsimd"]:
                a(e)

    return nc


def kernel(**inputs):
    g = _prep(inputs)
    hxs = g.pop("_hx_per_core")
    nc = build_program()
    in_maps = []
    for b in range(8):
        m = dict(g)
        m["hx"] = hxs[b]
        in_maps.append(m)
    res = run_bass_kernel_spmd(nc, in_maps, core_ids=list(range(8)))
    outs = [np.asarray(r["out"]).T for r in res.results]
    return np.ascontiguousarray(np.stack(outs, 0).astype(np.float32))


# version 12
# speedup vs baseline: 1.1252x; 1.0032x over previous
import numpy as np
import concourse.bass as bass
import concourse.mybir as mybir
from concourse.bass_utils import run_bass_kernel_spmd

F32 = mybir.dt.float32
BF16 = mybir.dt.bfloat16
AF = mybir.ActivationFunctionType
ALU = mybir.AluOpType
AX = mybir.AxisListType

D = 1024
DI = 2048
S = 8192
T = 512
NTILE = S // T
NS = 16
L = 4
KC_D = D // 128
KC_DI = DI // 128
ND_D = D // 128
ND_DI = DI // 128
EPS = 1e-5
DEPTH_FF = 7
PAR_FF = 8
NN_FF = 255
LOFF = [0, 1, 2, 3, 4, 5, 7, 11, 19]
NT_FF = 19
J2 = NT_FF * 128
SAME_ENGINE_EDGES = 2



def _vec2sb(v, ntiles):
    out = np.zeros(128 * ntiles, np.float32)
    out[:v.shape[0]] = v
    return np.ascontiguousarray(out.reshape(ntiles, 128).T)


def _ffn_perm():
    perm = np.full(J2, -1, np.int64)
    for d in range(DEPTH_FF + 1):
        sz = 1 << d
        base = 128 * LOFF[d]
        for p in range(PAR_FF):
            for i in range(sz):
                perm[base + p * sz + i] = p * NN_FF + (sz - 1 + i)
    return perm


def _tree_blocks():
    blocks = []
    for d in range(DEPTH_FF):
        psz = 1 << d
        csz = 1 << (d + 1)
        nchild_rows = PAR_FF * csz
        nct = (nchild_rows + 127) // 128
        for ctl in range(nct):
            ct = LOFF[d + 1] + ctl
            matL = np.zeros((128, 128), np.float32)
            matR = np.zeros((128, 128), np.float32)
            pt = None
            for cl in range(min(128, nchild_rows - ctl * 128)):
                crow = ctl * 128 + cl
                p = crow // csz
                ip = crow % csz
                prow = p * psz + (ip >> 1)
                side = ip & 1
                pt_this = LOFF[d] + prow // 128
                if pt is None:
                    pt = pt_this
                assert pt == pt_this
                (matR if side else matL)[prow % 128, cl] = 1.0
            blocks.append((d, pt, ct, 0, matL))
            blocks.append((d, pt, ct, 1, matR))
    return blocks


_TREE_BLOCKS = _tree_blocks()
_PERM = _ffn_perm()


def _prep(inputs):
    g = {}
    hs = np.asarray(inputs["L0_attn_hidden_states"], np.float32)
    g["_hx_per_core"] = [np.ascontiguousarray(hs[b].T) for b in range(8)]

    W_in = np.asarray(inputs["L0_ffn_W_in"], np.float32)
    b_in = np.asarray(inputs["L0_ffn_b_in"], np.float32)
    W_out = np.asarray(inputs["L0_ffn_W_out"], np.float32)
    WiT = np.zeros((D, J2), np.float32)
    WoT = np.zeros((J2, D), np.float32)
    bff = np.zeros(J2, np.float32)
    valid = _PERM >= 0
    WiT[:, valid] = W_in[_PERM[valid], :].T
    WoT[valid, :] = W_out[:, _PERM[valid]].T
    bff[valid] = b_in[_PERM[valid]]
    g["wffinT"] = np.ascontiguousarray(WiT)
    g["wffoutT"] = np.ascontiguousarray(WoT)
    g["bff"] = _vec2sb(bff, NT_FF)
    g["treeblk"] = np.ascontiguousarray(
        np.stack([m for (_, _, _, _, m) in _TREE_BLOCKS], 0))
    rootsel = np.zeros((1, 128), np.float32)
    rootsel[0, :PAR_FF] = 1.0
    g["rootsel"] = rootsel

    g["op1T"] = np.ascontiguousarray(np.asarray(inputs["L0_attn_op1_w"], np.float32).T)
    g["op2T"] = np.ascontiguousarray(np.asarray(inputs["L0_attn_op2_w"], np.float32).T)
    g["b1"] = _vec2sb(np.asarray(inputs["L0_attn_op1_b"], np.float32), ND_D)
    g["b2"] = _vec2sb(np.asarray(inputs["L0_attn_op2_b"], np.float32), ND_D)
    g["inng"] = _vec2sb(np.asarray(inputs["L0_attn_in_norm_g"], np.float32), ND_D)
    g["innb"] = _vec2sb(np.asarray(inputs["L0_attn_in_norm_b"], np.float32), ND_D)

    lng = np.asarray(inputs["L0_attn_ln_g"], np.float32)
    lnb = np.asarray(inputs["L0_attn_ln_b"], np.float32)
    g["lng"] = np.stack([_vec2sb(lng[l], ND_D) for l in range(L)])
    g["lnb"] = np.stack([_vec2sb(lnb[l], ND_D) for l in range(L)])
    g["inprojT"] = np.ascontiguousarray(
        np.asarray(inputs["L0_attn_in_proj_w"], np.float32).transpose(0, 2, 1))
    g["xprojT"] = np.ascontiguousarray(
        np.asarray(inputs["L0_attn_x_proj_w"], np.float32).transpose(0, 2, 1))
    g["outprojT"] = np.ascontiguousarray(
        np.asarray(inputs["L0_attn_out_proj_w"], np.float32).transpose(0, 2, 1))
    cw = np.asarray(inputs["L0_attn_conv_w"], np.float32)
    g["convw"] = np.ascontiguousarray(np.stack([
        np.stack([np.stack([_vec2sb(cw[l, c, :, k], ND_DI) for k in range(4)])
                  for c in range(2)]) for l in range(L)]))
    cb = np.asarray(inputs["L0_attn_conv_b"], np.float32)
    g["convb"] = np.ascontiguousarray(np.stack([
        np.stack([_vec2sb(cb[l, c], ND_DI) for c in range(2)]) for l in range(L)]))
    g["dtw"] = np.stack([_vec2sb(np.asarray(inputs["L0_attn_dt_proj_w"], np.float32)[l, :, 0], ND_DI) for l in range(L)])
    g["dtb"] = np.stack([_vec2sb(np.asarray(inputs["L0_attn_dt_proj_b"], np.float32)[l], ND_DI) for l in range(L)])
    g["Dv"] = np.stack([_vec2sb(np.asarray(inputs["L0_attn_D"], np.float32)[l], ND_DI) for l in range(L)])
    g["mlng"] = np.stack([_vec2sb(np.asarray(inputs["L0_attn_mamba_ln_g"], np.float32)[l], ND_DI) for l in range(L)])
    g["mlnb"] = np.stack([_vec2sb(np.asarray(inputs["L0_attn_mamba_ln_b"], np.float32)[l], ND_DI) for l in range(L)])
    g["onesm"] = np.ones((128, 128), np.float32)
    lv = np.zeros((4, 128, 307), np.float32)
    for l in range(4):
        cols = []
        cols.append(_vec2sb(lng[l], ND_D)); cols.append(_vec2sb(lnb[l], ND_D))
        cols.append(_vec2sb(np.asarray(inputs["L0_attn_in_norm_g"], np.float32), ND_D))
        cols.append(_vec2sb(np.asarray(inputs["L0_attn_in_norm_b"], np.float32), ND_D))
        for c in range(2):
            for k in range(4):
                cols.append(_vec2sb(cw[l, c, :, k], ND_DI))
        for c in range(2):
            cols.append(_vec2sb(cb[l, c], ND_DI))
        cols.append(_vec2sb(np.asarray(inputs["L0_attn_dt_proj_w"], np.float32)[l, :, 0], ND_DI))
        cols.append(_vec2sb(np.asarray(inputs["L0_attn_dt_proj_b"], np.float32)[l], ND_DI))
        cols.append(_vec2sb(np.asarray(inputs["L0_attn_D"], np.float32)[l], ND_DI))
        cols.append(_vec2sb(np.asarray(inputs["L0_attn_mamba_ln_g"], np.float32)[l], ND_DI))
        cols.append(_vec2sb(np.asarray(inputs["L0_attn_mamba_ln_b"], np.float32)[l], ND_DI))
        cols.append(_vec2sb(bff, NT_FF))
        cols.append(_vec2sb(np.asarray(inputs["L0_attn_op1_b"], np.float32), ND_D))
        cols.append(_vec2sb(np.asarray(inputs["L0_attn_op2_b"], np.float32), ND_D))
        lv[l] = np.concatenate(cols, axis=1)
    g["layvecs"] = np.ascontiguousarray(lv)
    sel = np.zeros((48, 16 * 128), np.float32)
    for n in range(16):
        sel[n, n * 128:(n + 1) * 128] = 1.0
        sel[32 + n, n * 128:(n + 1) * 128] = 1.0
    g["bcsel"] = sel
    return g



class Buf:
    __slots__ = ("name", "writer", "readers")

    def __init__(self, name):
        self.name = name
        self.writer = None
        self.readers = []


class Tracker:
    ENGINES = ("sync", "tensor", "vector", "scalar", "gpsimd")

    def __init__(self, sems, lane_pool):
        self.sems = dict(sems)
        self.lane_pool = list(lane_pool)
        self.cnt = {e: 0 for e in self.ENGINES}
        self.q = {e: [] for e in self.ENGINES}
        self.trace = []
        self.waited = {e: {} for e in self.ENGINES}

    def lane(self, name):
        if name not in self.sems:
            assert self.lane_pool, "out of DMA lane semaphores"
            self.sems[name] = self.lane_pool.pop()
            self.cnt[name] = 0
        return name

    def _waits_for(self, eng, reads, writes):
        need = {}

        def add(dep):
            if dep is None:
                return
            wsn, wcnt = dep
            if wsn == eng:
                if eng == "tensor":
                    return
                if SAME_ENGINE_EDGES is False:
                    return
                if SAME_ENGINE_EDGES is not True and                         self.cnt[eng] - wcnt >= SAME_ENGINE_EDGES:
                    return
            if need.get(wsn, 0) < wcnt:
                need[wsn] = wcnt
        for b in reads:
            add(b.writer)
        for b in writes:
            add(b.writer)
            for r in b.readers:
                add(r)
        return need

    def _push(self, eng, need, fns, inc, sn):
        sem = self.sems

        def action(e, need=need, fns=fns, inc=inc, sn=sn):
            for weng in sorted(need):
                e.wait_ge(sem[weng], need[weng])
            last = None
            for f in fns:
                last = f(e)
            last.then_inc(sem[sn], inc)
        self.q[eng].append(action)

    def emit(self, eng, fn, reads=(), writes=(), inc=1):
        return self.emit_group(eng, [fn], reads=reads, writes=writes, inc=inc)

    def emit_group(self, eng, fns, reads=(), writes=(), inc=1, sem_name=None):
        sn = sem_name or eng
        need = self._waits_for(eng, reads, writes)
        w = self.waited[eng]
        need = {k: v for k, v in need.items() if v > w.get(k, 0)}
        for k, v in need.items():
            w[k] = v
        self.cnt[sn] += inc
        self.trace.append((eng, sn, self.cnt[sn], tuple(b.name for b in reads),
                           tuple(b.name for b in writes), dict(need), len(fns)))
        self._push(eng, need, fns, inc, sn)
        dep = (sn, self.cnt[sn])
        for b in writes:
            b.writer = dep
            b.readers = []
        for b in reads:
            if b not in writes:
                b.readers.append(dep)
        return dep

    def barrier(self):
        snap = dict(self.cnt)
        sem = self.sems
        for eng in self.ENGINES:
            w = self.waited[eng]
            todo = {osn: v for osn, v in snap.items()
                    if osn != eng and v > 0 and v > w.get(osn, 0)}
            for osn, v in todo.items():
                w[osn] = v
            def action(e, todo=todo, eng=eng):
                for osn, v in todo.items():
                    e.wait_ge(sem[osn], v)
            self.q[eng].append(action)


def _ap(handle, offset, dims, npart=128):
    base = handle.ap()
    return bass.AP(tensor=base.tensor, offset=base.offset + offset,
                   ap=[[base.ap[0][0], npart]] + [[s, c] for (s, c) in dims])


def _dap(dram_ap, offset, dims):
    return bass.AP(tensor=dram_ap.tensor, offset=dram_ap.offset + offset,
                   ap=[[s, c] for (s, c) in dims])



def build_program(max_phase=99, dbg=False, ntile=NTILE):
    nc = bass.Bass("TRN2", target_bir_lowering=False)
    E = {}

    def ein(name, shape):
        E[name] = nc.dram_tensor(name, list(shape), F32, kind="ExternalInput").ap()

    ein("hx", (D, S))
    ein("wffinT", (D, J2)); ein("wffoutT", (J2, D)); ein("bff", (128, NT_FF))
    ein("treeblk", (len(_TREE_BLOCKS), 128, 128)); ein("rootsel", (1, 128))
    ein("op1T", (D, D)); ein("op2T", (D, D)); ein("b1", (128, ND_D)); ein("b2", (128, ND_D))
    ein("inng", (128, ND_D)); ein("innb", (128, ND_D))
    ein("lng", (L, 128, ND_D)); ein("lnb", (L, 128, ND_D))
    ein("inprojT", (L, D, 2 * DI)); ein("xprojT", (L, DI, 33)); ein("outprojT", (L, DI, D))
    ein("convw", (L, 2, 4, 128, ND_DI)); ein("convb", (L, 2, 128, ND_DI))
    ein("dtw", (L, 128, ND_DI)); ein("dtb", (L, 128, ND_DI)); ein("Dv", (L, 128, ND_DI))
    ein("mlng", (L, 128, ND_DI)); ein("mlnb", (L, 128, ND_DI))
    ein("onesm", (128, 128))
    ein("layvecs", (L, 128, 307))
    ein("bcsel", (48, 16 * 128))

    out_t = nc.dram_tensor("out", [D, S], F32, kind="ExternalOutput").ap()
    KD = "ExternalOutput" if dbg else "Internal"
    if dbg:
        dbg1 = nc.dram_tensor("dbg1", [D, S], F32, kind="ExternalOutput").ap()
        dbg2 = nc.dram_tensor("dbg2", [D, S], F32, kind="ExternalOutput").ap()
    zA = nc.dram_tensor("zA", [D, S], F32, kind=KD).ap()
    zB = nc.dram_tensor("zB", [D, S], F32, kind=KD).ap()
    x1s = nc.dram_tensor("x1s", [D, S], F32, kind=KD).ap()
    ln1s = nc.dram_tensor("ln1s", [D, S], BF16, kind=KD).ap()
    c_inproj = nc.dram_tensor("c_inproj", [L, 2 * ND_DI, 128, KC_D * 128], BF16, kind="Internal").ap()
    c_outproj = nc.dram_tensor("c_outproj", [L, ND_D, 128, KC_DI * 128], BF16, kind="Internal").ap()
    c_wffin = nc.dram_tensor("c_wffin", [NT_FF, 128, KC_D * 128], BF16, kind="Internal").ap()
    c_wffout = nc.dram_tensor("c_wffout", [ND_D, 128, NT_FF * 128], BF16, kind="Internal").ap()
    c_op1 = nc.dram_tensor("c_op1", [ND_D, 128, KC_D * 128], BF16, kind="Internal").ap()
    c_op2 = nc.dram_tensor("c_op2", [ND_D, 128, KC_D * 128], BF16, kind="Internal").ap()

    NBLK = len(_TREE_BLOCKS)
    import contextlib
    es = contextlib.ExitStack()
    with es:
        ones_f = es.enter_context(nc.sbuf_tensor("ones_f", [128, 128], F32))
        ones_b = es.enter_context(nc.sbuf_tensor("ones_b", [128, 128], BF16))
        treesb = es.enter_context(nc.sbuf_tensor("treesb", [128, NBLK * 128], BF16))
        rootsb = es.enter_context(nc.sbuf_tensor("rootsb", [1, 128], BF16))
        onerow_b = es.enter_context(nc.sbuf_tensor("onerow_b", [1, T], BF16))
        wstg = es.enter_context(nc.sbuf_tensor("wstg", [128, 384], F32))
        wstgb = es.enter_context(nc.sbuf_tensor("wstgb", [128, 384], BF16))
        xpw_b = es.enter_context(nc.sbuf_tensor("xpw_b", [128, KC_DI * 33], BF16))
        wbufA = es.enter_context(nc.sbuf_tensor("wbufA", [128, NT_FF * 128], BF16))
        wbufB = es.enter_context(nc.sbuf_tensor("wbufB", [128, NT_FF * 128], BF16))
        vecs = es.enter_context(nc.sbuf_tensor("vecs", [128, 320], F32))
        z_in = es.enter_context(nc.sbuf_tensor("z_in", [128, ND_D * T], F32))
        zo_pp = es.enter_context(nc.sbuf_tensor("zo_pp", [128, 2 * T], F32))
        xn_b = es.enter_context(nc.sbuf_tensor("xn_b", [128, ND_D * T], BF16))
        cin_b = es.enter_context(nc.sbuf_tensor("cin_b", [128, 2 * (T + 4)], BF16))
        xtails = es.enter_context(nc.sbuf_tensor("xtails", [128, ND_DI * 4], BF16))
        cmid_b = es.enter_context(nc.sbuf_tensor("cmid_b", [128, 2 * (T + 4)], BF16))
        ctails = es.enter_context(nc.sbuf_tensor("ctails", [128, ND_DI * 4], BF16))
        arena = es.enter_context(nc.sbuf_tensor("arena", [128, 49152], BF16))
        projsb = es.enter_context(nc.sbuf_tensor("projsb", [33, T], F32))
        bc_sb = es.enter_context(nc.sbuf_tensor("bc_sb", [48, T], BF16))
        bcsel_b = es.enter_context(nc.sbuf_tensor("bcsel_b", [48, 16 * 128], BF16))
        dtt = es.enter_context(nc.sbuf_tensor("dtt", [128, 2 * T], F32))
        dtxb = es.enter_context(nc.sbuf_tensor("dtxb", [128, 2 * T], BF16))
        abuf = es.enter_context(nc.sbuf_tensor("abuf", [128, 3 * T], BF16))

        q1 = es.enter_context(nc.sbuf_tensor("q1", [128, T], F32))
        q2 = es.enter_context(nc.sbuf_tensor("q2", [128, T], F32))
        states_sb = es.enter_context(nc.sbuf_tensor("states", [128, ND_DI * NS], F32))
        convacc = es.enter_context(nc.sbuf_tensor("convacc", [128, T], F32))
        sq_b = es.enter_context(nc.sbuf_tensor("sq_b", [128, ND_DI * T], BF16))
        statA = es.enter_context(nc.sbuf_tensor("statA", [128, 3 * T], F32))
        srow_b = es.enter_context(nc.sbuf_tensor("srow_b", [128, T], BF16))
        mrow_b = es.enter_context(nc.sbuf_tensor("mrow_b", [128, T], BF16))
        cst = es.enter_context(nc.sbuf_tensor("cst", [128, 12], F32))

        pb0 = es.enter_context(nc.psum_tensor("pb0", [128, T], F32))
        pb1 = es.enter_context(nc.psum_tensor("pb1", [128, T], F32))
        pb2 = es.enter_context(nc.psum_tensor("pb2", [128, T], F32))
        pb3 = es.enter_context(nc.psum_tensor("pb3", [128, T], F32))
        pb4 = es.enter_context(nc.psum_tensor("pb4", [128, T], F32))
        pb5 = es.enter_context(nc.psum_tensor("pb5", [128, T], F32))
        pb6 = es.enter_context(nc.psum_tensor("pb6", [128, T], F32))
        pb7 = es.enter_context(nc.psum_tensor("pb7", [128, T], F32))
        s_sync = es.enter_context(nc.semaphore("s_sync"))
        lane_sems = [es.enter_context(nc.semaphore(f"lane{i}")) for i in range(24)]
        s_tensor = es.enter_context(nc.semaphore("s_tensor"))
        s_vector = es.enter_context(nc.semaphore("s_vector"))
        s_scalar = es.enter_context(nc.semaphore("s_scalar"))
        s_gpsimd = es.enter_context(nc.semaphore("s_gpsimd"))
        block = es.enter_context(nc.Block())
        pb = [pb0, pb1, pb2, pb3, pb4, pb5, pb6, pb7]

        class TView:

            def __init__(self, h, off, nels):
                self.h, self.off, self.nels = h, off, nels

            def ap(self):
                b = self.h.ap()
                return bass.AP(tensor=b.tensor, offset=b.offset + self.off,
                               ap=[[b.ap[0][0], 128], [1, self.nels]])

            def __getitem__(self, key):
                base = self.ap()
                if isinstance(key, slice):
                    return base
                ps, fs = key
                assert (ps.start or 0) == 0
                f0 = fs.start or 0
                f1 = fs.stop if fs.stop is not None else self.nels
                return bass.AP(tensor=base.tensor, offset=base.offset + f0,
                               ap=[[base.ap[0][0], 128], [1, f1 - f0]])

        arena_f = arena.bitcast(F32)
        ufull_b = TView(sq_b, 0, NS * T)
        hbuf2_v = TView(arena, 24576, ND_DI * T)
        xi2_b = TView(arena, 0, ND_DI * T)
        sres_b = TView(arena, 8192, ND_DI * T)
        y2_b = TView(arena, 16384, ND_DI * T)
        hbuf_b = TView(arena, 24576, NS * T)
        brep_b = TView(arena, 32768, NS * T)
        crep_b = TView(arena, 40960, NS * T)
        acts_b = TView(arena, 0, NT_FF * T)
        dec_b = TView(arena, 9728, NT_FF * T)
        dmap_b = TView(arena, 19456, NT_FF * T)
        oLR_b = TView(arena, 29184, 8 * T)
        ln1_b = TView(arena, 33280, ND_D * T)
        f2_a = TView(arena_f, 18688, ND_D * T)
        tr = Tracker({"sync": s_sync, "tensor": s_tensor, "vector": s_vector,
                      "scalar": s_scalar, "gpsimd": s_gpsimd}, lane_sems)
        build_program.last_trace = tr.trace
        B = {}

        def tb(name):
            if name not in B:
                B[name] = Buf(name)
            return B[name]

        PS = [tb(f"ps{i}") for i in range(8)]
        bv = tb("vecs")

        VEC = {}
        _vc = 0
        for nm, n in [("lng", ND_D), ("lnb", ND_D), ("inng", ND_D), ("innb", ND_D),
                      ("cw00", ND_DI), ("cw01", ND_DI), ("cw02", ND_DI), ("cw03", ND_DI),
                      ("cw10", ND_DI), ("cw11", ND_DI), ("cw12", ND_DI), ("cw13", ND_DI),
                      ("cb0", ND_DI), ("cb1", ND_DI),
                      ("dtw", ND_DI), ("dtb", ND_DI), ("Dv", ND_DI),
                      ("mlng", ND_DI), ("mlnb", ND_DI),
                      ("bff", NT_FF), ("b1", ND_D), ("b2", ND_D)]:
            VEC[nm] = _vc
            _vc += n
        assert _vc <= 320

        def vap(name, i):
            c = VEC[name] + i
            return vecs[:, c:c + 1]

        def ld_vec(dst, src_ap, n):
            c = VEC[dst]
            tr.emit_group("sync", [lambda e, c=c, s=src_ap, n=n: e.dma_start(
                out=vecs[:, c:c + n], in_=s)], writes=(bv,), inc=16,
                sem_name=tr.lane("L_vecs"))

        def dma_in(dst_ap, src_ap, wb, rb=()):
            ln_ = tr.lane("L_" + wb[0].name)
            tr.emit_group("sync", [lambda e, d=dst_ap, s=src_ap: e.dma_start(out=d, in_=s)],
                          reads=rb, writes=wb, inc=16, sem_name=ln_)

        def dma_out(dst_ap, src_ap, rb, wb=()):
            ln_ = tr.lane("S_" + rb[0].name)
            tr.emit_group("gpsimd", [lambda e, d=dst_ap, s=src_ap: e.dma_start(out=d, in_=s)],
                          reads=rb, writes=wb, inc=16, sem_name=ln_)

        dma_in(ones_f[:], E["onesm"][:], (tb("ones_f"),))
        tr.emit("vector", lambda e: e.tensor_copy(ones_b[:], ones_f[:]),
                reads=(tb("ones_f"),), writes=(tb("ones_b"),))
        tr.emit("gpsimd", lambda e: e.memset(onerow_b[:], 1.0), writes=(tb("onerow_b"),))
        tr.emit("gpsimd", lambda e: e.memset(cst[:, 0:1], EPS), writes=(tb("cst"),))
        tr.emit("gpsimd", lambda e: e.memset(cst[:, 1:2], -0.5), writes=(tb("cst"),))
        tr.emit("gpsimd", lambda e: e.memset(cst[:, 2:3], 1.0), writes=(tb("cst"),))
        for _n in range(8, 16):
            tr.emit("gpsimd", lambda e, _n=_n: e.memset(cst[:, 4 + _n - 8:5 + _n - 8], -float(_n + 1)),
                    writes=(tb("cst"),))

        def cache_build(src2d, rows, cols, dst_ap_fn, wname):
            CC = 384
            for ch in range(rows // 128):
                for c0 in range(0, cols, CC):
                    cw_ = min(CC, cols - c0)
                    dma_in(wstg[:, 0:cw_], src2d[ch * 128:(ch + 1) * 128, c0:c0 + cw_],
                           (tb("wstg"),))
                    tr.emit("vector", lambda e, c=cw_: e.tensor_copy(wstgb[:, 0:c], wstg[:, 0:c]),
                            reads=(tb("wstg"),), writes=(tb("wstgb"),))
                    dma_out(dst_ap_fn(ch, c0, cw_), wstgb[:, 0:cw_], (tb("wstgb"),))

        for l in range(L):
            def dst_ip(kc, c0, cw_, l=l):
                return _dap(c_inproj,
                            l * (2 * ND_DI * 128 * KC_D * 128)
                            + (c0 // 128) * (128 * KC_D * 128) + kc * 128,
                            [(KC_D * 128, 128), (128 * KC_D * 128, cw_ // 128), (1, 128)])
            cache_build(E["inprojT"][l], D, 2 * DI, dst_ip, "ip")

            def dst_op(kc, c0, cw_, l=l):
                return _dap(c_outproj,
                            l * (ND_D * 128 * KC_DI * 128)
                            + (c0 // 128) * (128 * KC_DI * 128) + kc * 128,
                            [(KC_DI * 128, 128), (128 * KC_DI * 128, cw_ // 128), (1, 128)])
            cache_build(E["outprojT"][l], DI, D, dst_op, "op")

        cache_build(E["wffinT"], D, J2,
                    lambda kc, c0, cw_: _dap(c_wffin, (c0 // 128) * (128 * KC_D * 128) + kc * 128,
                                             [(KC_D * 128, 128), (128 * KC_D * 128, cw_ // 128), (1, 128)]), "fi")
        cache_build(E["wffoutT"], J2, D,
                    lambda kc, c0, cw_: _dap(c_wffout, (c0 // 128) * (128 * NT_FF * 128) + kc * 128,
                                             [(NT_FF * 128, 128), (128 * NT_FF * 128, cw_ // 128), (1, 128)]), "fo")
        cache_build(E["op1T"], D, D,
                    lambda kc, c0, cw_: _dap(c_op1, (c0 // 128) * (128 * KC_D * 128) + kc * 128,
                                             [(KC_D * 128, 128), (128 * KC_D * 128, cw_ // 128), (1, 128)]), "o1")
        cache_build(E["op2T"], D, D,
                    lambda kc, c0, cw_: _dap(c_op2, (c0 // 128) * (128 * KC_D * 128) + kc * 128,
                                             [(KC_D * 128, 128), (128 * KC_D * 128, cw_ // 128), (1, 128)]), "o2")

        for b0 in range(0, NBLK, 1):
            nb = 1
            src = _dap(E["treeblk"], b0 * 128 * 128,
                       [(128, 128), (128 * 128, nb), (1, 128)])
            dma_in(wstg[:, 0:nb * 128], src, (tb("wstg"),))
            tr.emit("vector", lambda e, b0=b0, nb=nb: e.tensor_copy(
                treesb[:, b0 * 128:(b0 + nb) * 128], wstg[:, 0:nb * 128]),
                reads=(tb("wstg"),), writes=(tb("treesb"),))
        for c0 in range(0, 2048, 128):
            dma_in(wstg[0:48, 0:128], E["bcsel"][:, c0:c0 + 128], (tb("wstg"),))
            tr.emit("vector", lambda e, c0=c0: e.tensor_copy(
                bcsel_b[0:48, c0:c0 + 128], wstg[0:48, 0:128]),
                reads=(tb("wstg"),), writes=(tb("bcsel_b"),))
        dma_in(wstg[0:1, 0:128], E["rootsel"][:], (tb("wstg"),))
        tr.emit("vector", lambda e: e.tensor_copy(rootsb[:], wstg[0:1, 0:128]),
                reads=(tb("wstg"),), writes=(tb("rootsb"),))
        tr.barrier()

        bstat = tb("stat")

        def ln_stats(src_tiles, nt, sq_view, src_is_bf, src_buf):
            Dtot = float(nt * 128)
            for i in range(nt):
                tr.emit("scalar", lambda e, i=i: e.activation(sq_view(i), src_tiles(i), AF.Square),
                        reads=(src_buf,), writes=(tb("sq_b"),))
            oc = ones_b[:, 0:1] if src_is_bf else ones_f[:, 0:1]
            ob = tb("ones_b") if src_is_bf else tb("ones_f")
            tr.emit_group("tensor", [
                (lambda e, i=i, st=(i == 0), sp=(i == nt - 1): e.matmul(
                    pb2[0:1, :], oc, src_tiles(i), start=st, stop=sp))
                for i in range(nt)], reads=(src_buf, ob), writes=(PS[2],))
            tr.emit_group("tensor", [
                (lambda e, i=i, st=(i == 0), sp=(i == nt - 1): e.matmul(
                    pb3[0:1, :], ones_b[:, 0:1], sq_view(i), start=st, stop=sp))
                for i in range(nt)], reads=(tb("sq_b"), tb("ones_b")), writes=(PS[3],))
            tr.emit("vector", lambda e: e.tensor_scalar(statA[0:1, 0:T], pb2[0:1, :], 1.0 / Dtot, None, ALU.mult),
                    reads=(PS[2],), writes=(bstat,))
            tr.emit("vector", lambda e: e.tensor_scalar(statA[32:33, 0:T], pb3[0:1, :], 1.0 / Dtot, None, ALU.mult),
                    reads=(PS[3],), writes=(bstat,))
            tr.emit("vector", lambda e: e.tensor_tensor(statA[32:33, T:2 * T], statA[0:1, 0:T], statA[0:1, 0:T], ALU.mult),
                    reads=(bstat,), writes=(bstat,))
            tr.emit("vector", lambda e: e.tensor_tensor(statA[64:65, 0:T], statA[32:33, 0:T], statA[32:33, T:2 * T], ALU.subtract),
                    reads=(bstat,), writes=(bstat,))
            tr.emit("scalar", lambda e: e.activation(statA[64:65, T:2 * T], statA[64:65, 0:T], AF.Ln, bias=cst[64:65, 0:1]),
                    reads=(bstat, tb("cst")), writes=(bstat,))
            tr.emit("scalar", lambda e: e.activation(statA[0:1, T:2 * T], statA[64:65, T:2 * T], AF.Exp, scale=cst[64:65, 1:2]),
                    reads=(bstat, tb("cst")), writes=(bstat,))
            tr.emit("vector", lambda e: e.tensor_tensor(statA[0:1, 2 * T:3 * T], statA[0:1, 0:T], statA[0:1, T:2 * T], ALU.mult),
                    reads=(bstat,), writes=(bstat,))
            tr.emit("tensor", lambda e: e.matmul(pb4[:], ones_f[0:1, :], statA[0:1, T:2 * T], start=True, stop=True),
                    reads=(bstat, tb("ones_f")), writes=(PS[4],))
            tr.emit("tensor", lambda e: e.matmul(pb5[:], ones_f[0:1, :], statA[0:1, 2 * T:3 * T], start=True, stop=True),
                    reads=(bstat, tb("ones_f")), writes=(PS[5],))

        def ln_apply(src_tiles, nt, out_fn, src_buf, gain=None, bias=None):
            for i in range(nt):
                tr.emit("vector", lambda e, i=i: e.tensor_tensor(q1[:], src_tiles(i), pb4[:], ALU.mult),
                        reads=(src_buf, PS[4]), writes=(tb("q1"),))
                if gain is None:
                    oap, ob = out_fn(i)
                    tr.emit("vector", lambda e, oap=oap: e.tensor_tensor(oap, q1[:], pb5[:], ALU.subtract),
                            reads=(tb("q1"), PS[5]), writes=(ob,))
                else:
                    tr.emit("vector", lambda e: e.tensor_tensor(q2[:], q1[:], pb5[:], ALU.subtract),
                            reads=(tb("q1"), PS[5]), writes=(tb("q2"),))
                    oap, ob = out_fn(i)
                    tr.emit("scalar", lambda e, oap=oap, g=gain(i), bb=bias(i): e.activation(
                        oap, q2[:], AF.Identity, scale=g, bias=bb),
                        reads=(tb("q2"), bv), writes=(ob,))

        def stream_w(cache_ap, idx, ncols, slot):
            wtile = wbufA if slot == 0 else wbufB
            wb = tb("wbufA") if slot == 0 else tb("wbufB")
            src = _dap(cache_ap, idx * 128 * ncols * 128, [(ncols * 128, 128), (1, ncols * 128)])
            dma_in(_ap(wtile, 0, [(1, ncols * 128)]), src, (wb,))
            return wtile, wb

        def ffn_tree(xtile, consume, xnb_buf):
            blocks_by_ct = {}
            for bi, (d, pt, ct, side, _) in enumerate(_TREE_BLOCKS):
                blocks_by_ct.setdefault((d, ct), []).append((bi, pt, side))
            for j in range(NT_FF):
                slot = j % 2
                pst, psb_ = (PS[slot], pb[slot])
                wtile, wb = stream_w(c_wffin, j, KC_D, slot)
                tr.emit_group("tensor", [
                    (lambda e, kc=kc, st=(kc == 0), sp=(kc == KC_D - 1), wtile=wtile, psb_=psb_:
                     e.matmul(psb_[:], wtile[:, kc * 128:(kc + 1) * 128], xtile(kc), start=st, stop=sp))
                    for kc in range(KC_D)], reads=(wb, xnb_buf), writes=(pst,))
                tr.emit("scalar", lambda e, j=j, psb_=psb_: e.activation(
                    acts_b[:, j * T:(j + 1) * T], psb_[:], AF.Silu,
                    bias=vecs[:, VEC["bff"] + j:VEC["bff"] + j + 1]),
                    reads=(bv,), writes=(tb("acts_b"), pst))
                tr.emit("vector", lambda e, j=j, psb_=psb_: e.tensor_scalar(
                    dec_b[:, j * T:(j + 1) * T], psb_[:],
                    vecs[:, VEC["bff"] + j:VEC["bff"] + j + 1], 0.0, ALU.add, ALU.is_gt),
                    reads=(bv,), writes=(tb("dec_b"), pst))
            tr.emit("tensor", lambda e: e.matmul(pb2[:], rootsb[0:1, :], onerow_b[0:1, :], start=True, stop=True),
                    reads=(tb("rootsb"), tb("onerow_b")), writes=(PS[2],))
            tr.emit("vector", lambda e: e.tensor_copy(dmap_b[:, 0:T], pb2[:]),
                    reads=(PS[2],), writes=(tb("dmap_b"),))
            for d in range(DEPTH_FF):
                pts = sorted({pt for (dd, pt, ct, s2, m) in _TREE_BLOCKS if dd == d})
                pt_slot = {pt: k for k, pt in enumerate(pts)}
                for pt, k in pt_slot.items():
                    tr.emit("vector", lambda e, pt=pt, k=k: e.tensor_tensor(
                        oLR_b[:, (2 * k + 1) * T:(2 * k + 2) * T],
                        dmap_b[:, pt * T:(pt + 1) * T], dec_b[:, pt * T:(pt + 1) * T], ALU.mult),
                        reads=(tb("dmap_b"), tb("dec_b")), writes=(tb("oLR_b"),))
                    tr.emit("vector", lambda e, pt=pt, k=k: e.tensor_tensor(
                        oLR_b[:, (2 * k) * T:(2 * k + 1) * T],
                        dmap_b[:, pt * T:(pt + 1) * T], oLR_b[:, (2 * k + 1) * T:(2 * k + 2) * T], ALU.subtract),
                        reads=(tb("dmap_b"),), writes=(tb("oLR_b"),))
                cts = sorted({ct for (dd, pt, ct, s2, m) in _TREE_BLOCKS if dd == d})
                for ci, ct in enumerate(cts):
                    blks = blocks_by_ct[(d, ct)]
                    slot = 2 + ci % 2
                    pst, psb_ = PS[slot], pb[slot]
                    tr.emit_group("tensor", [
                        (lambda e, bi=bi, sl=2 * pt_slot[pt] + side, st=(ii == 0), sp=(ii == len(blks) - 1), psb_=psb_:
                         e.matmul(psb_[:], treesb[:, bi * 128:(bi + 1) * 128],
                                  oLR_b[:, sl * T:(sl + 1) * T], start=st, stop=sp))
                        for ii, (bi, pt, side) in enumerate(blks)],
                        reads=(tb("treesb"), tb("oLR_b")), writes=(pst,))
                    tr.emit("vector", lambda e, ct=ct, psb_=psb_: e.tensor_copy(
                        dmap_b[:, ct * T:(ct + 1) * T], psb_[:]),
                        reads=(pst,), writes=(tb("dmap_b"),))
            for j in range(NT_FF):
                tr.emit("vector", lambda e, j=j: e.tensor_tensor(
                    acts_b[:, j * T:(j + 1) * T], acts_b[:, j * T:(j + 1) * T],
                    dmap_b[:, j * T:(j + 1) * T], ALU.mult),
                    reads=(tb("acts_b"), tb("dmap_b")), writes=(tb("acts_b"),))
            for dout in range(ND_D):
                slot = dout % 2
                pst, psb_ = PS[slot], pb[slot]
                wtile, wb = stream_w(c_wffout, dout, NT_FF, slot)
                tr.emit_group("tensor", [
                    (lambda e, kc=kc, st=(kc == 0), sp=(kc == NT_FF - 1), wtile=wtile, psb_=psb_:
                     e.matmul(psb_[:], wtile[:, kc * 128:(kc + 1) * 128],
                              acts_b[:, kc * T:(kc + 1) * T], start=st, stop=sp))
                    for kc in range(NT_FF)], reads=(wb, tb("acts_b")), writes=(pst,))
                consume(dout, pst, psb_)

        def z_tile_load(dst_handle, dst_buf, src_dram, t0, nt):
            dma_in(_ap(dst_handle, 0, [(T, nt), (1, T)]),
                   _dap(src_dram, t0, [(S, 128), (128 * S, nt), (1, T)]), (dst_buf,))

        if max_phase >= 1:
            ld_vec("bff", E["bff"][:], NT_FF)
            ld_vec("inng", E["inng"][:], ND_D)
            ld_vec("innb", E["innb"][:], ND_D)
            for ti in range(ntile):
                t0 = ti * T
                bz = tb("z_in")
                z_tile_load(z_in, bz, E["hx"], t0, ND_D)

                def srcf(i):
                    return z_in[:, i * T:(i + 1) * T]

                def sqf(i):
                    return sq_b[:, i * T:(i + 1) * T]
                ln_stats(srcf, ND_D, sqf, False, bz)
                ln_apply(srcf, ND_D,
                         lambda i: (xn_b[:, i * T:(i + 1) * T], tb("xn_b")), bz)

                def xtile(kc):
                    return xn_b[:, kc * T:(kc + 1) * T]

                def consume(dout, pst, psb_):
                    tr.emit("vector", lambda e, dout=dout, psb_=psb_: e.scalar_tensor_tensor(
                        f2_a[:, dout * T:(dout + 1) * T], psb_[:], 0.5,
                        z_in[:, dout * T:(dout + 1) * T], ALU.mult, ALU.add),
                        reads=(pst, bz), writes=(tb("f2_a"),))
                    dma_out(_dap(x1s, dout * 128 * S + t0, [(S, 128), (1, T)]),
                            f2_a[:, dout * T:(dout + 1) * T], (tb("f2_a"),))
                ffn_tree(xtile, consume, tb("xn_b"))

                def srcf2(i):
                    return f2_a[:, i * T:(i + 1) * T]
                ln_stats(srcf2, ND_D, sqf, False, tb("f2_a"))
                ln_apply(srcf2, ND_D,
                         lambda i: (ln1_b[:, i * T:(i + 1) * T], tb("ln1_b")), tb("f2_a"))
                for i in range(ND_D):
                    dma_out(_dap(ln1s, i * 128 * S + t0, [(S, 128), (1, T)]),
                            ln1_b[:, i * T:(i + 1) * T], (tb("ln1_b"),))

                def srcf3(i):
                    return ln1_b[:, i * T:(i + 1) * T]
                ln_stats(srcf3, ND_D, sqf, True, tb("ln1_b"))
                ln_apply(srcf3, ND_D,
                         lambda i: (z_in[:, i * T:(i + 1) * T], tb("z_in")), tb("ln1_b"),
                         gain=lambda i: vap("inng", i), bias=lambda i: vap("innb", i))
                for i in range(ND_D):
                    dma_out(_dap(zA, i * 128 * S + t0, [(S, 128), (1, T)]),
                            z_in[:, i * T:(i + 1) * T], (tb("z_in"),))
            tr.barrier()

        nmamba = max(0, min(L, max_phase - 1))
        for l in range(nmamba):
            zsrc = zA if l % 2 == 0 else zB
            zdst = zB if l % 2 == 0 else zA
            tr.emit_group("sync", [lambda e, l=l: e.dma_start(
                out=vecs[:, 0:307], in_=E["layvecs"][l])], writes=(bv,), inc=16,
                sem_name=tr.lane("L_vecs"))
            xsrc = _dap(E["xprojT"], l * DI * 33,
                        [(33, 128), (128 * 33, KC_DI), (1, 33)])
            tr.emit_group("sync", [lambda e, xsrc=xsrc: e.dma_start(
                out=_ap(statA, 0, [(33, KC_DI), (1, 33)]), in_=xsrc)],
                writes=(bstat,), inc=16, sem_name=tr.lane("L_statA"))
            tr.emit("vector", lambda e: e.tensor_copy(
                xpw_b[:, 0:KC_DI * 33], statA[:, 0:KC_DI * 33]),
                reads=(bstat,), writes=(tb("xpw_b"),))
            tr.emit("gpsimd", lambda e: e.memset(states_sb[:], 0.0), writes=(tb("states"),))
            tr.emit("gpsimd", lambda e: e.memset(xtails[:], 0.0), writes=(tb("xtails"),))
            tr.emit("gpsimd", lambda e: e.memset(ctails[:], 0.0), writes=(tb("ctails"),))

            for ti in range(ntile):
                t0 = ti * T
                bz = tb("z_in")
                z_tile_load(z_in, bz, zsrc, t0, ND_D)

                def msrc(i):
                    return z_in[:, i * T:(i + 1) * T]

                def msq(i):
                    return sq_b[:, i * T:(i + 1) * T]
                ln_stats(msrc, ND_D, msq, False, bz)
                ln_apply(msrc, ND_D,
                         lambda i: (xn_b[:, i * T:(i + 1) * T], tb("xn_b")), bz,
                         gain=lambda i: vap("lng", i), bias=lambda i: vap("lnb", i))
                for dout in range(2 * ND_DI):
                    slot = dout % 2
                    pst, psb_ = PS[slot], pb[slot]
                    wtile, wb = stream_w(c_inproj, l * 2 * ND_DI + dout, KC_D, slot)
                    tr.emit_group("tensor", [
                        (lambda e, kc=kc, st=(kc == 0), sp=(kc == KC_D - 1), wtile=wtile, psb_=psb_:
                         e.matmul(psb_[:], wtile[:, kc * 128:(kc + 1) * 128],
                                  xn_b[:, kc * T:(kc + 1) * T], start=st, stop=sp))
                        for kc in range(KC_D)], reads=(wb, tb("xn_b")), writes=(pst,))
                    if dout >= ND_DI:
                        dd = dout - ND_DI
                        tr.emit("scalar", lambda e, dd=dd, psb_=psb_: e.activation(
                            sres_b[:, dd * T:(dd + 1) * T], psb_[:], AF.Sigmoid),
                            reads=(pst,), writes=(tb("sres_b"),))
                        continue
                    dd = dout
                    ib = slot * (T + 4)
                    tr.emit("gpsimd", lambda e, ib=ib, dd=dd: e.tensor_copy(
                        cin_b[:, ib + 1:ib + 4], xtails[:, dd * 4:dd * 4 + 3]),
                        reads=(tb("xtails"),), writes=(tb("cin_b"),))
                    tr.emit("scalar", lambda e, ib=ib, psb_=psb_: e.activation(
                        cin_b[:, ib + 4:ib + 4 + T], psb_[:], AF.Silu),
                        reads=(pst,), writes=(tb("cin_b"),))
                    tr.emit("gpsimd", lambda e, ib=ib, dd=dd: e.tensor_copy(
                        xtails[:, dd * 4:dd * 4 + 3], cin_b[:, ib + 1 + T:ib + 4 + T]),
                        reads=(tb("cin_b"),), writes=(tb("xtails"),))
                    tr.emit("vector", lambda e, ib=ib, dd=dd: e.tensor_scalar(
                        convacc[:], cin_b[:, ib + 1:ib + 1 + T], vap("cw00", dd), vap("cb0", dd), ALU.mult, ALU.add),
                        reads=(tb("cin_b"), bv), writes=(tb("convacc"),))
                    for k in range(1, 4):
                        tr.emit("vector", lambda e, ib=ib, dd=dd, k=k: e.scalar_tensor_tensor(
                            convacc[:], cin_b[:, ib + 1 + k:ib + 1 + k + T], vap(f"cw0{k}", dd),
                            convacc[:], ALU.mult, ALU.add),
                            reads=(tb("cin_b"), tb("convacc"), bv), writes=(tb("convacc"),))
                    tr.emit("gpsimd", lambda e, ib=ib, dd=dd: e.tensor_copy(
                        cmid_b[:, ib + 1:ib + 4], ctails[:, dd * 4:dd * 4 + 3]),
                        reads=(tb("ctails"),), writes=(tb("cmid_b"),))
                    tr.emit("scalar", lambda e, ib=ib: e.activation(
                        cmid_b[:, ib + 4:ib + 4 + T], convacc[:], AF.Silu),
                        reads=(tb("convacc"),), writes=(tb("cmid_b"),))
                    tr.emit("gpsimd", lambda e, ib=ib, dd=dd: e.tensor_copy(
                        ctails[:, dd * 4:dd * 4 + 3], cmid_b[:, ib + 1 + T:ib + 4 + T]),
                        reads=(tb("cmid_b"),), writes=(tb("ctails"),))
                    tr.emit("vector", lambda e, ib=ib, dd=dd: e.tensor_scalar(
                        convacc[:], cmid_b[:, ib + 1:ib + 1 + T], vap("cw10", dd), vap("cb1", dd), ALU.mult, ALU.add),
                        reads=(tb("cmid_b"), bv), writes=(tb("convacc"),))
                    for k in range(1, 4):
                        tr.emit("vector", lambda e, ib=ib, dd=dd, k=k: e.scalar_tensor_tensor(
                            convacc[:], cmid_b[:, ib + 1 + k:ib + 1 + k + T], vap(f"cw1{k}", dd),
                            convacc[:], ALU.mult, ALU.add),
                            reads=(tb("cmid_b"), tb("convacc"), bv), writes=(tb("convacc"),))
                    tr.emit("scalar", lambda e, dd=dd: e.activation(
                        xi2_b[:, dd * T:(dd + 1) * T], convacc[:], AF.Silu),
                        reads=(tb("convacc"),), writes=(tb("xi2_b"),))
                tr.emit_group("tensor", [
                    (lambda e, kc=kc, st=(kc == 0), sp=(kc == KC_DI - 1): e.matmul(
                        pb2[0:1, :], xpw_b[:, kc * 33:kc * 33 + 1],
                        xi2_b[:, kc * T:(kc + 1) * T], start=st, stop=sp))
                    for kc in range(KC_DI)], reads=(tb("xpw_b"), tb("xi2_b")), writes=(PS[2],))
                tr.emit_group("tensor", [
                    (lambda e, kc=kc, st=(kc == 0), sp=(kc == KC_DI - 1): e.matmul(
                        pb6[0:16, :], xpw_b[:, kc * 33 + 1:kc * 33 + 17],
                        xi2_b[:, kc * T:(kc + 1) * T], start=st, stop=sp))
                    for kc in range(KC_DI)], reads=(tb("xpw_b"), tb("xi2_b")), writes=(PS[6],))
                tr.emit_group("tensor", [
                    (lambda e, kc=kc, st=(kc == 0), sp=(kc == KC_DI - 1): e.matmul(
                        pb7[0:16, :], xpw_b[:, kc * 33 + 17:kc * 33 + 33],
                        xi2_b[:, kc * T:(kc + 1) * T], start=st, stop=sp))
                    for kc in range(KC_DI)], reads=(tb("xpw_b"), tb("xi2_b")), writes=(PS[7],))
                tr.emit("vector", lambda e: e.tensor_copy(projsb[0:1, :], pb2[0:1, :]),
                        reads=(PS[2],), writes=(tb("projsb"),))
                tr.emit("vector", lambda e: e.tensor_copy(bc_sb[0:16, :], pb6[0:16, :]),
                        reads=(PS[6],), writes=(tb("bc_sb"),))
                tr.emit("vector", lambda e: e.tensor_copy(bc_sb[32:48, :], pb7[0:16, :]),
                        reads=(PS[7],), writes=(tb("bc_sb"),))
                tr.emit("tensor", lambda e: e.matmul(pb3[:], ones_f[0:1, :], projsb[0:1, :], start=True, stop=True),
                        reads=(tb("projsb"), tb("ones_f")), writes=(PS[3],))
                for n in range(NS):
                    slot = 6 + n % 2
                    pst, psb_ = PS[slot], pb[slot]
                    tr.emit("tensor", lambda e, n=n, psb_=psb_: e.matmul(
                        psb_[:], bcsel_b[0:16, n * 128:(n + 1) * 128], bc_sb[0:16, :],
                        start=True, stop=True),
                        reads=(tb("bcsel_b"), tb("bc_sb")), writes=(pst,))
                    tr.emit("vector", lambda e, n=n, psb_=psb_: e.tensor_copy(
                        brep_b[:, n * T:(n + 1) * T], psb_[:]), reads=(pst,), writes=(tb("brep_b"),))
                for n in range(NS):
                    slot = 6 + n % 2
                    pst, psb_ = PS[slot], pb[slot]
                    tr.emit("tensor", lambda e, n=n, psb_=psb_: e.matmul(
                        psb_[:], bcsel_b[32:48, n * 128:(n + 1) * 128], bc_sb[32:48, :],
                        start=True, stop=True),
                        reads=(tb("bcsel_b"), tb("bc_sb")), writes=(pst,))
                    tr.emit("vector", lambda e, n=n, psb_=psb_: e.tensor_copy(
                        crep_b[:, n * T:(n + 1) * T], psb_[:]), reads=(pst,), writes=(tb("crep_b"),))
                def emit_dt(ds):
                    sl = ds % 2
                    tr.emit("scalar", lambda e, ds=ds, sl=sl: e.activation(
                        q1[:], pb3[:], AF.Exp, scale=vap("dtw", ds), bias=vap("dtb", ds)),
                        reads=(PS[3], bv), writes=(tb("q1"),))
                    tr.emit("scalar", lambda e, sl=sl: e.activation(
                        dtt[:, sl * T:(sl + 1) * T], q1[:], AF.Ln, bias=cst[:, 2:3]),
                        reads=(tb("q1"), tb("cst")), writes=(tb(f"dtt{sl}"),))
                    tr.emit("gpsimd", lambda e, ds=ds, sl=sl: e.tensor_tensor(
                        dtxb[:, sl * T:(sl + 1) * T], dtt[:, sl * T:(sl + 1) * T],
                        xi2_b[:, ds * T:(ds + 1) * T], ALU.mult),
                        reads=(tb(f"dtt{sl}"), tb("xi2_b")), writes=(tb(f"dtxb{sl}"),))
                emit_dt(0)
                for ds in range(ND_DI):
                    sl = ds % 2
                    if ds + 1 < ND_DI:
                        emit_dt(ds + 1)
                    dba = dtxb.ap()
                    dtx_bc = bass.AP(tensor=dba.tensor, offset=dba.offset + sl * T,
                                     ap=[[dba.ap[0][0], 128], [0, NS], [1, T]])
                    tr.emit("vector", lambda e, dtx_bc=dtx_bc: e.tensor_tensor(
                        ufull_b[:], dtx_bc, brep_b[:], ALU.mult),
                        reads=(tb(f"dtxb{sl}"), tb("brep_b")), writes=(tb("sq_b"),))
                    def emit_a(n, sl=sl):
                        asl = n % 3
                        if n < 8:
                            tr.emit("vector", lambda e, n=n, asl=asl, sl=sl: e.tensor_scalar(
                                abuf[:, asl * T:(asl + 1) * T], dtt[:, sl * T:(sl + 1) * T],
                                -float(n + 1), 1.0, ALU.mult, ALU.add),
                                reads=(tb(f"dtt{sl}"),), writes=(tb(f"abuf{asl}"),))
                        else:
                            tr.emit("scalar", lambda e, n=n, asl=asl, sl=sl: e.activation(
                                abuf[:, asl * T:(asl + 1) * T], dtt[:, sl * T:(sl + 1) * T],
                                AF.Identity, scale=cst[:, 4 + n - 8:5 + n - 8], bias=cst[:, 2:3]),
                                reads=(tb(f"dtt{sl}"), tb("cst")), writes=(tb(f"abuf{asl}"),))
                    emit_a(0)
                    for n in range(NS):
                        if n + 1 < NS:
                            emit_a(n + 1)
                        sl = n % 3
                        tr.emit("vector", lambda e, n=n, ds=ds, sl=sl: e.tensor_tensor_scan(
                            hbuf_b[:, n * T:(n + 1) * T], abuf[:, sl * T:(sl + 1) * T],
                            ufull_b[:, n * T:(n + 1) * T],
                            states_sb[:, ds * NS + n:ds * NS + n + 1], ALU.mult, ALU.add),
                            reads=(tb(f"abuf{sl}"), tb("sq_b"), tb("states")),
                            writes=(tb(f"hbuf{n % 2}"),))
                    tr.emit("vector", lambda e, ds=ds: e.tensor_copy(
                        states_sb[:, ds * NS:(ds + 1) * NS], _ap(hbuf_b, T - 1, [(T, NS)])),
                        reads=(tb("hbuf0"), tb("hbuf1")), writes=(tb("states"),))
                    tr.emit("vector", lambda e: e.tensor_tensor(
                        hbuf_b[:], hbuf_b[:], crep_b[:], ALU.mult),
                        reads=(tb("crep_b"),), writes=(tb("hbuf0"), tb("hbuf1")))
                    tr.emit("vector", lambda e: e.tensor_tensor(
                        ufull_b[:, 0:8 * T], hbuf_b[:, 0:8 * T], hbuf_b[:, 8 * T:16 * T], ALU.add),
                        reads=(tb("hbuf0"), tb("hbuf1")), writes=(tb("sq_b"),))
                    tr.emit("vector", lambda e: e.tensor_tensor(
                        ufull_b[:, 8 * T:12 * T], ufull_b[:, 0:4 * T], ufull_b[:, 4 * T:8 * T], ALU.add),
                        reads=(tb("sq_b"),), writes=(tb("sq_b"),))
                    tr.emit("vector", lambda e: e.tensor_tensor(
                        ufull_b[:, 12 * T:14 * T], ufull_b[:, 8 * T:10 * T], ufull_b[:, 10 * T:12 * T], ALU.add),
                        reads=(tb("sq_b"),), writes=(tb("sq_b"),))
                    tr.emit("vector", lambda e: e.tensor_tensor(
                        q2[:], ufull_b[:, 12 * T:13 * T], ufull_b[:, 13 * T:14 * T], ALU.add),
                        reads=(tb("sq_b"),), writes=(tb("q2"),))
                    tr.emit("vector", lambda e, ds=ds: e.scalar_tensor_tensor(
                        y2_b[:, ds * T:(ds + 1) * T], xi2_b[:, ds * T:(ds + 1) * T],
                        vap("Dv", ds), q2[:], ALU.mult, ALU.add),
                        reads=(tb("xi2_b"), tb("q2"), bv), writes=(tb("y2_b"),))
                def y2src(i):
                    return y2_b[:, i * T:(i + 1) * T]

                def y2sq(i):
                    return sq_b[:, i * T:(i + 1) * T]
                ln_stats(y2src, ND_DI, y2sq, True, tb("y2_b"))
                tr.emit("vector", lambda e: e.tensor_copy(srow_b[:], pb4[:]),
                        reads=(PS[4],), writes=(tb("srow_b"),))
                tr.emit("vector", lambda e: e.tensor_copy(mrow_b[:], pb5[:]),
                        reads=(PS[5],), writes=(tb("mrow_b"),))
                sba = srow_b.ap()
                s_bc = bass.AP(tensor=sba.tensor, offset=sba.offset,
                               ap=[[sba.ap[0][0], 128], [0, ND_DI], [1, T]])
                mba = mrow_b.ap()
                m_bc = bass.AP(tensor=mba.tensor, offset=mba.offset,
                               ap=[[mba.ap[0][0], 128], [0, ND_DI], [1, T]])
                tr.emit("vector", lambda e, s_bc=s_bc: e.tensor_tensor(
                    hbuf2_v[:], y2_b[:], s_bc, ALU.mult),
                    reads=(tb("y2_b"), tb("srow_b")), writes=(tb("hbuf0"), tb("hbuf1")))
                tr.emit("vector", lambda e, m_bc=m_bc: e.tensor_tensor(
                    hbuf2_v[:], hbuf2_v[:], m_bc, ALU.subtract),
                    reads=(tb("mrow_b"),), writes=(tb("hbuf0"), tb("hbuf1")))
                for i in range(ND_DI):
                    tr.emit("scalar", lambda e, i=i: e.activation(
                        q1[:], hbuf2_v[:, i * T:(i + 1) * T], AF.Identity,
                        scale=vap("mlng", i), bias=vap("mlnb", i)),
                        reads=(tb("hbuf0"), tb("hbuf1"), bv), writes=(tb("q1"),))
                    tr.emit("vector", lambda e, i=i: e.tensor_tensor(
                        brep_b[:, i * T:(i + 1) * T], q1[:], sres_b[:, i * T:(i + 1) * T], ALU.mult),
                        reads=(tb("q1"), tb("sres_b")), writes=(tb("brep_b"),))
                for dout in range(ND_D):
                    slot = dout % 2
                    pst, psb_ = PS[slot], pb[slot]
                    wtile, wb = stream_w(c_outproj, l * ND_D + dout, KC_DI, slot)
                    tr.emit_group("tensor", [
                        (lambda e, kc=kc, st=(kc == 0), sp=(kc == KC_DI - 1), wtile=wtile, psb_=psb_:
                         e.matmul(psb_[:], wtile[:, kc * 128:(kc + 1) * 128],
                                  brep_b[:, kc * T:(kc + 1) * T], start=st, stop=sp))
                        for kc in range(KC_DI)], reads=(wb, tb("brep_b")), writes=(pst,))
                    zsl = dout % 2
                    tr.emit("vector", lambda e, dout=dout, psb_=psb_, zsl=zsl: e.tensor_tensor(
                        zo_pp[:, zsl * T:(zsl + 1) * T], psb_[:],
                        z_in[:, dout * T:(dout + 1) * T], ALU.add),
                        reads=(pst, bz), writes=(tb("zo_pp"),))
                    dma_out(_dap(zdst, dout * 128 * S + t0, [(S, 128), (1, T)]),
                            zo_pp[:, zsl * T:(zsl + 1) * T], (tb("zo_pp"),))
            tr.barrier()

        if max_phase >= 6:
            ld_vec("b1", E["b1"][:], ND_D)
            ld_vec("b2", E["b2"][:], ND_D)
            ld_vec("bff", E["bff"][:], NT_FF)
            zsrc = zA if L % 2 == 0 else zB
            for ti in range(ntile):
                t0 = ti * T
                bz = tb("z_in")
                z_tile_load(z_in, bz, zsrc, t0, ND_D)
                bx1 = tb("f2_a")
                z_tile_load(f2_a, bx1, x1s, t0, ND_D)
                bl1 = tb("ln1_b")
                z_tile_load(ln1_b, bl1, ln1s, t0, ND_D)
                for i in range(ND_D):
                    tr.emit("vector", lambda e, i=i: e.tensor_copy(
                        xn_b[:, i * T:(i + 1) * T], z_in[:, i * T:(i + 1) * T]),
                        reads=(bz,), writes=(tb("xn_b"),))
                for dout in range(ND_D):
                    slot = dout % 2
                    pst, psb_ = PS[slot], pb[slot]
                    wtile, wb = stream_w(c_op1, dout, KC_D, slot)
                    tr.emit_group("tensor", [
                        (lambda e, kc=kc, st=(kc == 0), sp=(kc == KC_D - 1), wtile=wtile, psb_=psb_:
                         e.matmul(psb_[:], wtile[:, kc * 128:(kc + 1) * 128],
                                  xn_b[:, kc * T:(kc + 1) * T], start=st, stop=sp))
                        for kc in range(KC_D)], reads=(wb, tb("xn_b")), writes=(pst,))
                    tr.emit("scalar", lambda e, dout=dout, psb_=psb_: e.activation(
                        acts_b[:, dout * T:(dout + 1) * T], psb_[:], AF.Gelu, bias=vap("b1", dout)),
                        reads=(pst, bv), writes=(tb("acts_b"),))
                for dout in range(ND_D):
                    slot = dout % 2
                    pst, psb_ = PS[slot], pb[slot]
                    wtile, wb = stream_w(c_op2, dout, KC_D, slot)
                    tr.emit_group("tensor", [
                        (lambda e, kc=kc, st=(kc == 0), sp=(kc == KC_D - 1), wtile=wtile, psb_=psb_:
                         e.matmul(psb_[:], wtile[:, kc * 128:(kc + 1) * 128],
                                  acts_b[:, kc * T:(kc + 1) * T], start=st, stop=sp))
                        for kc in range(KC_D)], reads=(wb, tb("acts_b")), writes=(pst,))
                    tr.emit("scalar", lambda e, dout=dout, psb_=psb_: e.activation(
                        q1[:], psb_[:], AF.Identity, bias=vap("b2", dout)),
                        reads=(pst, bv), writes=(tb("q1"),))
                    tr.emit("vector", lambda e, dout=dout: e.tensor_tensor(
                        q2[:], q1[:], ln1_b[:, dout * T:(dout + 1) * T], ALU.add),
                        reads=(tb("q1"), bl1), writes=(tb("q2"),))
                    tr.emit("vector", lambda e, dout=dout: e.tensor_tensor(
                        f2_a[:, dout * T:(dout + 1) * T], q2[:],
                        f2_a[:, dout * T:(dout + 1) * T], ALU.add),
                        reads=(tb("q2"), bx1), writes=(bx1,))

                if dbg:
                    for i in range(ND_D):
                        dma_out(_dap(dbg1, i * 128 * S + t0, [(S, 128), (1, T)]),
                                f2_a[:, i * T:(i + 1) * T], (bx1,))
                        tr.emit("vector", lambda e, i=i: e.tensor_copy(
                            q1[:], acts_b[:, i * T:(i + 1) * T]),
                            reads=(tb("acts_b"),), writes=(tb("q1"),))
                        dma_out(_dap(dbg2, i * 128 * S + t0, [(S, 128), (1, T)]),
                                q1[:], (tb("q1"),))

                def srcf6(i):
                    return f2_a[:, i * T:(i + 1) * T]

                def sqf6(i):
                    return sq_b[:, i * T:(i + 1) * T]
                ln_stats(srcf6, ND_D, sqf6, False, bx1)
                ln_apply(srcf6, ND_D,
                         lambda i: (xn_b[:, i * T:(i + 1) * T], tb("xn_b")), bx1)

                def xtile6(kc):
                    return xn_b[:, kc * T:(kc + 1) * T]

                def consume6(dout, pst, psb_):
                    zsl = dout % 2
                    tr.emit("vector", lambda e, dout=dout, psb_=psb_, zsl=zsl: e.scalar_tensor_tensor(
                        zo_pp[:, zsl * T:(zsl + 1) * T], psb_[:], 0.5,
                        f2_a[:, dout * T:(dout + 1) * T], ALU.mult, ALU.add),
                        reads=(pst, bx1), writes=(tb("zo_pp"),))
                    dma_out(_dap(out_t, dout * 128 * S + t0, [(S, 128), (1, T)]),
                            zo_pp[:, zsl * T:(zsl + 1) * T], (tb("zo_pp"),))
                ffn_tree(xtile6, consume6, tb("xn_b"))
        tr.barrier()

        @block.sync
        def _(e):
            for a in tr.q["sync"]:
                a(e)

        @block.tensor
        def _(e):
            for a in tr.q["tensor"]:
                a(e)

        @block.vector
        def _(e):
            for a in tr.q["vector"]:
                a(e)

        @block.scalar
        def _(e):
            for a in tr.q["scalar"]:
                a(e)

        @block.gpsimd
        def _(e):
            for a in tr.q["gpsimd"]:
                a(e)

    return nc


def kernel(**inputs):
    g = _prep(inputs)
    hxs = g.pop("_hx_per_core")
    nc = build_program()
    in_maps = []
    for b in range(8):
        m = dict(g)
        m["hx"] = hxs[b]
        in_maps.append(m)
    res = run_bass_kernel_spmd(nc, in_maps, core_ids=list(range(8)))
    outs = [np.asarray(r["out"]).T for r in res.results]
    return np.ascontiguousarray(np.stack(outs, 0).astype(np.float32))


# version 13
# speedup vs baseline: 1.1610x; 1.0318x over previous
import numpy as np
import concourse.bass as bass
import concourse.mybir as mybir
from concourse.bass_utils import run_bass_kernel_spmd

F32 = mybir.dt.float32
BF16 = mybir.dt.bfloat16
AF = mybir.ActivationFunctionType
ALU = mybir.AluOpType
AX = mybir.AxisListType

D = 1024
DI = 2048
S = 8192
T = 512
NTILE = S // T
NS = 16
L = 4
KC_D = D // 128
KC_DI = DI // 128
ND_D = D // 128
ND_DI = DI // 128
EPS = 1e-5
DEPTH_FF = 7
PAR_FF = 8
NN_FF = 255
LOFF = [0, 1, 2, 3, 4, 5, 7, 11, 19]
NT_FF = 19
J2 = NT_FF * 128
SAME_ENGINE_EDGES = 2



def _vec2sb(v, ntiles):
    out = np.zeros(128 * ntiles, np.float32)
    out[:v.shape[0]] = v
    return np.ascontiguousarray(out.reshape(ntiles, 128).T)


def _ffn_perm():
    perm = np.full(J2, -1, np.int64)
    for d in range(DEPTH_FF + 1):
        sz = 1 << d
        base = 128 * LOFF[d]
        for p in range(PAR_FF):
            for i in range(sz):
                perm[base + p * sz + i] = p * NN_FF + (sz - 1 + i)
    return perm


def _tree_blocks():
    blocks = []
    for d in range(DEPTH_FF):
        psz = 1 << d
        csz = 1 << (d + 1)
        nchild_rows = PAR_FF * csz
        nct = (nchild_rows + 127) // 128
        for ctl in range(nct):
            ct = LOFF[d + 1] + ctl
            matL = np.zeros((128, 128), np.float32)
            matR = np.zeros((128, 128), np.float32)
            pt = None
            for cl in range(min(128, nchild_rows - ctl * 128)):
                crow = ctl * 128 + cl
                p = crow // csz
                ip = crow % csz
                prow = p * psz + (ip >> 1)
                side = ip & 1
                pt_this = LOFF[d] + prow // 128
                if pt is None:
                    pt = pt_this
                assert pt == pt_this
                (matR if side else matL)[prow % 128, cl] = 1.0
            blocks.append((d, pt, ct, 0, matL))
            blocks.append((d, pt, ct, 1, matR))
    return blocks


_TREE_BLOCKS = _tree_blocks()
_PERM = _ffn_perm()


def _prep(inputs):
    g = {}
    hs = np.asarray(inputs["L0_attn_hidden_states"], np.float32)
    g["_hx_per_core"] = [np.ascontiguousarray(hs[b].T) for b in range(8)]

    W_in = np.asarray(inputs["L0_ffn_W_in"], np.float32)
    b_in = np.asarray(inputs["L0_ffn_b_in"], np.float32)
    W_out = np.asarray(inputs["L0_ffn_W_out"], np.float32)
    WiT = np.zeros((D, J2), np.float32)
    WoT = np.zeros((J2, D), np.float32)
    bff = np.zeros(J2, np.float32)
    valid = _PERM >= 0
    WiT[:, valid] = W_in[_PERM[valid], :].T
    WoT[valid, :] = W_out[:, _PERM[valid]].T
    bff[valid] = b_in[_PERM[valid]]
    g["wffinT"] = np.ascontiguousarray(WiT)
    g["wffoutT"] = np.ascontiguousarray(WoT)
    g["bff"] = _vec2sb(bff, NT_FF)
    g["treeblk"] = np.ascontiguousarray(
        np.stack([m for (_, _, _, _, m) in _TREE_BLOCKS], 0))
    rootsel = np.zeros((1, 128), np.float32)
    rootsel[0, :PAR_FF] = 1.0
    g["rootsel"] = rootsel

    g["op1T"] = np.ascontiguousarray(np.asarray(inputs["L0_attn_op1_w"], np.float32).T)
    g["op2T"] = np.ascontiguousarray(np.asarray(inputs["L0_attn_op2_w"], np.float32).T)
    g["b1"] = _vec2sb(np.asarray(inputs["L0_attn_op1_b"], np.float32), ND_D)
    g["b2"] = _vec2sb(np.asarray(inputs["L0_attn_op2_b"], np.float32), ND_D)
    g["inng"] = _vec2sb(np.asarray(inputs["L0_attn_in_norm_g"], np.float32), ND_D)
    g["innb"] = _vec2sb(np.asarray(inputs["L0_attn_in_norm_b"], np.float32), ND_D)

    lng = np.asarray(inputs["L0_attn_ln_g"], np.float32)
    lnb = np.asarray(inputs["L0_attn_ln_b"], np.float32)
    g["lng"] = np.stack([_vec2sb(lng[l], ND_D) for l in range(L)])
    g["lnb"] = np.stack([_vec2sb(lnb[l], ND_D) for l in range(L)])
    g["inprojT"] = np.ascontiguousarray(
        np.asarray(inputs["L0_attn_in_proj_w"], np.float32).transpose(0, 2, 1))
    g["xprojT"] = np.ascontiguousarray(
        np.asarray(inputs["L0_attn_x_proj_w"], np.float32).transpose(0, 2, 1))
    g["outprojT"] = np.ascontiguousarray(
        np.asarray(inputs["L0_attn_out_proj_w"], np.float32).transpose(0, 2, 1))
    cw = np.asarray(inputs["L0_attn_conv_w"], np.float32)
    g["convw"] = np.ascontiguousarray(np.stack([
        np.stack([np.stack([_vec2sb(cw[l, c, :, k], ND_DI) for k in range(4)])
                  for c in range(2)]) for l in range(L)]))
    cb = np.asarray(inputs["L0_attn_conv_b"], np.float32)
    g["convb"] = np.ascontiguousarray(np.stack([
        np.stack([_vec2sb(cb[l, c], ND_DI) for c in range(2)]) for l in range(L)]))
    g["dtw"] = np.stack([_vec2sb(np.asarray(inputs["L0_attn_dt_proj_w"], np.float32)[l, :, 0], ND_DI) for l in range(L)])
    g["dtb"] = np.stack([_vec2sb(np.asarray(inputs["L0_attn_dt_proj_b"], np.float32)[l], ND_DI) for l in range(L)])
    g["Dv"] = np.stack([_vec2sb(np.asarray(inputs["L0_attn_D"], np.float32)[l], ND_DI) for l in range(L)])
    g["mlng"] = np.stack([_vec2sb(np.asarray(inputs["L0_attn_mamba_ln_g"], np.float32)[l], ND_DI) for l in range(L)])
    g["mlnb"] = np.stack([_vec2sb(np.asarray(inputs["L0_attn_mamba_ln_b"], np.float32)[l], ND_DI) for l in range(L)])
    g["onesm"] = np.ones((128, 128), np.float32)
    lv = np.zeros((4, 128, 307), np.float32)
    for l in range(4):
        cols = []
        cols.append(_vec2sb(lng[l], ND_D)); cols.append(_vec2sb(lnb[l], ND_D))
        cols.append(_vec2sb(np.asarray(inputs["L0_attn_in_norm_g"], np.float32), ND_D))
        cols.append(_vec2sb(np.asarray(inputs["L0_attn_in_norm_b"], np.float32), ND_D))
        for c in range(2):
            for k in range(4):
                cols.append(_vec2sb(cw[l, c, :, k], ND_DI))
        for c in range(2):
            cols.append(_vec2sb(cb[l, c], ND_DI))
        cols.append(_vec2sb(np.asarray(inputs["L0_attn_dt_proj_w"], np.float32)[l, :, 0], ND_DI))
        cols.append(_vec2sb(np.asarray(inputs["L0_attn_dt_proj_b"], np.float32)[l], ND_DI))
        cols.append(_vec2sb(np.asarray(inputs["L0_attn_D"], np.float32)[l], ND_DI))
        cols.append(_vec2sb(np.asarray(inputs["L0_attn_mamba_ln_g"], np.float32)[l], ND_DI))
        cols.append(_vec2sb(np.asarray(inputs["L0_attn_mamba_ln_b"], np.float32)[l], ND_DI))
        cols.append(_vec2sb(bff, NT_FF))
        cols.append(_vec2sb(np.asarray(inputs["L0_attn_op1_b"], np.float32), ND_D))
        cols.append(_vec2sb(np.asarray(inputs["L0_attn_op2_b"], np.float32), ND_D))
        lv[l] = np.concatenate(cols, axis=1)
    g["layvecs"] = np.ascontiguousarray(lv)
    cd = np.zeros((L, 16, 128, 8, 128), np.float32)
    for l in range(L):
        for c in range(2):
            for k in range(4):
                wv = cw[l, c, :, k].reshape(16, 128)
                for dt_ in range(16):
                    np.fill_diagonal(cd[l, dt_, :, c * 4 + k, :], wv[dt_])
    g["convdiag"] = np.ascontiguousarray(cd.reshape(L, 2048, 1024))
    sel = np.zeros((48, 16 * 128), np.float32)
    for n in range(16):
        sel[n, n * 128:(n + 1) * 128] = 1.0
        sel[32 + n, n * 128:(n + 1) * 128] = 1.0
    g["bcsel"] = sel
    return g



class Buf:
    __slots__ = ("name", "writer", "readers")

    def __init__(self, name):
        self.name = name
        self.writer = None
        self.readers = []


class Tracker:
    ENGINES = ("sync", "tensor", "vector", "scalar", "gpsimd")

    def __init__(self, sems, lane_pool):
        self.sems = dict(sems)
        self.lane_pool = list(lane_pool)
        self.cnt = {e: 0 for e in self.ENGINES}
        self.q = {e: [] for e in self.ENGINES}
        self.trace = []
        self.waited = {e: {} for e in self.ENGINES}

    def lane(self, name):
        if name not in self.sems:
            assert self.lane_pool, "out of DMA lane semaphores"
            self.sems[name] = self.lane_pool.pop()
            self.cnt[name] = 0
        return name

    def _waits_for(self, eng, reads, writes):
        need = {}

        def add(dep):
            if dep is None:
                return
            wsn, wcnt = dep
            if wsn == eng:
                if eng == "tensor":
                    return
                if SAME_ENGINE_EDGES is False:
                    return
                if SAME_ENGINE_EDGES is not True and                         self.cnt[eng] - wcnt >= SAME_ENGINE_EDGES:
                    return
            if need.get(wsn, 0) < wcnt:
                need[wsn] = wcnt
        for b in reads:
            add(b.writer)
        for b in writes:
            add(b.writer)
            for r in b.readers:
                add(r)
        return need

    def _push(self, eng, need, fns, inc, sn):
        sem = self.sems

        def action(e, need=need, fns=fns, inc=inc, sn=sn):
            for weng in sorted(need):
                e.wait_ge(sem[weng], need[weng])
            last = None
            for f in fns:
                last = f(e)
            last.then_inc(sem[sn], inc)
        self.q[eng].append(action)

    def emit(self, eng, fn, reads=(), writes=(), inc=1):
        return self.emit_group(eng, [fn], reads=reads, writes=writes, inc=inc)

    def emit_group(self, eng, fns, reads=(), writes=(), inc=1, sem_name=None):
        sn = sem_name or eng
        need = self._waits_for(eng, reads, writes)
        w = self.waited[eng]
        need = {k: v for k, v in need.items() if v > w.get(k, 0)}
        for k, v in need.items():
            w[k] = v
        self.cnt[sn] += inc
        self.trace.append((eng, sn, self.cnt[sn], tuple(b.name for b in reads),
                           tuple(b.name for b in writes), dict(need), len(fns)))
        self._push(eng, need, fns, inc, sn)
        dep = (sn, self.cnt[sn])
        for b in writes:
            b.writer = dep
            b.readers = []
        for b in reads:
            if b not in writes:
                b.readers.append(dep)
        return dep

    def barrier(self):
        snap = dict(self.cnt)
        sem = self.sems
        for eng in self.ENGINES:
            w = self.waited[eng]
            todo = {osn: v for osn, v in snap.items()
                    if osn != eng and v > 0 and v > w.get(osn, 0)}
            for osn, v in todo.items():
                w[osn] = v
            def action(e, todo=todo, eng=eng):
                for osn, v in todo.items():
                    e.wait_ge(sem[osn], v)
            self.q[eng].append(action)


def _ap(handle, offset, dims, npart=128):
    base = handle.ap()
    return bass.AP(tensor=base.tensor, offset=base.offset + offset,
                   ap=[[base.ap[0][0], npart]] + [[s, c] for (s, c) in dims])


def _dap(dram_ap, offset, dims):
    return bass.AP(tensor=dram_ap.tensor, offset=dram_ap.offset + offset,
                   ap=[[s, c] for (s, c) in dims])



def build_program(max_phase=99, dbg=False, ntile=NTILE):
    nc = bass.Bass("TRN2", target_bir_lowering=False)
    E = {}

    def ein(name, shape):
        E[name] = nc.dram_tensor(name, list(shape), F32, kind="ExternalInput").ap()

    ein("hx", (D, S))
    ein("wffinT", (D, J2)); ein("wffoutT", (J2, D)); ein("bff", (128, NT_FF))
    ein("treeblk", (len(_TREE_BLOCKS), 128, 128)); ein("rootsel", (1, 128))
    ein("op1T", (D, D)); ein("op2T", (D, D)); ein("b1", (128, ND_D)); ein("b2", (128, ND_D))
    ein("inng", (128, ND_D)); ein("innb", (128, ND_D))
    ein("lng", (L, 128, ND_D)); ein("lnb", (L, 128, ND_D))
    ein("inprojT", (L, D, 2 * DI)); ein("xprojT", (L, DI, 33)); ein("outprojT", (L, DI, D))
    ein("convw", (L, 2, 4, 128, ND_DI)); ein("convb", (L, 2, 128, ND_DI))
    ein("dtw", (L, 128, ND_DI)); ein("dtb", (L, 128, ND_DI)); ein("Dv", (L, 128, ND_DI))
    ein("mlng", (L, 128, ND_DI)); ein("mlnb", (L, 128, ND_DI))
    ein("onesm", (128, 128))
    ein("layvecs", (L, 128, 307))
    ein("convdiag", (L, 2048, 1024))
    ein("bcsel", (48, 16 * 128))

    out_t = nc.dram_tensor("out", [D, S], F32, kind="ExternalOutput").ap()
    KD = "ExternalOutput" if dbg else "Internal"
    if dbg:
        dbg1 = nc.dram_tensor("dbg1", [D, S], F32, kind="ExternalOutput").ap()
        dbg2 = nc.dram_tensor("dbg2", [D, S], F32, kind="ExternalOutput").ap()
    zA = nc.dram_tensor("zA", [D, S], F32, kind=KD).ap()
    zB = nc.dram_tensor("zB", [D, S], F32, kind=KD).ap()
    x1s = nc.dram_tensor("x1s", [D, S], F32, kind=KD).ap()
    ln1s = nc.dram_tensor("ln1s", [D, S], BF16, kind=KD).ap()
    c_inproj = nc.dram_tensor("c_inproj", [L, 2 * ND_DI, 128, KC_D * 128], BF16, kind="Internal").ap()
    c_outproj = nc.dram_tensor("c_outproj", [L, ND_D, 128, KC_DI * 128], BF16, kind="Internal").ap()
    c_wffin = nc.dram_tensor("c_wffin", [NT_FF, 128, KC_D * 128], BF16, kind="Internal").ap()
    c_wffout = nc.dram_tensor("c_wffout", [ND_D, 128, NT_FF * 128], BF16, kind="Internal").ap()
    c_op1 = nc.dram_tensor("c_op1", [ND_D, 128, KC_D * 128], BF16, kind="Internal").ap()
    c_op2 = nc.dram_tensor("c_op2", [ND_D, 128, KC_D * 128], BF16, kind="Internal").ap()
    c_convd = nc.dram_tensor("c_convd", [L, ND_DI, 128, 1024], BF16, kind="Internal").ap()

    NBLK = len(_TREE_BLOCKS)
    import contextlib
    es = contextlib.ExitStack()
    with es:
        ones_f = es.enter_context(nc.sbuf_tensor("ones_f", [128, 128], F32))
        ones_b = es.enter_context(nc.sbuf_tensor("ones_b", [128, 128], BF16))
        treesb = es.enter_context(nc.sbuf_tensor("treesb", [128, NBLK * 128], BF16))
        rootsb = es.enter_context(nc.sbuf_tensor("rootsb", [1, 128], BF16))
        onerow_b = es.enter_context(nc.sbuf_tensor("onerow_b", [1, T], BF16))
        wstg = es.enter_context(nc.sbuf_tensor("wstg", [128, 384], F32))
        wstgb = es.enter_context(nc.sbuf_tensor("wstgb", [128, 384], BF16))
        xpw_b = es.enter_context(nc.sbuf_tensor("xpw_b", [128, KC_DI * 33], BF16))
        wbufA = es.enter_context(nc.sbuf_tensor("wbufA", [128, NT_FF * 128], BF16))
        wbufB = es.enter_context(nc.sbuf_tensor("wbufB", [128, NT_FF * 128], BF16))
        vecs = es.enter_context(nc.sbuf_tensor("vecs", [128, 320], F32))
        z_in = es.enter_context(nc.sbuf_tensor("z_in", [128, ND_D * T], F32))
        zo_pp = es.enter_context(nc.sbuf_tensor("zo_pp", [128, 2 * T], F32))
        xn_b = es.enter_context(nc.sbuf_tensor("xn_b", [128, ND_D * T], BF16))
        cin_b = es.enter_context(nc.sbuf_tensor("cin_b", [128, 2 * (T + 4)], BF16))
        xtails = es.enter_context(nc.sbuf_tensor("xtails", [128, ND_DI * 4], BF16))
        cmid_b = es.enter_context(nc.sbuf_tensor("cmid_b", [128, 2 * (T + 4)], BF16))
        ctails = es.enter_context(nc.sbuf_tensor("ctails", [128, ND_DI * 4], BF16))
        arena = es.enter_context(nc.sbuf_tensor("arena", [128, 49152], BF16))
        projsb = es.enter_context(nc.sbuf_tensor("projsb", [33, T], F32))
        bc_sb = es.enter_context(nc.sbuf_tensor("bc_sb", [48, T], BF16))
        bcsel_b = es.enter_context(nc.sbuf_tensor("bcsel_b", [48, 16 * 128], BF16))
        dtt = es.enter_context(nc.sbuf_tensor("dtt", [128, 2 * T], F32))
        dtxb = es.enter_context(nc.sbuf_tensor("dtxb", [128, 2 * T], BF16))
        wcv = es.enter_context(nc.sbuf_tensor("wcv", [128, 2 * 1024], BF16))
        abuf = es.enter_context(nc.sbuf_tensor("abuf", [128, 3 * T], BF16))

        q1 = es.enter_context(nc.sbuf_tensor("q1", [128, T], F32))
        q2 = es.enter_context(nc.sbuf_tensor("q2", [128, T], F32))
        states_sb = es.enter_context(nc.sbuf_tensor("states", [128, ND_DI * NS], F32))
        sq_b = es.enter_context(nc.sbuf_tensor("sq_b", [128, ND_DI * T], BF16))
        statA = es.enter_context(nc.sbuf_tensor("statA", [128, 3 * T], F32))
        srow_b = es.enter_context(nc.sbuf_tensor("srow_b", [128, T], BF16))
        mrow_b = es.enter_context(nc.sbuf_tensor("mrow_b", [128, T], BF16))
        cst = es.enter_context(nc.sbuf_tensor("cst", [128, 12], F32))

        pb0 = es.enter_context(nc.psum_tensor("pb0", [128, T], F32))
        pb1 = es.enter_context(nc.psum_tensor("pb1", [128, T], F32))
        pb2 = es.enter_context(nc.psum_tensor("pb2", [128, T], F32))
        pb3 = es.enter_context(nc.psum_tensor("pb3", [128, T], F32))
        pb4 = es.enter_context(nc.psum_tensor("pb4", [128, T], F32))
        pb5 = es.enter_context(nc.psum_tensor("pb5", [128, T], F32))
        pb6 = es.enter_context(nc.psum_tensor("pb6", [128, T], F32))
        pb7 = es.enter_context(nc.psum_tensor("pb7", [128, T], F32))
        s_sync = es.enter_context(nc.semaphore("s_sync"))
        lane_sems = [es.enter_context(nc.semaphore(f"lane{i}")) for i in range(24)]
        s_tensor = es.enter_context(nc.semaphore("s_tensor"))
        s_vector = es.enter_context(nc.semaphore("s_vector"))
        s_scalar = es.enter_context(nc.semaphore("s_scalar"))
        s_gpsimd = es.enter_context(nc.semaphore("s_gpsimd"))
        block = es.enter_context(nc.Block())
        pb = [pb0, pb1, pb2, pb3, pb4, pb5, pb6, pb7]

        class TView:

            def __init__(self, h, off, nels):
                self.h, self.off, self.nels = h, off, nels

            def ap(self):
                b = self.h.ap()
                return bass.AP(tensor=b.tensor, offset=b.offset + self.off,
                               ap=[[b.ap[0][0], 128], [1, self.nels]])

            def __getitem__(self, key):
                base = self.ap()
                if isinstance(key, slice):
                    return base
                ps, fs = key
                assert (ps.start or 0) == 0
                f0 = fs.start or 0
                f1 = fs.stop if fs.stop is not None else self.nels
                return bass.AP(tensor=base.tensor, offset=base.offset + f0,
                               ap=[[base.ap[0][0], 128], [1, f1 - f0]])

        arena_f = arena.bitcast(F32)
        ufull_b = TView(sq_b, 0, NS * T)
        hbuf2_v = TView(arena, 24576, ND_DI * T)
        xi2_b = TView(arena, 0, ND_DI * T)
        sres_b = TView(arena, 8192, ND_DI * T)
        y2_b = TView(arena, 16384, ND_DI * T)
        hbuf_b = TView(arena, 24576, NS * T)
        brep_b = TView(arena, 32768, NS * T)
        crep_b = TView(arena, 40960, NS * T)
        acts_b = TView(arena, 0, NT_FF * T)
        dec_b = TView(arena, 9728, NT_FF * T)
        dmap_b = TView(arena, 19456, NT_FF * T)
        oLR_b = TView(arena, 29184, 8 * T)
        ln1_b = TView(arena, 33280, ND_D * T)
        f2_a = TView(arena_f, 18688, ND_D * T)
        tr = Tracker({"sync": s_sync, "tensor": s_tensor, "vector": s_vector,
                      "scalar": s_scalar, "gpsimd": s_gpsimd}, lane_sems)
        build_program.last_trace = tr.trace
        B = {}

        def tb(name):
            if name not in B:
                B[name] = Buf(name)
            return B[name]

        PS = [tb(f"ps{i}") for i in range(8)]
        bv = tb("vecs")

        VEC = {}
        _vc = 0
        for nm, n in [("lng", ND_D), ("lnb", ND_D), ("inng", ND_D), ("innb", ND_D),
                      ("cw00", ND_DI), ("cw01", ND_DI), ("cw02", ND_DI), ("cw03", ND_DI),
                      ("cw10", ND_DI), ("cw11", ND_DI), ("cw12", ND_DI), ("cw13", ND_DI),
                      ("cb0", ND_DI), ("cb1", ND_DI),
                      ("dtw", ND_DI), ("dtb", ND_DI), ("Dv", ND_DI),
                      ("mlng", ND_DI), ("mlnb", ND_DI),
                      ("bff", NT_FF), ("b1", ND_D), ("b2", ND_D)]:
            VEC[nm] = _vc
            _vc += n
        assert _vc <= 320

        def vap(name, i):
            c = VEC[name] + i
            return vecs[:, c:c + 1]

        def ld_vec(dst, src_ap, n):
            c = VEC[dst]
            tr.emit_group("sync", [lambda e, c=c, s=src_ap, n=n: e.dma_start(
                out=vecs[:, c:c + n], in_=s)], writes=(bv,), inc=16,
                sem_name=tr.lane("L_vecs"))

        def dma_in(dst_ap, src_ap, wb, rb=()):
            ln_ = tr.lane("L_" + wb[0].name)
            tr.emit_group("sync", [lambda e, d=dst_ap, s=src_ap: e.dma_start(out=d, in_=s)],
                          reads=rb, writes=wb, inc=16, sem_name=ln_)

        def dma_out(dst_ap, src_ap, rb, wb=()):
            ln_ = tr.lane("S_" + rb[0].name)
            tr.emit_group("gpsimd", [lambda e, d=dst_ap, s=src_ap: e.dma_start(out=d, in_=s)],
                          reads=rb, writes=wb, inc=16, sem_name=ln_)

        dma_in(ones_f[:], E["onesm"][:], (tb("ones_f"),))
        tr.emit("vector", lambda e: e.tensor_copy(ones_b[:], ones_f[:]),
                reads=(tb("ones_f"),), writes=(tb("ones_b"),))
        tr.emit("gpsimd", lambda e: e.memset(onerow_b[:], 1.0), writes=(tb("onerow_b"),))
        tr.emit("gpsimd", lambda e: e.memset(cst[:, 0:1], EPS), writes=(tb("cst"),))
        tr.emit("gpsimd", lambda e: e.memset(cst[:, 1:2], -0.5), writes=(tb("cst"),))
        tr.emit("gpsimd", lambda e: e.memset(cst[:, 2:3], 1.0), writes=(tb("cst"),))
        for _n in range(8, 16):
            tr.emit("gpsimd", lambda e, _n=_n: e.memset(cst[:, 4 + _n - 8:5 + _n - 8], -float(_n + 1)),
                    writes=(tb("cst"),))

        def cache_build(src2d, rows, cols, dst_ap_fn, wname):
            CC = 384
            for ch in range(rows // 128):
                for c0 in range(0, cols, CC):
                    cw_ = min(CC, cols - c0)
                    dma_in(wstg[:, 0:cw_], src2d[ch * 128:(ch + 1) * 128, c0:c0 + cw_],
                           (tb("wstg"),))
                    tr.emit("vector", lambda e, c=cw_: e.tensor_copy(wstgb[:, 0:c], wstg[:, 0:c]),
                            reads=(tb("wstg"),), writes=(tb("wstgb"),))
                    dma_out(dst_ap_fn(ch, c0, cw_), wstgb[:, 0:cw_], (tb("wstgb"),))

        for l in range(L):
            def dst_ip(kc, c0, cw_, l=l):
                return _dap(c_inproj,
                            l * (2 * ND_DI * 128 * KC_D * 128)
                            + (c0 // 128) * (128 * KC_D * 128) + kc * 128,
                            [(KC_D * 128, 128), (128 * KC_D * 128, cw_ // 128), (1, 128)])
            cache_build(E["inprojT"][l], D, 2 * DI, dst_ip, "ip")

            def dst_op(kc, c0, cw_, l=l):
                return _dap(c_outproj,
                            l * (ND_D * 128 * KC_DI * 128)
                            + (c0 // 128) * (128 * KC_DI * 128) + kc * 128,
                            [(KC_DI * 128, 128), (128 * KC_DI * 128, cw_ // 128), (1, 128)])
            cache_build(E["outprojT"][l], DI, D, dst_op, "op")

        for l in range(L):
            def dst_cd(ch, c0, cw_, l=l):
                return _dap(c_convd, l * (ND_DI * 128 * 1024) + ch * (128 * 1024) + c0,
                            [(1024, 128), (1, cw_)])
            cache_build(E["convdiag"][l], DI, 1024, dst_cd, "cd")
        cache_build(E["wffinT"], D, J2,
                    lambda kc, c0, cw_: _dap(c_wffin, (c0 // 128) * (128 * KC_D * 128) + kc * 128,
                                             [(KC_D * 128, 128), (128 * KC_D * 128, cw_ // 128), (1, 128)]), "fi")
        cache_build(E["wffoutT"], J2, D,
                    lambda kc, c0, cw_: _dap(c_wffout, (c0 // 128) * (128 * NT_FF * 128) + kc * 128,
                                             [(NT_FF * 128, 128), (128 * NT_FF * 128, cw_ // 128), (1, 128)]), "fo")
        cache_build(E["op1T"], D, D,
                    lambda kc, c0, cw_: _dap(c_op1, (c0 // 128) * (128 * KC_D * 128) + kc * 128,
                                             [(KC_D * 128, 128), (128 * KC_D * 128, cw_ // 128), (1, 128)]), "o1")
        cache_build(E["op2T"], D, D,
                    lambda kc, c0, cw_: _dap(c_op2, (c0 // 128) * (128 * KC_D * 128) + kc * 128,
                                             [(KC_D * 128, 128), (128 * KC_D * 128, cw_ // 128), (1, 128)]), "o2")

        for b0 in range(0, NBLK, 1):
            nb = 1
            src = _dap(E["treeblk"], b0 * 128 * 128,
                       [(128, 128), (128 * 128, nb), (1, 128)])
            dma_in(wstg[:, 0:nb * 128], src, (tb("wstg"),))
            tr.emit("vector", lambda e, b0=b0, nb=nb: e.tensor_copy(
                treesb[:, b0 * 128:(b0 + nb) * 128], wstg[:, 0:nb * 128]),
                reads=(tb("wstg"),), writes=(tb("treesb"),))
        for c0 in range(0, 2048, 128):
            dma_in(wstg[0:48, 0:128], E["bcsel"][:, c0:c0 + 128], (tb("wstg"),))
            tr.emit("vector", lambda e, c0=c0: e.tensor_copy(
                bcsel_b[0:48, c0:c0 + 128], wstg[0:48, 0:128]),
                reads=(tb("wstg"),), writes=(tb("bcsel_b"),))
        dma_in(wstg[0:1, 0:128], E["rootsel"][:], (tb("wstg"),))
        tr.emit("vector", lambda e: e.tensor_copy(rootsb[:], wstg[0:1, 0:128]),
                reads=(tb("wstg"),), writes=(tb("rootsb"),))
        tr.barrier()

        bstat = tb("stat")

        def ln_stats(src_tiles, nt, sq_view, src_is_bf, src_buf):
            Dtot = float(nt * 128)
            for i in range(nt):
                tr.emit("scalar", lambda e, i=i: e.activation(sq_view(i), src_tiles(i), AF.Square),
                        reads=(src_buf,), writes=(tb("sq_b"),))
            oc = ones_b[:, 0:1] if src_is_bf else ones_f[:, 0:1]
            ob = tb("ones_b") if src_is_bf else tb("ones_f")
            tr.emit_group("tensor", [
                (lambda e, i=i, st=(i == 0), sp=(i == nt - 1): e.matmul(
                    pb2[0:1, :], oc, src_tiles(i), start=st, stop=sp))
                for i in range(nt)], reads=(src_buf, ob), writes=(PS[2],))
            tr.emit_group("tensor", [
                (lambda e, i=i, st=(i == 0), sp=(i == nt - 1): e.matmul(
                    pb3[0:1, :], ones_b[:, 0:1], sq_view(i), start=st, stop=sp))
                for i in range(nt)], reads=(tb("sq_b"), tb("ones_b")), writes=(PS[3],))
            tr.emit("vector", lambda e: e.tensor_scalar(statA[0:1, 0:T], pb2[0:1, :], 1.0 / Dtot, None, ALU.mult),
                    reads=(PS[2],), writes=(bstat,))
            tr.emit("vector", lambda e: e.tensor_scalar(statA[32:33, 0:T], pb3[0:1, :], 1.0 / Dtot, None, ALU.mult),
                    reads=(PS[3],), writes=(bstat,))
            tr.emit("vector", lambda e: e.tensor_tensor(statA[32:33, T:2 * T], statA[0:1, 0:T], statA[0:1, 0:T], ALU.mult),
                    reads=(bstat,), writes=(bstat,))
            tr.emit("vector", lambda e: e.tensor_tensor(statA[64:65, 0:T], statA[32:33, 0:T], statA[32:33, T:2 * T], ALU.subtract),
                    reads=(bstat,), writes=(bstat,))
            tr.emit("scalar", lambda e: e.activation(statA[64:65, T:2 * T], statA[64:65, 0:T], AF.Ln, bias=cst[64:65, 0:1]),
                    reads=(bstat, tb("cst")), writes=(bstat,))
            tr.emit("scalar", lambda e: e.activation(statA[0:1, T:2 * T], statA[64:65, T:2 * T], AF.Exp, scale=cst[64:65, 1:2]),
                    reads=(bstat, tb("cst")), writes=(bstat,))
            tr.emit("vector", lambda e: e.tensor_tensor(statA[0:1, 2 * T:3 * T], statA[0:1, 0:T], statA[0:1, T:2 * T], ALU.mult),
                    reads=(bstat,), writes=(bstat,))
            tr.emit("tensor", lambda e: e.matmul(pb4[:], ones_f[0:1, :], statA[0:1, T:2 * T], start=True, stop=True),
                    reads=(bstat, tb("ones_f")), writes=(PS[4],))
            tr.emit("tensor", lambda e: e.matmul(pb5[:], ones_f[0:1, :], statA[0:1, 2 * T:3 * T], start=True, stop=True),
                    reads=(bstat, tb("ones_f")), writes=(PS[5],))

        def ln_apply(src_tiles, nt, out_fn, src_buf, gain=None, bias=None):
            for i in range(nt):
                tr.emit("vector", lambda e, i=i: e.tensor_tensor(q1[:], src_tiles(i), pb4[:], ALU.mult),
                        reads=(src_buf, PS[4]), writes=(tb("q1"),))
                if gain is None:
                    oap, ob = out_fn(i)
                    tr.emit("vector", lambda e, oap=oap: e.tensor_tensor(oap, q1[:], pb5[:], ALU.subtract),
                            reads=(tb("q1"), PS[5]), writes=(ob,))
                else:
                    tr.emit("vector", lambda e: e.tensor_tensor(q2[:], q1[:], pb5[:], ALU.subtract),
                            reads=(tb("q1"), PS[5]), writes=(tb("q2"),))
                    oap, ob = out_fn(i)
                    tr.emit("scalar", lambda e, oap=oap, g=gain(i), bb=bias(i): e.activation(
                        oap, q2[:], AF.Identity, scale=g, bias=bb),
                        reads=(tb("q2"), bv), writes=(ob,))

        def stream_w(cache_ap, idx, ncols, slot):
            wtile = wbufA if slot == 0 else wbufB
            wb = tb("wbufA") if slot == 0 else tb("wbufB")
            src = _dap(cache_ap, idx * 128 * ncols * 128, [(ncols * 128, 128), (1, ncols * 128)])
            dma_in(_ap(wtile, 0, [(1, ncols * 128)]), src, (wb,))
            return wtile, wb

        def ffn_tree(xtile, consume, xnb_buf):
            blocks_by_ct = {}
            for bi, (d, pt, ct, side, _) in enumerate(_TREE_BLOCKS):
                blocks_by_ct.setdefault((d, ct), []).append((bi, pt, side))
            for j in range(NT_FF):
                slot = j % 2
                pst, psb_ = (PS[slot], pb[slot])
                wtile, wb = stream_w(c_wffin, j, KC_D, slot)
                tr.emit_group("tensor", [
                    (lambda e, kc=kc, st=(kc == 0), sp=(kc == KC_D - 1), wtile=wtile, psb_=psb_:
                     e.matmul(psb_[:], wtile[:, kc * 128:(kc + 1) * 128], xtile(kc), start=st, stop=sp))
                    for kc in range(KC_D)], reads=(wb, xnb_buf), writes=(pst,))
                tr.emit("scalar", lambda e, j=j, psb_=psb_: e.activation(
                    acts_b[:, j * T:(j + 1) * T], psb_[:], AF.Silu,
                    bias=vecs[:, VEC["bff"] + j:VEC["bff"] + j + 1]),
                    reads=(bv,), writes=(tb("acts_b"), pst))
                tr.emit("vector", lambda e, j=j, psb_=psb_: e.tensor_scalar(
                    dec_b[:, j * T:(j + 1) * T], psb_[:],
                    vecs[:, VEC["bff"] + j:VEC["bff"] + j + 1], 0.0, ALU.add, ALU.is_gt),
                    reads=(bv,), writes=(tb("dec_b"), pst))
            tr.emit("tensor", lambda e: e.matmul(pb2[:], rootsb[0:1, :], onerow_b[0:1, :], start=True, stop=True),
                    reads=(tb("rootsb"), tb("onerow_b")), writes=(PS[2],))
            tr.emit("vector", lambda e: e.tensor_copy(dmap_b[:, 0:T], pb2[:]),
                    reads=(PS[2],), writes=(tb("dmap_b"),))
            for d in range(DEPTH_FF):
                pts = sorted({pt for (dd, pt, ct, s2, m) in _TREE_BLOCKS if dd == d})
                pt_slot = {pt: k for k, pt in enumerate(pts)}
                for pt, k in pt_slot.items():
                    tr.emit("vector", lambda e, pt=pt, k=k: e.tensor_tensor(
                        oLR_b[:, (2 * k + 1) * T:(2 * k + 2) * T],
                        dmap_b[:, pt * T:(pt + 1) * T], dec_b[:, pt * T:(pt + 1) * T], ALU.mult),
                        reads=(tb("dmap_b"), tb("dec_b")), writes=(tb("oLR_b"),))
                    tr.emit("vector", lambda e, pt=pt, k=k: e.tensor_tensor(
                        oLR_b[:, (2 * k) * T:(2 * k + 1) * T],
                        dmap_b[:, pt * T:(pt + 1) * T], oLR_b[:, (2 * k + 1) * T:(2 * k + 2) * T], ALU.subtract),
                        reads=(tb("dmap_b"),), writes=(tb("oLR_b"),))
                cts = sorted({ct for (dd, pt, ct, s2, m) in _TREE_BLOCKS if dd == d})
                for ci, ct in enumerate(cts):
                    blks = blocks_by_ct[(d, ct)]
                    slot = 2 + ci % 2
                    pst, psb_ = PS[slot], pb[slot]
                    tr.emit_group("tensor", [
                        (lambda e, bi=bi, sl=2 * pt_slot[pt] + side, st=(ii == 0), sp=(ii == len(blks) - 1), psb_=psb_:
                         e.matmul(psb_[:], treesb[:, bi * 128:(bi + 1) * 128],
                                  oLR_b[:, sl * T:(sl + 1) * T], start=st, stop=sp))
                        for ii, (bi, pt, side) in enumerate(blks)],
                        reads=(tb("treesb"), tb("oLR_b")), writes=(pst,))
                    tr.emit("vector", lambda e, ct=ct, psb_=psb_: e.tensor_copy(
                        dmap_b[:, ct * T:(ct + 1) * T], psb_[:]),
                        reads=(pst,), writes=(tb("dmap_b"),))
            for j in range(NT_FF):
                tr.emit("vector", lambda e, j=j: e.tensor_tensor(
                    acts_b[:, j * T:(j + 1) * T], acts_b[:, j * T:(j + 1) * T],
                    dmap_b[:, j * T:(j + 1) * T], ALU.mult),
                    reads=(tb("acts_b"), tb("dmap_b")), writes=(tb("acts_b"),))
            for dout in range(ND_D):
                slot = dout % 2
                pst, psb_ = PS[slot], pb[slot]
                wtile, wb = stream_w(c_wffout, dout, NT_FF, slot)
                tr.emit_group("tensor", [
                    (lambda e, kc=kc, st=(kc == 0), sp=(kc == NT_FF - 1), wtile=wtile, psb_=psb_:
                     e.matmul(psb_[:], wtile[:, kc * 128:(kc + 1) * 128],
                              acts_b[:, kc * T:(kc + 1) * T], start=st, stop=sp))
                    for kc in range(NT_FF)], reads=(wb, tb("acts_b")), writes=(pst,))
                consume(dout, pst, psb_)

        def z_tile_load(dst_handle, dst_buf, src_dram, t0, nt):
            dma_in(_ap(dst_handle, 0, [(T, nt), (1, T)]),
                   _dap(src_dram, t0, [(S, 128), (128 * S, nt), (1, T)]), (dst_buf,))

        if max_phase >= 1:
            ld_vec("bff", E["bff"][:], NT_FF)
            ld_vec("inng", E["inng"][:], ND_D)
            ld_vec("innb", E["innb"][:], ND_D)
            for ti in range(ntile):
                t0 = ti * T
                bz = tb("z_in")
                z_tile_load(z_in, bz, E["hx"], t0, ND_D)

                def srcf(i):
                    return z_in[:, i * T:(i + 1) * T]

                def sqf(i):
                    return sq_b[:, i * T:(i + 1) * T]
                ln_stats(srcf, ND_D, sqf, False, bz)
                ln_apply(srcf, ND_D,
                         lambda i: (xn_b[:, i * T:(i + 1) * T], tb("xn_b")), bz)

                def xtile(kc):
                    return xn_b[:, kc * T:(kc + 1) * T]

                def consume(dout, pst, psb_):
                    tr.emit("vector", lambda e, dout=dout, psb_=psb_: e.scalar_tensor_tensor(
                        f2_a[:, dout * T:(dout + 1) * T], psb_[:], 0.5,
                        z_in[:, dout * T:(dout + 1) * T], ALU.mult, ALU.add),
                        reads=(pst, bz), writes=(tb("f2_a"),))
                    dma_out(_dap(x1s, dout * 128 * S + t0, [(S, 128), (1, T)]),
                            f2_a[:, dout * T:(dout + 1) * T], (tb("f2_a"),))
                ffn_tree(xtile, consume, tb("xn_b"))

                def srcf2(i):
                    return f2_a[:, i * T:(i + 1) * T]
                ln_stats(srcf2, ND_D, sqf, False, tb("f2_a"))
                ln_apply(srcf2, ND_D,
                         lambda i: (ln1_b[:, i * T:(i + 1) * T], tb("ln1_b")), tb("f2_a"))
                for i in range(ND_D):
                    dma_out(_dap(ln1s, i * 128 * S + t0, [(S, 128), (1, T)]),
                            ln1_b[:, i * T:(i + 1) * T], (tb("ln1_b"),))

                def srcf3(i):
                    return ln1_b[:, i * T:(i + 1) * T]
                ln_stats(srcf3, ND_D, sqf, True, tb("ln1_b"))
                ln_apply(srcf3, ND_D,
                         lambda i: (z_in[:, i * T:(i + 1) * T], tb("z_in")), tb("ln1_b"),
                         gain=lambda i: vap("inng", i), bias=lambda i: vap("innb", i))
                for i in range(ND_D):
                    dma_out(_dap(zA, i * 128 * S + t0, [(S, 128), (1, T)]),
                            z_in[:, i * T:(i + 1) * T], (tb("z_in"),))
            tr.barrier()

        nmamba = max(0, min(L, max_phase - 1))
        for l in range(nmamba):
            zsrc = zA if l % 2 == 0 else zB
            zdst = zB if l % 2 == 0 else zA
            tr.emit_group("sync", [lambda e, l=l: e.dma_start(
                out=vecs[:, 0:307], in_=E["layvecs"][l])], writes=(bv,), inc=16,
                sem_name=tr.lane("L_vecs"))
            xsrc = _dap(E["xprojT"], l * DI * 33,
                        [(33, 128), (128 * 33, KC_DI), (1, 33)])
            tr.emit_group("sync", [lambda e, xsrc=xsrc: e.dma_start(
                out=_ap(statA, 0, [(33, KC_DI), (1, 33)]), in_=xsrc)],
                writes=(bstat,), inc=16, sem_name=tr.lane("L_statA"))
            tr.emit("vector", lambda e: e.tensor_copy(
                xpw_b[:, 0:KC_DI * 33], statA[:, 0:KC_DI * 33]),
                reads=(bstat,), writes=(tb("xpw_b"),))
            tr.emit("gpsimd", lambda e: e.memset(states_sb[:], 0.0), writes=(tb("states"),))
            tr.emit("gpsimd", lambda e: e.memset(xtails[:], 0.0), writes=(tb("xtails"),))
            tr.emit("gpsimd", lambda e: e.memset(ctails[:], 0.0), writes=(tb("ctails"),))

            for ti in range(ntile):
                t0 = ti * T
                bz = tb("z_in")
                z_tile_load(z_in, bz, zsrc, t0, ND_D)

                def msrc(i):
                    return z_in[:, i * T:(i + 1) * T]

                def msq(i):
                    return sq_b[:, i * T:(i + 1) * T]
                ln_stats(msrc, ND_D, msq, False, bz)
                ln_apply(msrc, ND_D,
                         lambda i: (xn_b[:, i * T:(i + 1) * T], tb("xn_b")), bz,
                         gain=lambda i: vap("lng", i), bias=lambda i: vap("lnb", i))
                for dout in range(2 * ND_DI):
                    slot = dout % 2
                    pst, psb_ = PS[slot], pb[slot]
                    wtile, wb = stream_w(c_inproj, l * 2 * ND_DI + dout, KC_D, slot)
                    tr.emit_group("tensor", [
                        (lambda e, kc=kc, st=(kc == 0), sp=(kc == KC_D - 1), wtile=wtile, psb_=psb_:
                         e.matmul(psb_[:], wtile[:, kc * 128:(kc + 1) * 128],
                                  xn_b[:, kc * T:(kc + 1) * T], start=st, stop=sp))
                        for kc in range(KC_D)], reads=(wb, tb("xn_b")), writes=(pst,))
                    if dout >= ND_DI:
                        dd = dout - ND_DI
                        tr.emit("scalar", lambda e, dd=dd, psb_=psb_: e.activation(
                            sres_b[:, dd * T:(dd + 1) * T], psb_[:], AF.Sigmoid),
                            reads=(pst,), writes=(tb("sres_b"),))
                        continue
                    dd = dout
                    ib = slot * (T + 4)
                    wcvb = tb(f"wcv{slot}")
                    tr.emit_group("sync", [lambda e, dd=dd, slot=slot, l=l: e.dma_start(
                        out=wcv[:, slot * 1024:(slot + 1) * 1024],
                        in_=_dap(c_convd, l * (ND_DI * 128 * 1024) + dd * (128 * 1024),
                                 [(1024, 128), (1, 1024)]))],
                        reads=(), writes=(wcvb,), inc=16, sem_name=tr.lane(f"L_wcv{slot}"))
                    tr.emit("gpsimd", lambda e, ib=ib, dd=dd: e.tensor_copy(
                        cin_b[:, ib + 1:ib + 4], xtails[:, dd * 4:dd * 4 + 3]),
                        reads=(tb("xtails"),), writes=(tb("cin_b"),))
                    tr.emit("scalar", lambda e, ib=ib, psb_=psb_: e.activation(
                        cin_b[:, ib + 4:ib + 4 + T], psb_[:], AF.Silu),
                        reads=(pst,), writes=(tb("cin_b"),))
                    tr.emit("gpsimd", lambda e, ib=ib, dd=dd: e.tensor_copy(
                        xtails[:, dd * 4:dd * 4 + 3], cin_b[:, ib + 1 + T:ib + 4 + T]),
                        reads=(tb("cin_b"),), writes=(tb("xtails"),))
                    cps, cpb = (PS[2], pb2) if dd % 2 == 0 else (PS[3], pb3)
                    tr.emit_group("tensor", [
                        (lambda e, k=k, ib=ib, slot=slot, st=(k == 0), sp=(k == 3), cpb=cpb:
                         e.matmul(cpb[:], wcv[:, slot * 1024 + k * 128:slot * 1024 + (k + 1) * 128],
                                  cin_b[:, ib + 1 + k:ib + 1 + k + T], start=st, stop=sp))
                        for k in range(4)], reads=(wcvb, tb("cin_b")), writes=(cps,))
                    tr.emit("gpsimd", lambda e, ib=ib, dd=dd: e.tensor_copy(
                        cmid_b[:, ib + 1:ib + 4], ctails[:, dd * 4:dd * 4 + 3]),
                        reads=(tb("ctails"),), writes=(tb("cmid_b"),))
                    tr.emit("scalar", lambda e, ib=ib, dd=dd, cpb=cpb: e.activation(
                        cmid_b[:, ib + 4:ib + 4 + T], cpb[:], AF.Silu, bias=vap("cb0", dd)),
                        reads=(cps, bv), writes=(tb("cmid_b"),))
                    tr.emit("gpsimd", lambda e, ib=ib, dd=dd: e.tensor_copy(
                        ctails[:, dd * 4:dd * 4 + 3], cmid_b[:, ib + 1 + T:ib + 4 + T]),
                        reads=(tb("cmid_b"),), writes=(tb("ctails"),))
                    tr.emit_group("tensor", [
                        (lambda e, k=k, ib=ib, slot=slot, st=(k == 0), sp=(k == 3), cpb=cpb:
                         e.matmul(cpb[:], wcv[:, slot * 1024 + (4 + k) * 128:slot * 1024 + (5 + k) * 128],
                                  cmid_b[:, ib + 1 + k:ib + 1 + k + T], start=st, stop=sp))
                        for k in range(4)], reads=(wcvb, tb("cmid_b")), writes=(cps,))
                    tr.emit("scalar", lambda e, dd=dd, cpb=cpb: e.activation(
                        xi2_b[:, dd * T:(dd + 1) * T], cpb[:], AF.Silu, bias=vap("cb1", dd)),
                        reads=(cps, bv), writes=(tb("xi2_b"),))
                tr.emit_group("tensor", [
                    (lambda e, kc=kc, st=(kc == 0), sp=(kc == KC_DI - 1): e.matmul(
                        pb2[0:1, :], xpw_b[:, kc * 33:kc * 33 + 1],
                        xi2_b[:, kc * T:(kc + 1) * T], start=st, stop=sp))
                    for kc in range(KC_DI)], reads=(tb("xpw_b"), tb("xi2_b")), writes=(PS[2],))
                tr.emit_group("tensor", [
                    (lambda e, kc=kc, st=(kc == 0), sp=(kc == KC_DI - 1): e.matmul(
                        pb6[0:16, :], xpw_b[:, kc * 33 + 1:kc * 33 + 17],
                        xi2_b[:, kc * T:(kc + 1) * T], start=st, stop=sp))
                    for kc in range(KC_DI)], reads=(tb("xpw_b"), tb("xi2_b")), writes=(PS[6],))
                tr.emit_group("tensor", [
                    (lambda e, kc=kc, st=(kc == 0), sp=(kc == KC_DI - 1): e.matmul(
                        pb7[0:16, :], xpw_b[:, kc * 33 + 17:kc * 33 + 33],
                        xi2_b[:, kc * T:(kc + 1) * T], start=st, stop=sp))
                    for kc in range(KC_DI)], reads=(tb("xpw_b"), tb("xi2_b")), writes=(PS[7],))
                tr.emit("vector", lambda e: e.tensor_copy(projsb[0:1, :], pb2[0:1, :]),
                        reads=(PS[2],), writes=(tb("projsb"),))
                tr.emit("vector", lambda e: e.tensor_copy(bc_sb[0:16, :], pb6[0:16, :]),
                        reads=(PS[6],), writes=(tb("bc_sb"),))
                tr.emit("vector", lambda e: e.tensor_copy(bc_sb[32:48, :], pb7[0:16, :]),
                        reads=(PS[7],), writes=(tb("bc_sb"),))
                tr.emit("tensor", lambda e: e.matmul(pb3[:], ones_f[0:1, :], projsb[0:1, :], start=True, stop=True),
                        reads=(tb("projsb"), tb("ones_f")), writes=(PS[3],))
                for n in range(NS):
                    slot = 6 + n % 2
                    pst, psb_ = PS[slot], pb[slot]
                    tr.emit("tensor", lambda e, n=n, psb_=psb_: e.matmul(
                        psb_[:], bcsel_b[0:16, n * 128:(n + 1) * 128], bc_sb[0:16, :],
                        start=True, stop=True),
                        reads=(tb("bcsel_b"), tb("bc_sb")), writes=(pst,))
                    tr.emit("vector", lambda e, n=n, psb_=psb_: e.tensor_copy(
                        brep_b[:, n * T:(n + 1) * T], psb_[:]), reads=(pst,), writes=(tb("brep_b"),))
                for n in range(NS):
                    slot = 6 + n % 2
                    pst, psb_ = PS[slot], pb[slot]
                    tr.emit("tensor", lambda e, n=n, psb_=psb_: e.matmul(
                        psb_[:], bcsel_b[32:48, n * 128:(n + 1) * 128], bc_sb[32:48, :],
                        start=True, stop=True),
                        reads=(tb("bcsel_b"), tb("bc_sb")), writes=(pst,))
                    tr.emit("vector", lambda e, n=n, psb_=psb_: e.tensor_copy(
                        crep_b[:, n * T:(n + 1) * T], psb_[:]), reads=(pst,), writes=(tb("crep_b"),))
                def emit_dt(ds):
                    sl = ds % 2
                    tr.emit("scalar", lambda e, ds=ds, sl=sl: e.activation(
                        q1[:], pb3[:], AF.Exp, scale=vap("dtw", ds), bias=vap("dtb", ds)),
                        reads=(PS[3], bv), writes=(tb("q1"),))
                    tr.emit("scalar", lambda e, sl=sl: e.activation(
                        dtt[:, sl * T:(sl + 1) * T], q1[:], AF.Ln, bias=cst[:, 2:3]),
                        reads=(tb("q1"), tb("cst")), writes=(tb(f"dtt{sl}"),))
                    tr.emit("gpsimd", lambda e, ds=ds, sl=sl: e.tensor_tensor(
                        dtxb[:, sl * T:(sl + 1) * T], dtt[:, sl * T:(sl + 1) * T],
                        xi2_b[:, ds * T:(ds + 1) * T], ALU.mult),
                        reads=(tb(f"dtt{sl}"), tb("xi2_b")), writes=(tb(f"dtxb{sl}"),))
                emit_dt(0)
                for ds in range(ND_DI):
                    sl = ds % 2
                    if ds + 1 < ND_DI:
                        emit_dt(ds + 1)
                    dba = dtxb.ap()
                    dtx_bc = bass.AP(tensor=dba.tensor, offset=dba.offset + sl * T,
                                     ap=[[dba.ap[0][0], 128], [0, NS], [1, T]])
                    tr.emit("vector", lambda e, dtx_bc=dtx_bc: e.tensor_tensor(
                        ufull_b[:], dtx_bc, brep_b[:], ALU.mult),
                        reads=(tb(f"dtxb{sl}"), tb("brep_b")), writes=(tb("sq_b"),))
                    def emit_a(n, sl=sl):
                        asl = n % 3
                        if n < 8:
                            tr.emit("vector", lambda e, n=n, asl=asl, sl=sl: e.tensor_scalar(
                                abuf[:, asl * T:(asl + 1) * T], dtt[:, sl * T:(sl + 1) * T],
                                -float(n + 1), 1.0, ALU.mult, ALU.add),
                                reads=(tb(f"dtt{sl}"),), writes=(tb(f"abuf{asl}"),))
                        else:
                            tr.emit("scalar", lambda e, n=n, asl=asl, sl=sl: e.activation(
                                abuf[:, asl * T:(asl + 1) * T], dtt[:, sl * T:(sl + 1) * T],
                                AF.Identity, scale=cst[:, 4 + n - 8:5 + n - 8], bias=cst[:, 2:3]),
                                reads=(tb(f"dtt{sl}"), tb("cst")), writes=(tb(f"abuf{asl}"),))
                    emit_a(0)
                    for n in range(NS):
                        if n + 1 < NS:
                            emit_a(n + 1)
                        sl = n % 3
                        tr.emit("vector", lambda e, n=n, ds=ds, sl=sl: e.tensor_tensor_scan(
                            hbuf_b[:, n * T:(n + 1) * T], abuf[:, sl * T:(sl + 1) * T],
                            ufull_b[:, n * T:(n + 1) * T],
                            states_sb[:, ds * NS + n:ds * NS + n + 1], ALU.mult, ALU.add),
                            reads=(tb(f"abuf{sl}"), tb("sq_b"), tb("states")),
                            writes=(tb(f"hbuf{n % 2}"),))
                    tr.emit("vector", lambda e, ds=ds: e.tensor_copy(
                        states_sb[:, ds * NS:(ds + 1) * NS], _ap(hbuf_b, T - 1, [(T, NS)])),
                        reads=(tb("hbuf0"), tb("hbuf1")), writes=(tb("states"),))
                    tr.emit("vector", lambda e: e.tensor_tensor(
                        hbuf_b[:], hbuf_b[:], crep_b[:], ALU.mult),
                        reads=(tb("crep_b"),), writes=(tb("hbuf0"), tb("hbuf1")))
                    tr.emit("vector", lambda e: e.tensor_tensor(
                        ufull_b[:, 0:8 * T], hbuf_b[:, 0:8 * T], hbuf_b[:, 8 * T:16 * T], ALU.add),
                        reads=(tb("hbuf0"), tb("hbuf1")), writes=(tb("sq_b"),))
                    tr.emit("vector", lambda e: e.tensor_tensor(
                        ufull_b[:, 8 * T:12 * T], ufull_b[:, 0:4 * T], ufull_b[:, 4 * T:8 * T], ALU.add),
                        reads=(tb("sq_b"),), writes=(tb("sq_b"),))
                    tr.emit("vector", lambda e: e.tensor_tensor(
                        ufull_b[:, 12 * T:14 * T], ufull_b[:, 8 * T:10 * T], ufull_b[:, 10 * T:12 * T], ALU.add),
                        reads=(tb("sq_b"),), writes=(tb("sq_b"),))
                    tr.emit("vector", lambda e: e.tensor_tensor(
                        q2[:], ufull_b[:, 12 * T:13 * T], ufull_b[:, 13 * T:14 * T], ALU.add),
                        reads=(tb("sq_b"),), writes=(tb("q2"),))
                    tr.emit("vector", lambda e, ds=ds: e.scalar_tensor_tensor(
                        y2_b[:, ds * T:(ds + 1) * T], xi2_b[:, ds * T:(ds + 1) * T],
                        vap("Dv", ds), q2[:], ALU.mult, ALU.add),
                        reads=(tb("xi2_b"), tb("q2"), bv), writes=(tb("y2_b"),))
                def y2src(i):
                    return y2_b[:, i * T:(i + 1) * T]

                def y2sq(i):
                    return sq_b[:, i * T:(i + 1) * T]
                ln_stats(y2src, ND_DI, y2sq, True, tb("y2_b"))
                tr.emit("vector", lambda e: e.tensor_copy(srow_b[:], pb4[:]),
                        reads=(PS[4],), writes=(tb("srow_b"),))
                tr.emit("vector", lambda e: e.tensor_copy(mrow_b[:], pb5[:]),
                        reads=(PS[5],), writes=(tb("mrow_b"),))
                sba = srow_b.ap()
                s_bc = bass.AP(tensor=sba.tensor, offset=sba.offset,
                               ap=[[sba.ap[0][0], 128], [0, ND_DI], [1, T]])
                mba = mrow_b.ap()
                m_bc = bass.AP(tensor=mba.tensor, offset=mba.offset,
                               ap=[[mba.ap[0][0], 128], [0, ND_DI], [1, T]])
                tr.emit("vector", lambda e, s_bc=s_bc: e.tensor_tensor(
                    hbuf2_v[:], y2_b[:], s_bc, ALU.mult),
                    reads=(tb("y2_b"), tb("srow_b")), writes=(tb("hbuf0"), tb("hbuf1")))
                tr.emit("vector", lambda e, m_bc=m_bc: e.tensor_tensor(
                    hbuf2_v[:], hbuf2_v[:], m_bc, ALU.subtract),
                    reads=(tb("mrow_b"),), writes=(tb("hbuf0"), tb("hbuf1")))
                for i in range(ND_DI):
                    tr.emit("scalar", lambda e, i=i: e.activation(
                        q1[:], hbuf2_v[:, i * T:(i + 1) * T], AF.Identity,
                        scale=vap("mlng", i), bias=vap("mlnb", i)),
                        reads=(tb("hbuf0"), tb("hbuf1"), bv), writes=(tb("q1"),))
                    tr.emit("vector", lambda e, i=i: e.tensor_tensor(
                        brep_b[:, i * T:(i + 1) * T], q1[:], sres_b[:, i * T:(i + 1) * T], ALU.mult),
                        reads=(tb("q1"), tb("sres_b")), writes=(tb("brep_b"),))
                for dout in range(ND_D):
                    slot = dout % 2
                    pst, psb_ = PS[slot], pb[slot]
                    wtile, wb = stream_w(c_outproj, l * ND_D + dout, KC_DI, slot)
                    tr.emit_group("tensor", [
                        (lambda e, kc=kc, st=(kc == 0), sp=(kc == KC_DI - 1), wtile=wtile, psb_=psb_:
                         e.matmul(psb_[:], wtile[:, kc * 128:(kc + 1) * 128],
                                  brep_b[:, kc * T:(kc + 1) * T], start=st, stop=sp))
                        for kc in range(KC_DI)], reads=(wb, tb("brep_b")), writes=(pst,))
                    zsl = dout % 2
                    tr.emit("vector", lambda e, dout=dout, psb_=psb_, zsl=zsl: e.tensor_tensor(
                        zo_pp[:, zsl * T:(zsl + 1) * T], psb_[:],
                        z_in[:, dout * T:(dout + 1) * T], ALU.add),
                        reads=(pst, bz), writes=(tb("zo_pp"),))
                    dma_out(_dap(zdst, dout * 128 * S + t0, [(S, 128), (1, T)]),
                            zo_pp[:, zsl * T:(zsl + 1) * T], (tb("zo_pp"),))
            tr.barrier()

        if max_phase >= 6:
            ld_vec("b1", E["b1"][:], ND_D)
            ld_vec("b2", E["b2"][:], ND_D)
            ld_vec("bff", E["bff"][:], NT_FF)
            zsrc = zA if L % 2 == 0 else zB
            for ti in range(ntile):
                t0 = ti * T
                bz = tb("z_in")
                z_tile_load(z_in, bz, zsrc, t0, ND_D)
                bx1 = tb("f2_a")
                z_tile_load(f2_a, bx1, x1s, t0, ND_D)
                bl1 = tb("ln1_b")
                z_tile_load(ln1_b, bl1, ln1s, t0, ND_D)
                for i in range(ND_D):
                    tr.emit("vector", lambda e, i=i: e.tensor_copy(
                        xn_b[:, i * T:(i + 1) * T], z_in[:, i * T:(i + 1) * T]),
                        reads=(bz,), writes=(tb("xn_b"),))
                for dout in range(ND_D):
                    slot = dout % 2
                    pst, psb_ = PS[slot], pb[slot]
                    wtile, wb = stream_w(c_op1, dout, KC_D, slot)
                    tr.emit_group("tensor", [
                        (lambda e, kc=kc, st=(kc == 0), sp=(kc == KC_D - 1), wtile=wtile, psb_=psb_:
                         e.matmul(psb_[:], wtile[:, kc * 128:(kc + 1) * 128],
                                  xn_b[:, kc * T:(kc + 1) * T], start=st, stop=sp))
                        for kc in range(KC_D)], reads=(wb, tb("xn_b")), writes=(pst,))
                    tr.emit("scalar", lambda e, dout=dout, psb_=psb_: e.activation(
                        acts_b[:, dout * T:(dout + 1) * T], psb_[:], AF.Gelu, bias=vap("b1", dout)),
                        reads=(pst, bv), writes=(tb("acts_b"),))
                for dout in range(ND_D):
                    slot = dout % 2
                    pst, psb_ = PS[slot], pb[slot]
                    wtile, wb = stream_w(c_op2, dout, KC_D, slot)
                    tr.emit_group("tensor", [
                        (lambda e, kc=kc, st=(kc == 0), sp=(kc == KC_D - 1), wtile=wtile, psb_=psb_:
                         e.matmul(psb_[:], wtile[:, kc * 128:(kc + 1) * 128],
                                  acts_b[:, kc * T:(kc + 1) * T], start=st, stop=sp))
                        for kc in range(KC_D)], reads=(wb, tb("acts_b")), writes=(pst,))
                    tr.emit("scalar", lambda e, dout=dout, psb_=psb_: e.activation(
                        q1[:], psb_[:], AF.Identity, bias=vap("b2", dout)),
                        reads=(pst, bv), writes=(tb("q1"),))
                    tr.emit("vector", lambda e, dout=dout: e.tensor_tensor(
                        q2[:], q1[:], ln1_b[:, dout * T:(dout + 1) * T], ALU.add),
                        reads=(tb("q1"), bl1), writes=(tb("q2"),))
                    tr.emit("vector", lambda e, dout=dout: e.tensor_tensor(
                        f2_a[:, dout * T:(dout + 1) * T], q2[:],
                        f2_a[:, dout * T:(dout + 1) * T], ALU.add),
                        reads=(tb("q2"), bx1), writes=(bx1,))

                if dbg:
                    for i in range(ND_D):
                        dma_out(_dap(dbg1, i * 128 * S + t0, [(S, 128), (1, T)]),
                                f2_a[:, i * T:(i + 1) * T], (bx1,))
                        tr.emit("vector", lambda e, i=i: e.tensor_copy(
                            q1[:], acts_b[:, i * T:(i + 1) * T]),
                            reads=(tb("acts_b"),), writes=(tb("q1"),))
                        dma_out(_dap(dbg2, i * 128 * S + t0, [(S, 128), (1, T)]),
                                q1[:], (tb("q1"),))

                def srcf6(i):
                    return f2_a[:, i * T:(i + 1) * T]

                def sqf6(i):
                    return sq_b[:, i * T:(i + 1) * T]
                ln_stats(srcf6, ND_D, sqf6, False, bx1)
                ln_apply(srcf6, ND_D,
                         lambda i: (xn_b[:, i * T:(i + 1) * T], tb("xn_b")), bx1)

                def xtile6(kc):
                    return xn_b[:, kc * T:(kc + 1) * T]

                def consume6(dout, pst, psb_):
                    zsl = dout % 2
                    tr.emit("vector", lambda e, dout=dout, psb_=psb_, zsl=zsl: e.scalar_tensor_tensor(
                        zo_pp[:, zsl * T:(zsl + 1) * T], psb_[:], 0.5,
                        f2_a[:, dout * T:(dout + 1) * T], ALU.mult, ALU.add),
                        reads=(pst, bx1), writes=(tb("zo_pp"),))
                    dma_out(_dap(out_t, dout * 128 * S + t0, [(S, 128), (1, T)]),
                            zo_pp[:, zsl * T:(zsl + 1) * T], (tb("zo_pp"),))
                ffn_tree(xtile6, consume6, tb("xn_b"))
        tr.barrier()

        @block.sync
        def _(e):
            for a in tr.q["sync"]:
                a(e)

        @block.tensor
        def _(e):
            for a in tr.q["tensor"]:
                a(e)

        @block.vector
        def _(e):
            for a in tr.q["vector"]:
                a(e)

        @block.scalar
        def _(e):
            for a in tr.q["scalar"]:
                a(e)

        @block.gpsimd
        def _(e):
            for a in tr.q["gpsimd"]:
                a(e)

    return nc


def kernel(**inputs):
    g = _prep(inputs)
    hxs = g.pop("_hx_per_core")
    nc = build_program()
    in_maps = []
    for b in range(8):
        m = dict(g)
        m["hx"] = hxs[b]
        in_maps.append(m)
    res = run_bass_kernel_spmd(nc, in_maps, core_ids=list(range(8)))
    outs = [np.asarray(r["out"]).T for r in res.results]
    return np.ascontiguousarray(np.stack(outs, 0).astype(np.float32))


# version 14 (final)
# speedup vs baseline: 1.1663x; 1.0046x over previous
import numpy as np
import concourse.bass as bass
import concourse.mybir as mybir
from concourse.bass_utils import run_bass_kernel_spmd

F32 = mybir.dt.float32
BF16 = mybir.dt.bfloat16
AF = mybir.ActivationFunctionType
ALU = mybir.AluOpType
AX = mybir.AxisListType

D = 1024
DI = 2048
S = 8192
T = 512
NTILE = S // T
NS = 16
L = 4
KC_D = D // 128
KC_DI = DI // 128
ND_D = D // 128
ND_DI = DI // 128
EPS = 1e-5
DEPTH_FF = 7
PAR_FF = 8
NN_FF = 255
LOFF = [0, 1, 2, 3, 4, 5, 7, 11, 19]
NT_FF = 19
J2 = NT_FF * 128
SAME_ENGINE_EDGES = 2



def _vec2sb(v, ntiles):
    out = np.zeros(128 * ntiles, np.float32)
    out[:v.shape[0]] = v
    return np.ascontiguousarray(out.reshape(ntiles, 128).T)


def _ffn_perm():
    perm = np.full(J2, -1, np.int64)
    for d in range(DEPTH_FF + 1):
        sz = 1 << d
        base = 128 * LOFF[d]
        for p in range(PAR_FF):
            for i in range(sz):
                perm[base + p * sz + i] = p * NN_FF + (sz - 1 + i)
    return perm


def _tree_blocks():
    blocks = []
    for d in range(DEPTH_FF):
        psz = 1 << d
        csz = 1 << (d + 1)
        nchild_rows = PAR_FF * csz
        nct = (nchild_rows + 127) // 128
        for ctl in range(nct):
            ct = LOFF[d + 1] + ctl
            matL = np.zeros((128, 128), np.float32)
            matR = np.zeros((128, 128), np.float32)
            pt = None
            for cl in range(min(128, nchild_rows - ctl * 128)):
                crow = ctl * 128 + cl
                p = crow // csz
                ip = crow % csz
                prow = p * psz + (ip >> 1)
                side = ip & 1
                pt_this = LOFF[d] + prow // 128
                if pt is None:
                    pt = pt_this
                assert pt == pt_this
                (matR if side else matL)[prow % 128, cl] = 1.0
            blocks.append((d, pt, ct, 0, matL))
            blocks.append((d, pt, ct, 1, matR))
    return blocks


_TREE_BLOCKS = _tree_blocks()
_PERM = _ffn_perm()


def _prep(inputs):
    g = {}
    hs = np.asarray(inputs["L0_attn_hidden_states"], np.float32)
    g["_hx_per_core"] = [np.ascontiguousarray(hs[b].T) for b in range(8)]

    W_in = np.asarray(inputs["L0_ffn_W_in"], np.float32)
    b_in = np.asarray(inputs["L0_ffn_b_in"], np.float32)
    W_out = np.asarray(inputs["L0_ffn_W_out"], np.float32)
    WiT = np.zeros((D, J2), np.float32)
    WoT = np.zeros((J2, D), np.float32)
    bff = np.zeros(J2, np.float32)
    valid = _PERM >= 0
    WiT[:, valid] = W_in[_PERM[valid], :].T
    WoT[valid, :] = W_out[:, _PERM[valid]].T
    bff[valid] = b_in[_PERM[valid]]
    g["wffinT"] = np.ascontiguousarray(WiT)
    g["wffoutT"] = np.ascontiguousarray(WoT)
    g["bff"] = _vec2sb(bff, NT_FF)
    g["treeblk"] = np.ascontiguousarray(
        np.stack([m for (_, _, _, _, m) in _TREE_BLOCKS], 0))
    rootsel = np.zeros((1, 128), np.float32)
    rootsel[0, :PAR_FF] = 1.0
    g["rootsel"] = rootsel

    g["op1T"] = np.ascontiguousarray(np.asarray(inputs["L0_attn_op1_w"], np.float32).T)
    g["op2T"] = np.ascontiguousarray(np.asarray(inputs["L0_attn_op2_w"], np.float32).T)
    g["b1"] = _vec2sb(np.asarray(inputs["L0_attn_op1_b"], np.float32), ND_D)
    g["b2"] = _vec2sb(np.asarray(inputs["L0_attn_op2_b"], np.float32), ND_D)
    g["inng"] = _vec2sb(np.asarray(inputs["L0_attn_in_norm_g"], np.float32), ND_D)
    g["innb"] = _vec2sb(np.asarray(inputs["L0_attn_in_norm_b"], np.float32), ND_D)

    lng = np.asarray(inputs["L0_attn_ln_g"], np.float32)
    lnb = np.asarray(inputs["L0_attn_ln_b"], np.float32)
    g["lng"] = np.stack([_vec2sb(lng[l], ND_D) for l in range(L)])
    g["lnb"] = np.stack([_vec2sb(lnb[l], ND_D) for l in range(L)])
    g["inprojT"] = np.ascontiguousarray(
        np.asarray(inputs["L0_attn_in_proj_w"], np.float32).transpose(0, 2, 1))
    g["xprojT"] = np.ascontiguousarray(
        np.asarray(inputs["L0_attn_x_proj_w"], np.float32).transpose(0, 2, 1))
    g["outprojT"] = np.ascontiguousarray(
        np.asarray(inputs["L0_attn_out_proj_w"], np.float32).transpose(0, 2, 1))
    cw = np.asarray(inputs["L0_attn_conv_w"], np.float32)
    g["convw"] = np.ascontiguousarray(np.stack([
        np.stack([np.stack([_vec2sb(cw[l, c, :, k], ND_DI) for k in range(4)])
                  for c in range(2)]) for l in range(L)]))
    cb = np.asarray(inputs["L0_attn_conv_b"], np.float32)
    g["convb"] = np.ascontiguousarray(np.stack([
        np.stack([_vec2sb(cb[l, c], ND_DI) for c in range(2)]) for l in range(L)]))
    g["dtw"] = np.stack([_vec2sb(np.asarray(inputs["L0_attn_dt_proj_w"], np.float32)[l, :, 0], ND_DI) for l in range(L)])
    g["dtb"] = np.stack([_vec2sb(np.asarray(inputs["L0_attn_dt_proj_b"], np.float32)[l], ND_DI) for l in range(L)])
    g["Dv"] = np.stack([_vec2sb(np.asarray(inputs["L0_attn_D"], np.float32)[l], ND_DI) for l in range(L)])
    g["mlng"] = np.stack([_vec2sb(np.asarray(inputs["L0_attn_mamba_ln_g"], np.float32)[l], ND_DI) for l in range(L)])
    g["mlnb"] = np.stack([_vec2sb(np.asarray(inputs["L0_attn_mamba_ln_b"], np.float32)[l], ND_DI) for l in range(L)])
    g["onesm"] = np.ones((128, 128), np.float32)
    lv = np.zeros((4, 128, 307), np.float32)
    for l in range(4):
        cols = []
        cols.append(_vec2sb(lng[l], ND_D)); cols.append(_vec2sb(lnb[l], ND_D))
        cols.append(_vec2sb(np.asarray(inputs["L0_attn_in_norm_g"], np.float32), ND_D))
        cols.append(_vec2sb(np.asarray(inputs["L0_attn_in_norm_b"], np.float32), ND_D))
        for c in range(2):
            for k in range(4):
                cols.append(_vec2sb(cw[l, c, :, k], ND_DI))
        for c in range(2):
            cols.append(_vec2sb(cb[l, c], ND_DI))
        cols.append(_vec2sb(np.asarray(inputs["L0_attn_dt_proj_w"], np.float32)[l, :, 0], ND_DI))
        cols.append(_vec2sb(np.asarray(inputs["L0_attn_dt_proj_b"], np.float32)[l], ND_DI))
        cols.append(_vec2sb(np.asarray(inputs["L0_attn_D"], np.float32)[l], ND_DI))
        cols.append(_vec2sb(np.asarray(inputs["L0_attn_mamba_ln_g"], np.float32)[l], ND_DI))
        cols.append(_vec2sb(np.asarray(inputs["L0_attn_mamba_ln_b"], np.float32)[l], ND_DI))
        cols.append(_vec2sb(bff, NT_FF))
        cols.append(_vec2sb(np.asarray(inputs["L0_attn_op1_b"], np.float32), ND_D))
        cols.append(_vec2sb(np.asarray(inputs["L0_attn_op2_b"], np.float32), ND_D))
        lv[l] = np.concatenate(cols, axis=1)
    g["layvecs"] = np.ascontiguousarray(lv)
    cd = np.zeros((L, 16, 128, 8, 128), np.float32)
    for l in range(L):
        for c in range(2):
            for k in range(4):
                wv = cw[l, c, :, k].reshape(16, 128)
                for dt_ in range(16):
                    np.fill_diagonal(cd[l, dt_, :, c * 4 + k, :], wv[dt_])
    g["convdiag"] = np.ascontiguousarray(cd.reshape(L, 2048, 1024))
    sel = np.zeros((48, 16 * 128), np.float32)
    for n in range(16):
        sel[n, n * 128:(n + 1) * 128] = 1.0
        sel[32 + n, n * 128:(n + 1) * 128] = 1.0
    g["bcsel"] = sel
    return g



class Buf:
    __slots__ = ("name", "writer", "readers")

    def __init__(self, name):
        self.name = name
        self.writer = None
        self.readers = []


class Tracker:
    ENGINES = ("sync", "tensor", "vector", "scalar", "gpsimd")

    def __init__(self, sems, lane_pool):
        self.sems = dict(sems)
        self.lane_pool = list(lane_pool)
        self.cnt = {e: 0 for e in self.ENGINES}
        self.q = {e: [] for e in self.ENGINES}
        self.trace = []
        self.waited = {e: {} for e in self.ENGINES}

    def lane(self, name):
        if name not in self.sems:
            assert self.lane_pool, "out of DMA lane semaphores"
            self.sems[name] = self.lane_pool.pop()
            self.cnt[name] = 0
        return name

    def _waits_for(self, eng, reads, writes):
        need = {}

        def add(dep):
            if dep is None:
                return
            wsn, wcnt = dep
            if wsn == eng:
                if eng == "tensor":
                    return
                if SAME_ENGINE_EDGES is False:
                    return
                if SAME_ENGINE_EDGES is not True and                         self.cnt[eng] - wcnt >= SAME_ENGINE_EDGES:
                    return
            if need.get(wsn, 0) < wcnt:
                need[wsn] = wcnt
        for b in reads:
            add(b.writer)
        for b in writes:
            add(b.writer)
            for r in b.readers:
                add(r)
        return need

    def _push(self, eng, need, fns, inc, sn):
        sem = self.sems

        def action(e, need=need, fns=fns, inc=inc, sn=sn):
            for weng in sorted(need):
                e.wait_ge(sem[weng], need[weng])
            last = None
            for f in fns:
                last = f(e)
            last.then_inc(sem[sn], inc)
        self.q[eng].append(action)

    def emit(self, eng, fn, reads=(), writes=(), inc=1):
        return self.emit_group(eng, [fn], reads=reads, writes=writes, inc=inc)

    def emit_group(self, eng, fns, reads=(), writes=(), inc=1, sem_name=None):
        sn = sem_name or eng
        need = self._waits_for(eng, reads, writes)
        w = self.waited[eng]
        need = {k: v for k, v in need.items() if v > w.get(k, 0)}
        for k, v in need.items():
            w[k] = v
        self.cnt[sn] += inc
        self.trace.append((eng, sn, self.cnt[sn], tuple(b.name for b in reads),
                           tuple(b.name for b in writes), dict(need), len(fns)))
        self._push(eng, need, fns, inc, sn)
        dep = (sn, self.cnt[sn])
        for b in writes:
            b.writer = dep
            b.readers = []
        for b in reads:
            if b not in writes:
                b.readers.append(dep)
        return dep

    def barrier(self):
        snap = dict(self.cnt)
        sem = self.sems
        for eng in self.ENGINES:
            w = self.waited[eng]
            todo = {osn: v for osn, v in snap.items()
                    if osn != eng and v > 0 and v > w.get(osn, 0)}
            for osn, v in todo.items():
                w[osn] = v
            def action(e, todo=todo, eng=eng):
                for osn, v in todo.items():
                    e.wait_ge(sem[osn], v)
            self.q[eng].append(action)


def _ap(handle, offset, dims, npart=128):
    base = handle.ap()
    return bass.AP(tensor=base.tensor, offset=base.offset + offset,
                   ap=[[base.ap[0][0], npart]] + [[s, c] for (s, c) in dims])


def _dap(dram_ap, offset, dims):
    return bass.AP(tensor=dram_ap.tensor, offset=dram_ap.offset + offset,
                   ap=[[s, c] for (s, c) in dims])



def build_program(max_phase=99, dbg=False, ntile=NTILE):
    nc = bass.Bass("TRN2", target_bir_lowering=False)
    E = {}

    def ein(name, shape):
        E[name] = nc.dram_tensor(name, list(shape), F32, kind="ExternalInput").ap()

    ein("hx", (D, S))
    ein("wffinT", (D, J2)); ein("wffoutT", (J2, D)); ein("bff", (128, NT_FF))
    ein("treeblk", (len(_TREE_BLOCKS), 128, 128)); ein("rootsel", (1, 128))
    ein("op1T", (D, D)); ein("op2T", (D, D)); ein("b1", (128, ND_D)); ein("b2", (128, ND_D))
    ein("inng", (128, ND_D)); ein("innb", (128, ND_D))
    ein("lng", (L, 128, ND_D)); ein("lnb", (L, 128, ND_D))
    ein("inprojT", (L, D, 2 * DI)); ein("xprojT", (L, DI, 33)); ein("outprojT", (L, DI, D))
    ein("convw", (L, 2, 4, 128, ND_DI)); ein("convb", (L, 2, 128, ND_DI))
    ein("dtw", (L, 128, ND_DI)); ein("dtb", (L, 128, ND_DI)); ein("Dv", (L, 128, ND_DI))
    ein("mlng", (L, 128, ND_DI)); ein("mlnb", (L, 128, ND_DI))
    ein("onesm", (128, 128))
    ein("layvecs", (L, 128, 307))
    ein("convdiag", (L, 2048, 1024))
    ein("bcsel", (48, 16 * 128))

    out_t = nc.dram_tensor("out", [D, S], F32, kind="ExternalOutput").ap()
    KD = "ExternalOutput" if dbg else "Internal"
    if dbg:
        dbg1 = nc.dram_tensor("dbg1", [D, S], F32, kind="ExternalOutput").ap()
        dbg2 = nc.dram_tensor("dbg2", [D, S], F32, kind="ExternalOutput").ap()
    zA = nc.dram_tensor("zA", [D, S], F32, kind=KD).ap()
    zB = nc.dram_tensor("zB", [D, S], F32, kind=KD).ap()
    x1s = nc.dram_tensor("x1s", [D, S], F32, kind=KD).ap()
    ln1s = nc.dram_tensor("ln1s", [D, S], BF16, kind=KD).ap()
    c_inproj = nc.dram_tensor("c_inproj", [L, 2 * ND_DI, 128, KC_D * 128], BF16, kind="Internal").ap()
    c_outproj = nc.dram_tensor("c_outproj", [L, ND_D, 128, KC_DI * 128], BF16, kind="Internal").ap()
    c_wffin = nc.dram_tensor("c_wffin", [NT_FF, 128, KC_D * 128], BF16, kind="Internal").ap()
    c_wffout = nc.dram_tensor("c_wffout", [ND_D, 128, NT_FF * 128], BF16, kind="Internal").ap()
    c_op1 = nc.dram_tensor("c_op1", [ND_D, 128, KC_D * 128], BF16, kind="Internal").ap()
    c_op2 = nc.dram_tensor("c_op2", [ND_D, 128, KC_D * 128], BF16, kind="Internal").ap()
    c_convd = nc.dram_tensor("c_convd", [L, ND_DI, 128, 1024], BF16, kind="Internal").ap()

    NBLK = len(_TREE_BLOCKS)
    import contextlib
    es = contextlib.ExitStack()
    with es:
        ones_f = es.enter_context(nc.sbuf_tensor("ones_f", [128, 128], F32))
        ones_b = es.enter_context(nc.sbuf_tensor("ones_b", [128, 128], BF16))
        treesb = es.enter_context(nc.sbuf_tensor("treesb", [128, NBLK * 128], BF16))
        rootsb = es.enter_context(nc.sbuf_tensor("rootsb", [1, 128], BF16))
        onerow_b = es.enter_context(nc.sbuf_tensor("onerow_b", [1, T], BF16))
        wstg = es.enter_context(nc.sbuf_tensor("wstg", [128, 384], F32))
        wstgb = es.enter_context(nc.sbuf_tensor("wstgb", [128, 384], BF16))
        xpw_b = es.enter_context(nc.sbuf_tensor("xpw_b", [128, KC_DI * 33], BF16))
        wbufA = es.enter_context(nc.sbuf_tensor("wbufA", [128, NT_FF * 128], BF16))
        wbufB = es.enter_context(nc.sbuf_tensor("wbufB", [128, NT_FF * 128], BF16))
        vecs = es.enter_context(nc.sbuf_tensor("vecs", [128, 320], F32))
        z_in = es.enter_context(nc.sbuf_tensor("z_in", [128, ND_D * T], F32))
        zo_pp = es.enter_context(nc.sbuf_tensor("zo_pp", [128, 2 * T], F32))
        xn_b = es.enter_context(nc.sbuf_tensor("xn_b", [128, ND_D * T], BF16))
        cin_b = es.enter_context(nc.sbuf_tensor("cin_b", [128, 4 * (T + 4)], BF16))
        xtails = es.enter_context(nc.sbuf_tensor("xtails", [128, ND_DI * 4], BF16))
        cmid_b = es.enter_context(nc.sbuf_tensor("cmid_b", [128, 4 * (T + 4)], BF16))
        ctails = es.enter_context(nc.sbuf_tensor("ctails", [128, ND_DI * 4], BF16))
        arena = es.enter_context(nc.sbuf_tensor("arena", [128, 49152], BF16))
        projsb = es.enter_context(nc.sbuf_tensor("projsb", [33, T], F32))
        bc_sb = es.enter_context(nc.sbuf_tensor("bc_sb", [48, T], BF16))
        bcsel_b = es.enter_context(nc.sbuf_tensor("bcsel_b", [48, 16 * 128], BF16))
        dtt = es.enter_context(nc.sbuf_tensor("dtt", [128, 2 * T], F32))
        dtxb = es.enter_context(nc.sbuf_tensor("dtxb", [128, 2 * T], BF16))
        wcv = es.enter_context(nc.sbuf_tensor("wcv", [128, 4 * 512], BF16))
        abuf = es.enter_context(nc.sbuf_tensor("abuf", [128, 3 * T], BF16))

        q1 = es.enter_context(nc.sbuf_tensor("q1", [128, T], F32))
        q2 = es.enter_context(nc.sbuf_tensor("q2", [128, T], F32))
        states_sb = es.enter_context(nc.sbuf_tensor("states", [128, ND_DI * NS], F32))
        sq_b = es.enter_context(nc.sbuf_tensor("sq_b", [128, ND_DI * T], BF16))
        statA = es.enter_context(nc.sbuf_tensor("statA", [128, 3 * T], F32))

        cst = es.enter_context(nc.sbuf_tensor("cst", [128, 12], F32))

        pb0 = es.enter_context(nc.psum_tensor("pb0", [128, T], F32))
        pb1 = es.enter_context(nc.psum_tensor("pb1", [128, T], F32))
        pb2 = es.enter_context(nc.psum_tensor("pb2", [128, T], F32))
        pb3 = es.enter_context(nc.psum_tensor("pb3", [128, T], F32))
        pb4 = es.enter_context(nc.psum_tensor("pb4", [128, T], F32))
        pb5 = es.enter_context(nc.psum_tensor("pb5", [128, T], F32))
        pb6 = es.enter_context(nc.psum_tensor("pb6", [128, T], F32))
        pb7 = es.enter_context(nc.psum_tensor("pb7", [128, T], F32))
        s_sync = es.enter_context(nc.semaphore("s_sync"))
        lane_sems = [es.enter_context(nc.semaphore(f"lane{i}")) for i in range(24)]
        s_tensor = es.enter_context(nc.semaphore("s_tensor"))
        s_vector = es.enter_context(nc.semaphore("s_vector"))
        s_scalar = es.enter_context(nc.semaphore("s_scalar"))
        s_gpsimd = es.enter_context(nc.semaphore("s_gpsimd"))
        block = es.enter_context(nc.Block())
        pb = [pb0, pb1, pb2, pb3, pb4, pb5, pb6, pb7]

        class TView:

            def __init__(self, h, off, nels):
                self.h, self.off, self.nels = h, off, nels

            def ap(self):
                b = self.h.ap()
                return bass.AP(tensor=b.tensor, offset=b.offset + self.off,
                               ap=[[b.ap[0][0], 128], [1, self.nels]])

            def __getitem__(self, key):
                base = self.ap()
                if isinstance(key, slice):
                    return base
                ps, fs = key
                assert (ps.start or 0) == 0
                f0 = fs.start or 0
                f1 = fs.stop if fs.stop is not None else self.nels
                return bass.AP(tensor=base.tensor, offset=base.offset + f0,
                               ap=[[base.ap[0][0], 128], [1, f1 - f0]])

        arena_f = arena.bitcast(F32)
        ufull_b = TView(sq_b, 0, NS * T)
        hbuf2_v = TView(arena, 24576, ND_DI * T)
        srow_b = TView(arena, 40960, T)
        mrow_b = TView(arena, 41472, T)
        xi2_b = TView(arena, 0, ND_DI * T)
        sres_b = TView(arena, 8192, ND_DI * T)
        y2_b = TView(arena, 16384, ND_DI * T)
        hbuf_b = TView(arena, 24576, NS * T)
        brep_b = TView(arena, 32768, NS * T)
        crep_b = TView(arena, 40960, NS * T)
        acts_b = TView(arena, 0, NT_FF * T)
        dec_b = TView(arena, 9728, NT_FF * T)
        dmap_b = TView(arena, 19456, NT_FF * T)
        oLR_b = TView(arena, 29184, 8 * T)
        ln1_b = TView(arena, 33280, ND_D * T)
        f2_a = TView(arena_f, 18688, ND_D * T)
        tr = Tracker({"sync": s_sync, "tensor": s_tensor, "vector": s_vector,
                      "scalar": s_scalar, "gpsimd": s_gpsimd}, lane_sems)
        build_program.last_trace = tr.trace
        B = {}

        def tb(name):
            if name not in B:
                B[name] = Buf(name)
            return B[name]

        PS = [tb(f"ps{i}") for i in range(8)]
        bv = tb("vecs")

        VEC = {}
        _vc = 0
        for nm, n in [("lng", ND_D), ("lnb", ND_D), ("inng", ND_D), ("innb", ND_D),
                      ("cw00", ND_DI), ("cw01", ND_DI), ("cw02", ND_DI), ("cw03", ND_DI),
                      ("cw10", ND_DI), ("cw11", ND_DI), ("cw12", ND_DI), ("cw13", ND_DI),
                      ("cb0", ND_DI), ("cb1", ND_DI),
                      ("dtw", ND_DI), ("dtb", ND_DI), ("Dv", ND_DI),
                      ("mlng", ND_DI), ("mlnb", ND_DI),
                      ("bff", NT_FF), ("b1", ND_D), ("b2", ND_D)]:
            VEC[nm] = _vc
            _vc += n
        assert _vc <= 320

        def vap(name, i):
            c = VEC[name] + i
            return vecs[:, c:c + 1]

        def ld_vec(dst, src_ap, n):
            c = VEC[dst]
            tr.emit_group("sync", [lambda e, c=c, s=src_ap, n=n: e.dma_start(
                out=vecs[:, c:c + n], in_=s)], writes=(bv,), inc=16,
                sem_name=tr.lane("L_vecs"))

        def dma_in(dst_ap, src_ap, wb, rb=()):
            ln_ = tr.lane("L_" + wb[0].name)
            tr.emit_group("sync", [lambda e, d=dst_ap, s=src_ap: e.dma_start(out=d, in_=s)],
                          reads=rb, writes=wb, inc=16, sem_name=ln_)

        def dma_out(dst_ap, src_ap, rb, wb=()):
            ln_ = tr.lane("S_" + rb[0].name)
            tr.emit_group("gpsimd", [lambda e, d=dst_ap, s=src_ap: e.dma_start(out=d, in_=s)],
                          reads=rb, writes=wb, inc=16, sem_name=ln_)

        dma_in(ones_f[:], E["onesm"][:], (tb("ones_f"),))
        tr.emit("vector", lambda e: e.tensor_copy(ones_b[:], ones_f[:]),
                reads=(tb("ones_f"),), writes=(tb("ones_b"),))
        tr.emit("gpsimd", lambda e: e.memset(onerow_b[:], 1.0), writes=(tb("onerow_b"),))
        tr.emit("gpsimd", lambda e: e.memset(cst[:, 0:1], EPS), writes=(tb("cst"),))
        tr.emit("gpsimd", lambda e: e.memset(cst[:, 1:2], -0.5), writes=(tb("cst"),))
        tr.emit("gpsimd", lambda e: e.memset(cst[:, 2:3], 1.0), writes=(tb("cst"),))
        for _n in range(8, 16):
            tr.emit("gpsimd", lambda e, _n=_n: e.memset(cst[:, 4 + _n - 8:5 + _n - 8], -float(_n + 1)),
                    writes=(tb("cst"),))

        def cache_build(src2d, rows, cols, dst_ap_fn, wname):
            CC = 384
            for ch in range(rows // 128):
                for c0 in range(0, cols, CC):
                    cw_ = min(CC, cols - c0)
                    dma_in(wstg[:, 0:cw_], src2d[ch * 128:(ch + 1) * 128, c0:c0 + cw_],
                           (tb("wstg"),))
                    tr.emit("vector", lambda e, c=cw_: e.tensor_copy(wstgb[:, 0:c], wstg[:, 0:c]),
                            reads=(tb("wstg"),), writes=(tb("wstgb"),))
                    dma_out(dst_ap_fn(ch, c0, cw_), wstgb[:, 0:cw_], (tb("wstgb"),))

        for l in range(L):
            def dst_ip(kc, c0, cw_, l=l):
                return _dap(c_inproj,
                            l * (2 * ND_DI * 128 * KC_D * 128)
                            + (c0 // 128) * (128 * KC_D * 128) + kc * 128,
                            [(KC_D * 128, 128), (128 * KC_D * 128, cw_ // 128), (1, 128)])
            cache_build(E["inprojT"][l], D, 2 * DI, dst_ip, "ip")

            def dst_op(kc, c0, cw_, l=l):
                return _dap(c_outproj,
                            l * (ND_D * 128 * KC_DI * 128)
                            + (c0 // 128) * (128 * KC_DI * 128) + kc * 128,
                            [(KC_DI * 128, 128), (128 * KC_DI * 128, cw_ // 128), (1, 128)])
            cache_build(E["outprojT"][l], DI, D, dst_op, "op")

        for l in range(L):
            def dst_cd(ch, c0, cw_, l=l):
                return _dap(c_convd, l * (ND_DI * 128 * 1024) + ch * (128 * 1024) + c0,
                            [(1024, 128), (1, cw_)])
            cache_build(E["convdiag"][l], DI, 1024, dst_cd, "cd")
        cache_build(E["wffinT"], D, J2,
                    lambda kc, c0, cw_: _dap(c_wffin, (c0 // 128) * (128 * KC_D * 128) + kc * 128,
                                             [(KC_D * 128, 128), (128 * KC_D * 128, cw_ // 128), (1, 128)]), "fi")
        cache_build(E["wffoutT"], J2, D,
                    lambda kc, c0, cw_: _dap(c_wffout, (c0 // 128) * (128 * NT_FF * 128) + kc * 128,
                                             [(NT_FF * 128, 128), (128 * NT_FF * 128, cw_ // 128), (1, 128)]), "fo")
        cache_build(E["op1T"], D, D,
                    lambda kc, c0, cw_: _dap(c_op1, (c0 // 128) * (128 * KC_D * 128) + kc * 128,
                                             [(KC_D * 128, 128), (128 * KC_D * 128, cw_ // 128), (1, 128)]), "o1")
        cache_build(E["op2T"], D, D,
                    lambda kc, c0, cw_: _dap(c_op2, (c0 // 128) * (128 * KC_D * 128) + kc * 128,
                                             [(KC_D * 128, 128), (128 * KC_D * 128, cw_ // 128), (1, 128)]), "o2")

        for b0 in range(0, NBLK, 1):
            nb = 1
            src = _dap(E["treeblk"], b0 * 128 * 128,
                       [(128, 128), (128 * 128, nb), (1, 128)])
            dma_in(wstg[:, 0:nb * 128], src, (tb("wstg"),))
            tr.emit("vector", lambda e, b0=b0, nb=nb: e.tensor_copy(
                treesb[:, b0 * 128:(b0 + nb) * 128], wstg[:, 0:nb * 128]),
                reads=(tb("wstg"),), writes=(tb("treesb"),))
        for c0 in range(0, 2048, 128):
            dma_in(wstg[0:48, 0:128], E["bcsel"][:, c0:c0 + 128], (tb("wstg"),))
            tr.emit("vector", lambda e, c0=c0: e.tensor_copy(
                bcsel_b[0:48, c0:c0 + 128], wstg[0:48, 0:128]),
                reads=(tb("wstg"),), writes=(tb("bcsel_b"),))
        dma_in(wstg[0:1, 0:128], E["rootsel"][:], (tb("wstg"),))
        tr.emit("vector", lambda e: e.tensor_copy(rootsb[:], wstg[0:1, 0:128]),
                reads=(tb("wstg"),), writes=(tb("rootsb"),))
        tr.barrier()

        bstat = tb("stat")

        def ln_stats(src_tiles, nt, sq_view, src_is_bf, src_buf):
            Dtot = float(nt * 128)
            for i in range(nt):
                tr.emit("scalar", lambda e, i=i: e.activation(sq_view(i), src_tiles(i), AF.Square),
                        reads=(src_buf,), writes=(tb("sq_b"),))
            oc = ones_b[:, 0:1] if src_is_bf else ones_f[:, 0:1]
            ob = tb("ones_b") if src_is_bf else tb("ones_f")
            tr.emit_group("tensor", [
                (lambda e, i=i, st=(i == 0), sp=(i == nt - 1): e.matmul(
                    pb2[0:1, :], oc, src_tiles(i), start=st, stop=sp))
                for i in range(nt)], reads=(src_buf, ob), writes=(PS[2],))
            tr.emit_group("tensor", [
                (lambda e, i=i, st=(i == 0), sp=(i == nt - 1): e.matmul(
                    pb3[0:1, :], ones_b[:, 0:1], sq_view(i), start=st, stop=sp))
                for i in range(nt)], reads=(tb("sq_b"), tb("ones_b")), writes=(PS[3],))
            tr.emit("vector", lambda e: e.tensor_scalar(statA[0:1, 0:T], pb2[0:1, :], 1.0 / Dtot, None, ALU.mult),
                    reads=(PS[2],), writes=(bstat,))
            tr.emit("vector", lambda e: e.tensor_scalar(statA[32:33, 0:T], pb3[0:1, :], 1.0 / Dtot, None, ALU.mult),
                    reads=(PS[3],), writes=(bstat,))
            tr.emit("vector", lambda e: e.tensor_tensor(statA[32:33, T:2 * T], statA[0:1, 0:T], statA[0:1, 0:T], ALU.mult),
                    reads=(bstat,), writes=(bstat,))
            tr.emit("vector", lambda e: e.tensor_tensor(statA[64:65, 0:T], statA[32:33, 0:T], statA[32:33, T:2 * T], ALU.subtract),
                    reads=(bstat,), writes=(bstat,))
            tr.emit("scalar", lambda e: e.activation(statA[64:65, T:2 * T], statA[64:65, 0:T], AF.Ln, bias=cst[64:65, 0:1]),
                    reads=(bstat, tb("cst")), writes=(bstat,))
            tr.emit("scalar", lambda e: e.activation(statA[0:1, T:2 * T], statA[64:65, T:2 * T], AF.Exp, scale=cst[64:65, 1:2]),
                    reads=(bstat, tb("cst")), writes=(bstat,))
            tr.emit("vector", lambda e: e.tensor_tensor(statA[0:1, 2 * T:3 * T], statA[0:1, 0:T], statA[0:1, T:2 * T], ALU.mult),
                    reads=(bstat,), writes=(bstat,))
            tr.emit("tensor", lambda e: e.matmul(pb4[:], ones_f[0:1, :], statA[0:1, T:2 * T], start=True, stop=True),
                    reads=(bstat, tb("ones_f")), writes=(PS[4],))
            tr.emit("tensor", lambda e: e.matmul(pb5[:], ones_f[0:1, :], statA[0:1, 2 * T:3 * T], start=True, stop=True),
                    reads=(bstat, tb("ones_f")), writes=(PS[5],))

        def ln_apply(src_tiles, nt, out_fn, src_buf, gain=None, bias=None):
            for i in range(nt):
                tr.emit("vector", lambda e, i=i: e.tensor_tensor(q1[:], src_tiles(i), pb4[:], ALU.mult),
                        reads=(src_buf, PS[4]), writes=(tb("q1"),))
                if gain is None:
                    oap, ob = out_fn(i)
                    tr.emit("vector", lambda e, oap=oap: e.tensor_tensor(oap, q1[:], pb5[:], ALU.subtract),
                            reads=(tb("q1"), PS[5]), writes=(ob,))
                else:
                    tr.emit("vector", lambda e: e.tensor_tensor(q2[:], q1[:], pb5[:], ALU.subtract),
                            reads=(tb("q1"), PS[5]), writes=(tb("q2"),))
                    oap, ob = out_fn(i)
                    tr.emit("scalar", lambda e, oap=oap, g=gain(i), bb=bias(i): e.activation(
                        oap, q2[:], AF.Identity, scale=g, bias=bb),
                        reads=(tb("q2"), bv), writes=(ob,))

        def stream_w(cache_ap, idx, ncols, slot):
            wtile = wbufA if slot == 0 else wbufB
            wb = tb("wbufA") if slot == 0 else tb("wbufB")
            src = _dap(cache_ap, idx * 128 * ncols * 128, [(ncols * 128, 128), (1, ncols * 128)])
            dma_in(_ap(wtile, 0, [(1, ncols * 128)]), src, (wb,))
            return wtile, wb

        def ffn_tree(xtile, consume, xnb_buf):
            blocks_by_ct = {}
            for bi, (d, pt, ct, side, _) in enumerate(_TREE_BLOCKS):
                blocks_by_ct.setdefault((d, ct), []).append((bi, pt, side))
            for j in range(NT_FF):
                slot = j % 2
                pst, psb_ = (PS[slot], pb[slot])
                wtile, wb = stream_w(c_wffin, j, KC_D, slot)
                tr.emit_group("tensor", [
                    (lambda e, kc=kc, st=(kc == 0), sp=(kc == KC_D - 1), wtile=wtile, psb_=psb_:
                     e.matmul(psb_[:], wtile[:, kc * 128:(kc + 1) * 128], xtile(kc), start=st, stop=sp))
                    for kc in range(KC_D)], reads=(wb, xnb_buf), writes=(pst,))
                tr.emit("scalar", lambda e, j=j, psb_=psb_: e.activation(
                    acts_b[:, j * T:(j + 1) * T], psb_[:], AF.Silu,
                    bias=vecs[:, VEC["bff"] + j:VEC["bff"] + j + 1]),
                    reads=(bv,), writes=(tb("acts_b"), pst))
                tr.emit("vector", lambda e, j=j, psb_=psb_: e.tensor_scalar(
                    dec_b[:, j * T:(j + 1) * T], psb_[:],
                    vecs[:, VEC["bff"] + j:VEC["bff"] + j + 1], 0.0, ALU.add, ALU.is_gt),
                    reads=(bv,), writes=(tb("dec_b"), pst))
            tr.emit("tensor", lambda e: e.matmul(pb2[:], rootsb[0:1, :], onerow_b[0:1, :], start=True, stop=True),
                    reads=(tb("rootsb"), tb("onerow_b")), writes=(PS[2],))
            tr.emit("vector", lambda e: e.tensor_copy(dmap_b[:, 0:T], pb2[:]),
                    reads=(PS[2],), writes=(tb("dmap_b"),))
            for d in range(DEPTH_FF):
                pts = sorted({pt for (dd, pt, ct, s2, m) in _TREE_BLOCKS if dd == d})
                pt_slot = {pt: k for k, pt in enumerate(pts)}
                for pt, k in pt_slot.items():
                    tr.emit("vector", lambda e, pt=pt, k=k: e.tensor_tensor(
                        oLR_b[:, (2 * k + 1) * T:(2 * k + 2) * T],
                        dmap_b[:, pt * T:(pt + 1) * T], dec_b[:, pt * T:(pt + 1) * T], ALU.mult),
                        reads=(tb("dmap_b"), tb("dec_b")), writes=(tb("oLR_b"),))
                    tr.emit("vector", lambda e, pt=pt, k=k: e.tensor_tensor(
                        oLR_b[:, (2 * k) * T:(2 * k + 1) * T],
                        dmap_b[:, pt * T:(pt + 1) * T], oLR_b[:, (2 * k + 1) * T:(2 * k + 2) * T], ALU.subtract),
                        reads=(tb("dmap_b"),), writes=(tb("oLR_b"),))
                cts = sorted({ct for (dd, pt, ct, s2, m) in _TREE_BLOCKS if dd == d})
                for ci, ct in enumerate(cts):
                    blks = blocks_by_ct[(d, ct)]
                    slot = 2 + ci % 2
                    pst, psb_ = PS[slot], pb[slot]
                    tr.emit_group("tensor", [
                        (lambda e, bi=bi, sl=2 * pt_slot[pt] + side, st=(ii == 0), sp=(ii == len(blks) - 1), psb_=psb_:
                         e.matmul(psb_[:], treesb[:, bi * 128:(bi + 1) * 128],
                                  oLR_b[:, sl * T:(sl + 1) * T], start=st, stop=sp))
                        for ii, (bi, pt, side) in enumerate(blks)],
                        reads=(tb("treesb"), tb("oLR_b")), writes=(pst,))
                    tr.emit("vector", lambda e, ct=ct, psb_=psb_: e.tensor_copy(
                        dmap_b[:, ct * T:(ct + 1) * T], psb_[:]),
                        reads=(pst,), writes=(tb("dmap_b"),))
            for j in range(NT_FF):
                tr.emit("vector", lambda e, j=j: e.tensor_tensor(
                    acts_b[:, j * T:(j + 1) * T], acts_b[:, j * T:(j + 1) * T],
                    dmap_b[:, j * T:(j + 1) * T], ALU.mult),
                    reads=(tb("acts_b"), tb("dmap_b")), writes=(tb("acts_b"),))
            for dout in range(ND_D):
                slot = dout % 2
                pst, psb_ = PS[slot], pb[slot]
                wtile, wb = stream_w(c_wffout, dout, NT_FF, slot)
                tr.emit_group("tensor", [
                    (lambda e, kc=kc, st=(kc == 0), sp=(kc == NT_FF - 1), wtile=wtile, psb_=psb_:
                     e.matmul(psb_[:], wtile[:, kc * 128:(kc + 1) * 128],
                              acts_b[:, kc * T:(kc + 1) * T], start=st, stop=sp))
                    for kc in range(NT_FF)], reads=(wb, tb("acts_b")), writes=(pst,))
                consume(dout, pst, psb_)

        def z_tile_load(dst_handle, dst_buf, src_dram, t0, nt):
            dma_in(_ap(dst_handle, 0, [(T, nt), (1, T)]),
                   _dap(src_dram, t0, [(S, 128), (128 * S, nt), (1, T)]), (dst_buf,))

        if max_phase >= 1:
            ld_vec("bff", E["bff"][:], NT_FF)
            ld_vec("inng", E["inng"][:], ND_D)
            ld_vec("innb", E["innb"][:], ND_D)
            for ti in range(ntile):
                t0 = ti * T
                bz = tb("z_in")
                z_tile_load(z_in, bz, E["hx"], t0, ND_D)

                def srcf(i):
                    return z_in[:, i * T:(i + 1) * T]

                def sqf(i):
                    return sq_b[:, i * T:(i + 1) * T]
                ln_stats(srcf, ND_D, sqf, False, bz)
                ln_apply(srcf, ND_D,
                         lambda i: (xn_b[:, i * T:(i + 1) * T], tb("xn_b")), bz)

                def xtile(kc):
                    return xn_b[:, kc * T:(kc + 1) * T]

                def consume(dout, pst, psb_):
                    tr.emit("vector", lambda e, dout=dout, psb_=psb_: e.scalar_tensor_tensor(
                        f2_a[:, dout * T:(dout + 1) * T], psb_[:], 0.5,
                        z_in[:, dout * T:(dout + 1) * T], ALU.mult, ALU.add),
                        reads=(pst, bz), writes=(tb("f2_a"),))
                    dma_out(_dap(x1s, dout * 128 * S + t0, [(S, 128), (1, T)]),
                            f2_a[:, dout * T:(dout + 1) * T], (tb("f2_a"),))
                ffn_tree(xtile, consume, tb("xn_b"))

                def srcf2(i):
                    return f2_a[:, i * T:(i + 1) * T]
                ln_stats(srcf2, ND_D, sqf, False, tb("f2_a"))
                ln_apply(srcf2, ND_D,
                         lambda i: (ln1_b[:, i * T:(i + 1) * T], tb("ln1_b")), tb("f2_a"))
                for i in range(ND_D):
                    dma_out(_dap(ln1s, i * 128 * S + t0, [(S, 128), (1, T)]),
                            ln1_b[:, i * T:(i + 1) * T], (tb("ln1_b"),))

                def srcf3(i):
                    return ln1_b[:, i * T:(i + 1) * T]
                ln_stats(srcf3, ND_D, sqf, True, tb("ln1_b"))
                ln_apply(srcf3, ND_D,
                         lambda i: (z_in[:, i * T:(i + 1) * T], tb("z_in")), tb("ln1_b"),
                         gain=lambda i: vap("inng", i), bias=lambda i: vap("innb", i))
                for i in range(ND_D):
                    dma_out(_dap(zA, i * 128 * S + t0, [(S, 128), (1, T)]),
                            z_in[:, i * T:(i + 1) * T], (tb("z_in"),))
            tr.barrier()

        nmamba = max(0, min(L, max_phase - 1))
        for l in range(nmamba):
            zsrc = zA if l % 2 == 0 else zB
            zdst = zB if l % 2 == 0 else zA
            tr.emit_group("sync", [lambda e, l=l: e.dma_start(
                out=vecs[:, 0:307], in_=E["layvecs"][l])], writes=(bv,), inc=16,
                sem_name=tr.lane("L_vecs"))
            xsrc = _dap(E["xprojT"], l * DI * 33,
                        [(33, 128), (128 * 33, KC_DI), (1, 33)])
            tr.emit_group("sync", [lambda e, xsrc=xsrc: e.dma_start(
                out=_ap(statA, 0, [(33, KC_DI), (1, 33)]), in_=xsrc)],
                writes=(bstat,), inc=16, sem_name=tr.lane("L_statA"))
            tr.emit("vector", lambda e: e.tensor_copy(
                xpw_b[:, 0:KC_DI * 33], statA[:, 0:KC_DI * 33]),
                reads=(bstat,), writes=(tb("xpw_b"),))
            tr.emit("gpsimd", lambda e: e.memset(states_sb[:], 0.0), writes=(tb("states"),))
            tr.emit("gpsimd", lambda e: e.memset(xtails[:], 0.0), writes=(tb("xtails"),))
            tr.emit("gpsimd", lambda e: e.memset(ctails[:], 0.0), writes=(tb("ctails"),))

            for ti in range(ntile):
                t0 = ti * T
                bz = tb("z_in")
                z_tile_load(z_in, bz, zsrc, t0, ND_D)

                def msrc(i):
                    return z_in[:, i * T:(i + 1) * T]

                def msq(i):
                    return sq_b[:, i * T:(i + 1) * T]
                ln_stats(msrc, ND_D, msq, False, bz)
                ln_apply(msrc, ND_D,
                         lambda i: (xn_b[:, i * T:(i + 1) * T], tb("xn_b")), bz,
                         gain=lambda i: vap("lng", i), bias=lambda i: vap("lnb", i))
                def emit_conv1(dd, l=l):
                    cs = dd % 4
                    ib = cs * (T + 4)
                    ws = dd % 2
                    wcvb = tb(f"wcv{ws}")
                    tr.emit_group("sync", [lambda e, dd=dd, ws=ws, l=l: e.dma_start(
                        out=wcv[:, ws * 512:(ws + 1) * 512],
                        in_=_dap(c_convd, l * (ND_DI * 128 * 1024) + dd * (128 * 1024),
                                 [(1024, 128), (1, 512)]))],
                        reads=(), writes=(wcvb,), inc=16, sem_name=tr.lane(f"L_wcv{ws}"))
                    tr.emit("gpsimd", lambda e, ib=ib, dd=dd: e.tensor_copy(
                        xtails[:, dd * 4:dd * 4 + 3], cin_b[:, ib + 1 + T:ib + 4 + T]),
                        reads=(tb("cin_b"),), writes=(tb("xtails"),))
                    cps, cpb = (PS[2], pb2) if dd % 2 == 0 else (PS[3], pb3)
                    tr.emit_group("tensor", [
                        (lambda e, k=k, ib=ib, ws=ws, st=(k == 0), sp=(k == 3), cpb=cpb:
                         e.matmul(cpb[:], wcv[:, ws * 512 + k * 128:ws * 512 + (k + 1) * 128],
                                  cin_b[:, ib + 1 + k:ib + 1 + k + T], start=st, stop=sp))
                        for k in range(4)], reads=(wcvb, tb("cin_b")), writes=(cps,))
                    tr.emit("gpsimd", lambda e, ib=ib, dd=dd: e.tensor_copy(
                        cmid_b[:, ib + 1:ib + 4], ctails[:, dd * 4:dd * 4 + 3]),
                        reads=(tb("ctails"),), writes=(tb("cmid_b"),))
                    tr.emit("scalar", lambda e, ib=ib, dd=dd, cpb=cpb: e.activation(
                        cmid_b[:, ib + 4:ib + 4 + T], cpb[:], AF.Silu, bias=vap("cb0", dd)),
                        reads=(cps, bv), writes=(tb("cmid_b"),))
                    tr.emit("gpsimd", lambda e, ib=ib, dd=dd: e.tensor_copy(
                        ctails[:, dd * 4:dd * 4 + 3], cmid_b[:, ib + 1 + T:ib + 4 + T]),
                        reads=(tb("cmid_b"),), writes=(tb("ctails"),))

                def emit_conv2(dd, l=l):
                    cs = dd % 4
                    ib = cs * (T + 4)
                    ws = 2 + dd % 2
                    wcvb = tb(f"wcv{ws}")
                    tr.emit_group("sync", [lambda e, dd=dd, ws=ws, l=l: e.dma_start(
                        out=wcv[:, ws * 512:(ws + 1) * 512],
                        in_=_dap(c_convd, l * (ND_DI * 128 * 1024) + dd * (128 * 1024) + 512,
                                 [(1024, 128), (1, 512)]))],
                        reads=(), writes=(wcvb,), inc=16, sem_name=tr.lane(f"L_wcv{ws}"))
                    cps, cpb = (PS[4], pb4) if dd % 2 == 0 else (PS[5], pb5)
                    tr.emit_group("tensor", [
                        (lambda e, k=k, ib=ib, ws=ws, st=(k == 0), sp=(k == 3), cpb=cpb:
                         e.matmul(cpb[:], wcv[:, ws * 512 + k * 128:ws * 512 + (k + 1) * 128],
                                  cmid_b[:, ib + 1 + k:ib + 1 + k + T], start=st, stop=sp))
                        for k in range(4)], reads=(wcvb, tb("cmid_b")), writes=(cps,))
                    tr.emit("scalar", lambda e, dd=dd, cpb=cpb: e.activation(
                        xi2_b[:, dd * T:(dd + 1) * T], cpb[:], AF.Silu, bias=vap("cb1", dd)),
                        reads=(cps, bv), writes=(tb("xi2_b"),))

                for dout in range(2 * ND_DI):
                    slot = dout % 2
                    pst, psb_ = PS[slot], pb[slot]
                    wtile, wb = stream_w(c_inproj, l * 2 * ND_DI + dout, KC_D, slot)
                    tr.emit_group("tensor", [
                        (lambda e, kc=kc, st=(kc == 0), sp=(kc == KC_D - 1), wtile=wtile, psb_=psb_:
                         e.matmul(psb_[:], wtile[:, kc * 128:(kc + 1) * 128],
                                  xn_b[:, kc * T:(kc + 1) * T], start=st, stop=sp))
                        for kc in range(KC_D)], reads=(wb, tb("xn_b")), writes=(pst,))
                    if dout >= ND_DI:
                        dd2 = dout - ND_DI
                        tr.emit("scalar", lambda e, dd2=dd2, psb_=psb_: e.activation(
                            sres_b[:, dd2 * T:(dd2 + 1) * T], psb_[:], AF.Sigmoid),
                            reads=(pst,), writes=(tb("sres_b"),))
                    else:
                        dd2 = dout
                        cs = dd2 % 4
                        ib = cs * (T + 4)
                        tr.emit("gpsimd", lambda e, ib=ib, dd2=dd2: e.tensor_copy(
                            cin_b[:, ib + 1:ib + 4], xtails[:, dd2 * 4:dd2 * 4 + 3]),
                            reads=(tb("xtails"),), writes=(tb("cin_b"),))
                        tr.emit("scalar", lambda e, ib=ib, psb_=psb_: e.activation(
                            cin_b[:, ib + 4:ib + 4 + T], psb_[:], AF.Silu),
                            reads=(pst,), writes=(tb("cin_b"),))
                    if dout >= 2 and dout - 2 < ND_DI:
                        emit_conv1(dout - 2)
                    if dout >= 4 and dout - 4 < ND_DI:
                        emit_conv2(dout - 4)
                tr.emit_group("tensor", [
                    (lambda e, kc=kc, st=(kc == 0), sp=(kc == KC_DI - 1): e.matmul(
                        pb2[0:1, :], xpw_b[:, kc * 33:kc * 33 + 1],
                        xi2_b[:, kc * T:(kc + 1) * T], start=st, stop=sp))
                    for kc in range(KC_DI)], reads=(tb("xpw_b"), tb("xi2_b")), writes=(PS[2],))
                tr.emit_group("tensor", [
                    (lambda e, kc=kc, st=(kc == 0), sp=(kc == KC_DI - 1): e.matmul(
                        pb6[0:16, :], xpw_b[:, kc * 33 + 1:kc * 33 + 17],
                        xi2_b[:, kc * T:(kc + 1) * T], start=st, stop=sp))
                    for kc in range(KC_DI)], reads=(tb("xpw_b"), tb("xi2_b")), writes=(PS[6],))
                tr.emit_group("tensor", [
                    (lambda e, kc=kc, st=(kc == 0), sp=(kc == KC_DI - 1): e.matmul(
                        pb7[0:16, :], xpw_b[:, kc * 33 + 17:kc * 33 + 33],
                        xi2_b[:, kc * T:(kc + 1) * T], start=st, stop=sp))
                    for kc in range(KC_DI)], reads=(tb("xpw_b"), tb("xi2_b")), writes=(PS[7],))
                tr.emit("vector", lambda e: e.tensor_copy(projsb[0:1, :], pb2[0:1, :]),
                        reads=(PS[2],), writes=(tb("projsb"),))
                tr.emit("vector", lambda e: e.tensor_copy(bc_sb[0:16, :], pb6[0:16, :]),
                        reads=(PS[6],), writes=(tb("bc_sb"),))
                tr.emit("vector", lambda e: e.tensor_copy(bc_sb[32:48, :], pb7[0:16, :]),
                        reads=(PS[7],), writes=(tb("bc_sb"),))
                tr.emit("tensor", lambda e: e.matmul(pb3[:], ones_f[0:1, :], projsb[0:1, :], start=True, stop=True),
                        reads=(tb("projsb"), tb("ones_f")), writes=(PS[3],))
                for n in range(NS):
                    slot = 6 + n % 2
                    pst, psb_ = PS[slot], pb[slot]
                    tr.emit("tensor", lambda e, n=n, psb_=psb_: e.matmul(
                        psb_[:], bcsel_b[0:16, n * 128:(n + 1) * 128], bc_sb[0:16, :],
                        start=True, stop=True),
                        reads=(tb("bcsel_b"), tb("bc_sb")), writes=(pst,))
                    tr.emit("vector", lambda e, n=n, psb_=psb_: e.tensor_copy(
                        brep_b[:, n * T:(n + 1) * T], psb_[:]), reads=(pst,), writes=(tb("brep_b"),))
                for n in range(NS):
                    slot = 6 + n % 2
                    pst, psb_ = PS[slot], pb[slot]
                    tr.emit("tensor", lambda e, n=n, psb_=psb_: e.matmul(
                        psb_[:], bcsel_b[32:48, n * 128:(n + 1) * 128], bc_sb[32:48, :],
                        start=True, stop=True),
                        reads=(tb("bcsel_b"), tb("bc_sb")), writes=(pst,))
                    tr.emit("vector", lambda e, n=n, psb_=psb_: e.tensor_copy(
                        crep_b[:, n * T:(n + 1) * T], psb_[:]), reads=(pst,), writes=(tb("crep_b"),))
                def emit_dt(ds):
                    sl = ds % 2
                    tr.emit("scalar", lambda e, ds=ds, sl=sl: e.activation(
                        q1[:], pb3[:], AF.Exp, scale=vap("dtw", ds), bias=vap("dtb", ds)),
                        reads=(PS[3], bv), writes=(tb("q1"),))
                    tr.emit("scalar", lambda e, sl=sl: e.activation(
                        dtt[:, sl * T:(sl + 1) * T], q1[:], AF.Ln, bias=cst[:, 2:3]),
                        reads=(tb("q1"), tb("cst")), writes=(tb(f"dtt{sl}"),))
                    tr.emit("gpsimd", lambda e, ds=ds, sl=sl: e.tensor_tensor(
                        dtxb[:, sl * T:(sl + 1) * T], dtt[:, sl * T:(sl + 1) * T],
                        xi2_b[:, ds * T:(ds + 1) * T], ALU.mult),
                        reads=(tb(f"dtt{sl}"), tb("xi2_b")), writes=(tb(f"dtxb{sl}"),))
                emit_dt(0)
                for ds in range(ND_DI):
                    sl = ds % 2
                    if ds + 1 < ND_DI:
                        emit_dt(ds + 1)
                    dba = dtxb.ap()
                    dtx_bc = bass.AP(tensor=dba.tensor, offset=dba.offset + sl * T,
                                     ap=[[dba.ap[0][0], 128], [0, NS], [1, T]])
                    tr.emit("vector", lambda e, dtx_bc=dtx_bc: e.tensor_tensor(
                        ufull_b[:], dtx_bc, brep_b[:], ALU.mult),
                        reads=(tb(f"dtxb{sl}"), tb("brep_b")), writes=(tb("sq_b"),))
                    def emit_a(n, sl=sl):
                        asl = n % 3
                        if n < 8:
                            tr.emit("vector", lambda e, n=n, asl=asl, sl=sl: e.tensor_scalar(
                                abuf[:, asl * T:(asl + 1) * T], dtt[:, sl * T:(sl + 1) * T],
                                -float(n + 1), 1.0, ALU.mult, ALU.add),
                                reads=(tb(f"dtt{sl}"),), writes=(tb(f"abuf{asl}"),))
                        else:
                            tr.emit("scalar", lambda e, n=n, asl=asl, sl=sl: e.activation(
                                abuf[:, asl * T:(asl + 1) * T], dtt[:, sl * T:(sl + 1) * T],
                                AF.Identity, scale=cst[:, 4 + n - 8:5 + n - 8], bias=cst[:, 2:3]),
                                reads=(tb(f"dtt{sl}"), tb("cst")), writes=(tb(f"abuf{asl}"),))
                    emit_a(0)
                    for n in range(NS):
                        if n + 1 < NS:
                            emit_a(n + 1)
                        sl = n % 3
                        tr.emit("vector", lambda e, n=n, ds=ds, sl=sl: e.tensor_tensor_scan(
                            hbuf_b[:, n * T:(n + 1) * T], abuf[:, sl * T:(sl + 1) * T],
                            ufull_b[:, n * T:(n + 1) * T],
                            states_sb[:, ds * NS + n:ds * NS + n + 1], ALU.mult, ALU.add),
                            reads=(tb(f"abuf{sl}"), tb("sq_b"), tb("states")),
                            writes=(tb(f"hbuf{n % 2}"),))
                    tr.emit("vector", lambda e, ds=ds: e.tensor_copy(
                        states_sb[:, ds * NS:(ds + 1) * NS], _ap(hbuf_b, T - 1, [(T, NS)])),
                        reads=(tb("hbuf0"), tb("hbuf1")), writes=(tb("states"),))
                    tr.emit("vector", lambda e: e.tensor_tensor(
                        hbuf_b[:], hbuf_b[:], crep_b[:], ALU.mult),
                        reads=(tb("crep_b"),), writes=(tb("hbuf0"), tb("hbuf1")))
                    tr.emit("vector", lambda e: e.tensor_tensor(
                        ufull_b[:, 0:8 * T], hbuf_b[:, 0:8 * T], hbuf_b[:, 8 * T:16 * T], ALU.add),
                        reads=(tb("hbuf0"), tb("hbuf1")), writes=(tb("sq_b"),))
                    tr.emit("vector", lambda e: e.tensor_tensor(
                        ufull_b[:, 8 * T:12 * T], ufull_b[:, 0:4 * T], ufull_b[:, 4 * T:8 * T], ALU.add),
                        reads=(tb("sq_b"),), writes=(tb("sq_b"),))
                    tr.emit("vector", lambda e: e.tensor_tensor(
                        ufull_b[:, 12 * T:14 * T], ufull_b[:, 8 * T:10 * T], ufull_b[:, 10 * T:12 * T], ALU.add),
                        reads=(tb("sq_b"),), writes=(tb("sq_b"),))
                    tr.emit("vector", lambda e: e.tensor_tensor(
                        q2[:], ufull_b[:, 12 * T:13 * T], ufull_b[:, 13 * T:14 * T], ALU.add),
                        reads=(tb("sq_b"),), writes=(tb("q2"),))
                    tr.emit("vector", lambda e, ds=ds: e.scalar_tensor_tensor(
                        y2_b[:, ds * T:(ds + 1) * T], xi2_b[:, ds * T:(ds + 1) * T],
                        vap("Dv", ds), q2[:], ALU.mult, ALU.add),
                        reads=(tb("xi2_b"), tb("q2"), bv), writes=(tb("y2_b"),))
                def y2src(i):
                    return y2_b[:, i * T:(i + 1) * T]

                def y2sq(i):
                    return sq_b[:, i * T:(i + 1) * T]
                ln_stats(y2src, ND_DI, y2sq, True, tb("y2_b"))
                tr.emit("vector", lambda e: e.tensor_copy(srow_b[:], pb4[:]),
                        reads=(PS[4],), writes=(tb("srow_b"), tb("crep_b")))
                tr.emit("vector", lambda e: e.tensor_copy(mrow_b[:], pb5[:]),
                        reads=(PS[5],), writes=(tb("mrow_b"), tb("crep_b")))
                sba = srow_b.ap()
                s_bc = bass.AP(tensor=sba.tensor, offset=sba.offset,
                               ap=[[sba.ap[0][0], 128], [0, ND_DI], [1, T]])
                mba = mrow_b.ap()
                m_bc = bass.AP(tensor=mba.tensor, offset=mba.offset,
                               ap=[[mba.ap[0][0], 128], [0, ND_DI], [1, T]])
                tr.emit("vector", lambda e, s_bc=s_bc: e.tensor_tensor(
                    hbuf2_v[:], y2_b[:], s_bc, ALU.mult),
                    reads=(tb("y2_b"), tb("srow_b")), writes=(tb("hbuf0"), tb("hbuf1")))
                tr.emit("vector", lambda e, m_bc=m_bc: e.tensor_tensor(
                    hbuf2_v[:], hbuf2_v[:], m_bc, ALU.subtract),
                    reads=(tb("mrow_b"),), writes=(tb("hbuf0"), tb("hbuf1")))
                for i in range(ND_DI):
                    tr.emit("scalar", lambda e, i=i: e.activation(
                        q1[:], hbuf2_v[:, i * T:(i + 1) * T], AF.Identity,
                        scale=vap("mlng", i), bias=vap("mlnb", i)),
                        reads=(tb("hbuf0"), tb("hbuf1"), bv), writes=(tb("q1"),))
                    tr.emit("vector", lambda e, i=i: e.tensor_tensor(
                        brep_b[:, i * T:(i + 1) * T], q1[:], sres_b[:, i * T:(i + 1) * T], ALU.mult),
                        reads=(tb("q1"), tb("sres_b")), writes=(tb("brep_b"),))
                for dout in range(ND_D):
                    slot = dout % 2
                    pst, psb_ = PS[slot], pb[slot]
                    wtile, wb = stream_w(c_outproj, l * ND_D + dout, KC_DI, slot)
                    tr.emit_group("tensor", [
                        (lambda e, kc=kc, st=(kc == 0), sp=(kc == KC_DI - 1), wtile=wtile, psb_=psb_:
                         e.matmul(psb_[:], wtile[:, kc * 128:(kc + 1) * 128],
                                  brep_b[:, kc * T:(kc + 1) * T], start=st, stop=sp))
                        for kc in range(KC_DI)], reads=(wb, tb("brep_b")), writes=(pst,))
                    zsl = dout % 2
                    tr.emit("vector", lambda e, dout=dout, psb_=psb_, zsl=zsl: e.tensor_tensor(
                        zo_pp[:, zsl * T:(zsl + 1) * T], psb_[:],
                        z_in[:, dout * T:(dout + 1) * T], ALU.add),
                        reads=(pst, bz), writes=(tb("zo_pp"),))
                    dma_out(_dap(zdst, dout * 128 * S + t0, [(S, 128), (1, T)]),
                            zo_pp[:, zsl * T:(zsl + 1) * T], (tb("zo_pp"),))
            tr.barrier()

        if max_phase >= 6:
            ld_vec("b1", E["b1"][:], ND_D)
            ld_vec("b2", E["b2"][:], ND_D)
            ld_vec("bff", E["bff"][:], NT_FF)
            zsrc = zA if L % 2 == 0 else zB
            for ti in range(ntile):
                t0 = ti * T
                bz = tb("z_in")
                z_tile_load(z_in, bz, zsrc, t0, ND_D)
                bx1 = tb("f2_a")
                z_tile_load(f2_a, bx1, x1s, t0, ND_D)
                bl1 = tb("ln1_b")
                z_tile_load(ln1_b, bl1, ln1s, t0, ND_D)
                for i in range(ND_D):
                    tr.emit("vector", lambda e, i=i: e.tensor_copy(
                        xn_b[:, i * T:(i + 1) * T], z_in[:, i * T:(i + 1) * T]),
                        reads=(bz,), writes=(tb("xn_b"),))
                for dout in range(ND_D):
                    slot = dout % 2
                    pst, psb_ = PS[slot], pb[slot]
                    wtile, wb = stream_w(c_op1, dout, KC_D, slot)
                    tr.emit_group("tensor", [
                        (lambda e, kc=kc, st=(kc == 0), sp=(kc == KC_D - 1), wtile=wtile, psb_=psb_:
                         e.matmul(psb_[:], wtile[:, kc * 128:(kc + 1) * 128],
                                  xn_b[:, kc * T:(kc + 1) * T], start=st, stop=sp))
                        for kc in range(KC_D)], reads=(wb, tb("xn_b")), writes=(pst,))
                    tr.emit("scalar", lambda e, dout=dout, psb_=psb_: e.activation(
                        acts_b[:, dout * T:(dout + 1) * T], psb_[:], AF.Gelu, bias=vap("b1", dout)),
                        reads=(pst, bv), writes=(tb("acts_b"),))
                for dout in range(ND_D):
                    slot = dout % 2
                    pst, psb_ = PS[slot], pb[slot]
                    wtile, wb = stream_w(c_op2, dout, KC_D, slot)
                    tr.emit_group("tensor", [
                        (lambda e, kc=kc, st=(kc == 0), sp=(kc == KC_D - 1), wtile=wtile, psb_=psb_:
                         e.matmul(psb_[:], wtile[:, kc * 128:(kc + 1) * 128],
                                  acts_b[:, kc * T:(kc + 1) * T], start=st, stop=sp))
                        for kc in range(KC_D)], reads=(wb, tb("acts_b")), writes=(pst,))
                    tr.emit("scalar", lambda e, dout=dout, psb_=psb_: e.activation(
                        q1[:], psb_[:], AF.Identity, bias=vap("b2", dout)),
                        reads=(pst, bv), writes=(tb("q1"),))
                    tr.emit("vector", lambda e, dout=dout: e.tensor_tensor(
                        q2[:], q1[:], ln1_b[:, dout * T:(dout + 1) * T], ALU.add),
                        reads=(tb("q1"), bl1), writes=(tb("q2"),))
                    tr.emit("vector", lambda e, dout=dout: e.tensor_tensor(
                        f2_a[:, dout * T:(dout + 1) * T], q2[:],
                        f2_a[:, dout * T:(dout + 1) * T], ALU.add),
                        reads=(tb("q2"), bx1), writes=(bx1,))

                if dbg:
                    for i in range(ND_D):
                        dma_out(_dap(dbg1, i * 128 * S + t0, [(S, 128), (1, T)]),
                                f2_a[:, i * T:(i + 1) * T], (bx1,))
                        tr.emit("vector", lambda e, i=i: e.tensor_copy(
                            q1[:], acts_b[:, i * T:(i + 1) * T]),
                            reads=(tb("acts_b"),), writes=(tb("q1"),))
                        dma_out(_dap(dbg2, i * 128 * S + t0, [(S, 128), (1, T)]),
                                q1[:], (tb("q1"),))

                def srcf6(i):
                    return f2_a[:, i * T:(i + 1) * T]

                def sqf6(i):
                    return sq_b[:, i * T:(i + 1) * T]
                ln_stats(srcf6, ND_D, sqf6, False, bx1)
                ln_apply(srcf6, ND_D,
                         lambda i: (xn_b[:, i * T:(i + 1) * T], tb("xn_b")), bx1)

                def xtile6(kc):
                    return xn_b[:, kc * T:(kc + 1) * T]

                def consume6(dout, pst, psb_):
                    zsl = dout % 2
                    tr.emit("vector", lambda e, dout=dout, psb_=psb_, zsl=zsl: e.scalar_tensor_tensor(
                        zo_pp[:, zsl * T:(zsl + 1) * T], psb_[:], 0.5,
                        f2_a[:, dout * T:(dout + 1) * T], ALU.mult, ALU.add),
                        reads=(pst, bx1), writes=(tb("zo_pp"),))
                    dma_out(_dap(out_t, dout * 128 * S + t0, [(S, 128), (1, T)]),
                            zo_pp[:, zsl * T:(zsl + 1) * T], (tb("zo_pp"),))
                ffn_tree(xtile6, consume6, tb("xn_b"))
        tr.barrier()

        @block.sync
        def _(e):
            for a in tr.q["sync"]:
                a(e)

        @block.tensor
        def _(e):
            for a in tr.q["tensor"]:
                a(e)

        @block.vector
        def _(e):
            for a in tr.q["vector"]:
                a(e)

        @block.scalar
        def _(e):
            for a in tr.q["scalar"]:
                a(e)

        @block.gpsimd
        def _(e):
            for a in tr.q["gpsimd"]:
                a(e)

    return nc


def kernel(**inputs):
    g = _prep(inputs)
    hxs = g.pop("_hx_per_core")
    nc = build_program()
    in_maps = []
    for b in range(8):
        m = dict(g)
        m["hx"] = hxs[b]
        in_maps.append(m)
    res = run_bass_kernel_spmd(nc, in_maps, core_ids=list(range(8)))
    outs = [np.asarray(r["out"]).T for r in res.results]
    return np.ascontiguousarray(np.stack(outs, 0).astype(np.float32))
